# version 22
# baseline (speedup 1.0000x reference)
import math
import numpy as np
import ml_dtypes
import concourse.bass as bass
import concourse.mybir as mybir
from concourse.bass_utils import run_bass_kernel_spmd

F32 = mybir.dt.float32
BF16 = mybir.dt.bfloat16
AF = mybir.ActivationFunctionType
ALU = mybir.AluOpType
AX = mybir.AxisListType

NCORE = 8
B = 2
S = 4096
R = B * S
D = 4096
HD = 128
QC = 512
WCOLS = 768
DH = 11008
NHT = DH // 128
NHP = NHT // 2
RK = 16
OWN = R // NCORE
EPS = 1e-5
INV_SQRT_HD = 1.0 / math.sqrt(HD)
NEG_BIG = -30000.0

ENGS = ("sync", "tensor", "vector", "scalar", "gpsimd")


def _host_consts():
    c = {}
    c["ident"] = np.eye(128, dtype=ml_dtypes.bfloat16)
    p = np.zeros((128, 128), dtype=np.float32)
    for i in range(128):
        p[i ^ 1, i] = 1.0
    c["perm"] = p.astype(ml_dtypes.bfloat16)
    inv_freq = 1.0 / (10000.0 ** (np.arange(0, HD, 2, dtype=np.float64) / HD))
    ang = np.arange(S, dtype=np.float64)[None, :] * inv_freq[:, None]
    cos, sin = np.cos(ang), np.sin(ang)
    Ct = np.empty((128, S), np.float32)
    St = np.empty((128, S), np.float32)
    Ct[0::2] = cos
    Ct[1::2] = cos
    St[0::2] = -sin
    St[1::2] = sin
    c["rope_c"] = Ct.astype(ml_dtypes.bfloat16)
    c["rope_s"] = St.astype(ml_dtypes.bfloat16)
    masks = np.zeros((128, 4 * 512), np.float32)
    for k in range(4):
        tloc = 128 * k + np.arange(128)[:, None]
        masks[:, k * 512:(k + 1) * 512] = np.where(
            tloc > np.arange(512)[None, :], NEG_BIG, 0.0)
    c["masks"] = masks.astype(ml_dtypes.bfloat16)
    c["ones_col"] = np.ones((128, 1), dtype=ml_dtypes.bfloat16)
    c["ones_row"] = np.ones((1, 128), dtype=ml_dtypes.bfloat16)
    c["ones_row_f"] = np.ones((1, 128), dtype=np.float32)
    c["fconst"] = np.tile(np.array([[0.0, 1.0, EPS, -0.5, -1.0, INV_SQRT_HD, 2.0, 0.5]],
                           dtype=np.float32), (128, 1))
    return c


class Sem:
    def __init__(self, handle):
        self.h = handle
        self.n = 0

    def inc(self, k=1):
        self.n += k
        return self.n


class Builder:
    def __init__(self, taps=(), stop=99):
        self.nc = bass.Bass("TRN2", target_bir_lowering=False)
        self.taps = set(taps)
        self.stop = stop
        self.plan = {e: [] for e in ENGS}
        self.stack = []
        self.t = {}

    def on(self, eng, fn):
        self.plan[eng].append(fn)

    def w(self, eng, sem, val):
        assert val <= sem.n, f"wait on future value {val} > {sem.n}"
        if val > 0:
            self.on(eng, lambda e, s=sem.h, v=val: e.wait_ge(s, v))

    def wt(self, eng, tok):
        if tok is not None:
            self.w(eng, tok[0], tok[1])

    def dma(self, eng, out, in_, sem):
        def f(e, out=out, in_=in_, s=sem.h):
            e.dma_start(out=out, in_=in_).then_inc(s, 16)
        self.on(eng, f)
        return sem.inc(16)

    def sem(self, name):
        g = self.nc.semaphore(name)
        self.stack.append((name, g))
        return Sem(g.__enter__())

    def sbuf(self, name, shape, dtyp):
        g = self.nc.sbuf_tensor(name, shape, dtyp)
        self.stack.append((name, g))
        t = g.__enter__()
        self.t[name] = t
        return t

    def psum(self, name, shape, dtyp=F32):
        g = self.nc.psum_tensor(name, shape, dtyp)
        self.stack.append((name, g))
        t = g.__enter__()
        self.t[name] = t
        return t

    def free_through(self, name):
        while self.stack:
            n, g = self.stack.pop()
            g.__exit__(None, None, None)
            if n == name:
                return
        raise RuntimeError(f"{name} not on stack")

    def emit(self):
        nc = self.nc
        with nc.Block() as block:
            @block.sync
            def _(e):
                for fn in self.plan["sync"]:
                    fn(e)

            @block.tensor
            def _(e):
                for fn in self.plan["tensor"]:
                    fn(e)

            @block.vector
            def _(e):
                for fn in self.plan["vector"]:
                    fn(e)

            @block.scalar
            def _(e):
                for fn in self.plan["scalar"]:
                    fn(e)

            @block.gpsimd
            def _(e):
                for fn in self.plan["gpsimd"]:
                    fn(e)
        while self.stack:
            _, g = self.stack.pop()
            g.__exit__(None, None, None)
        return nc

    def build(self):
        nc = self.nc
        taps = self.taps
        on, w, wt, dma = self.on, self.w, self.wt, self.dma

        def ein(name, shape, dtyp=F32):
            return nc.dram_tensor(name, shape, dtyp, kind="ExternalInput").ap()

        x_in = ein("x_rows", [OWN, D])
        wqkv_in = ein("wqkv_sl", [D, WCOLS])
        wo_in = ein("wo_sl", [512, D])
        gate_in = ein("gate_sl", [DH, 512])
        up_in = ein("up_sl", [DH, 512])
        down_in = ein("down_sl", [D, 1408])
        ga_in = ein("gate_a", [RK, D])
        gb_in = ein("gate_b", [DH, RK])
        ua_in = ein("up_a", [RK, D])
        ub_in = ein("up_b", [DH, RK])
        da_in = ein("down_a", [RK, DH])
        db_in = ein("down_b", [D, RK])
        ident_in = ein("ident", [128, 128], BF16)
        perm_in = ein("perm", [128, 128], BF16)
        ropec_in = ein("rope_c", [128, S], BF16)
        ropes_in = ein("rope_s", [128, S], BF16)
        masks_in = ein("masks", [128, 4 * 512], BF16)
        onesc_in = ein("ones_col", [128, 1], BF16)
        onesr_in = ein("ones_row", [1, 128], BF16)
        onesrf_in = ein("ones_row_f", [1, 128], F32)
        fconst_in = ein("fconst", [128, 8], F32)

        out_ap = nc.dram_tensor("out_rows", [OWN, D], F32, kind="ExternalOutput").ap()

        def idr(name, shape, dtyp):
            return nc.dram_tensor(name, shape, dtyp, kind="Internal").ap()

        wo_bf = idr("wo_bf", [512, D], BF16)
        gate_bf = idr("gate_bf", [DH, 512], BF16)
        up_bf = idr("up_bf", [DH, 512], BF16)
        down_bf = idr("down_bf", [D, 1408], BF16)
        x1t_agin_a = idr("x1t_agin_a", [D, 512], BF16)
        x1t_agin_b = idr("x1t_agin_b", [D, 512], BF16)
        x1t_gaA = idr("x1t_gaA", [NCORE * 2048, 512], BF16)
        x1t_gaB = idr("x1t_gaB", [NCORE * 2048, 512], BF16)
        x1t_gbA = idr("x1t_gbA", [NCORE * 2048, 512], BF16)
        x1t_gbB = idr("x1t_gbB", [NCORE * 2048, 512], BF16)
        wo_agin = idr("wo_agin", [4 * 8 * 128, 512], BF16)
        wo_g = idr("wo_g", [NCORE * 4 * 8 * 128, 512], BF16)
        NHPA, NHPB = 22, 21
        gate_agin = idr("gate_agin", [NHP * 4 * 128, 256], BF16)
        gate_gA = idr("gate_gA", [NCORE * NHPA * 4 * 128, 256], BF16)
        gate_gB = idr("gate_gB", [NCORE * NHPB * 4 * 128, 256], BF16)
        up_agin = idr("up_agin", [NHP * 4 * 128, 256], BF16)
        up_gA = idr("up_gA", [NCORE * NHPA * 4 * 128, 256], BF16)
        up_gB = idr("up_gB", [NCORE * NHPB * 4 * 128, 256], BF16)
        down_aginA = idr("down_aginA", [11 * 4 * 128, 512], BF16)
        down_aginB = idr("down_aginB", [11 * 4 * 128, 512], BF16)
        down_gA = idr("down_gA", [NCORE * 11 * 4 * 128, 512], BF16)
        down_gB = idr("down_gB", [NCORE * 11 * 4 * 128, 512], BF16)
        out4_dram = idr("out4_dram", [512, D], BF16)
        ga_bfd = idr("ga_bfd", [RK, D], BF16)
        ua_bfd = idr("ua_bfd", [RK, D], BF16)
        da_bfd = idr("da_bfd", [RK, DH], BF16)
        gb_bfd = idr("gb_bfd", [DH, RK], BF16)
        ub_bfd = idr("ub_bfd", [DH, RK], BF16)
        db_bfd = idr("db_bfd", [D, RK], BF16)
        at_agin_a = idr("at_agin_a", [QC, S], BF16)
        at_agin_b = idr("at_agin_b", [QC, S], BF16)
        atg2 = idr("atg2", [2 * NCORE * QC, S], BF16)
        x2_dram = idr("x2_dram", [OWN, D], F32)
        gaT_d = idr("gaT_d", [128, 32 * RK], BF16)
        uaT_d = idr("uaT_d", [128, 32 * RK], BF16)
        daT_d = idr("daT_d", [128, NHT * RK], BF16)
        lorapk_d = idr("lorapk_d", [128, DH], BF16)
        stats_agin = idr("stats_agin", [OWN, 2], F32)
        stats_g = idr("stats_g", [NCORE * OWN, 2], F32)

        wo_g4 = wo_g.rearrange("(r dt cc p) f -> r dt cc p f", r=NCORE, dt=4, cc=8, p=128)
        gate_g4A = gate_gA.rearrange("(r hp dt p) f -> r hp dt p f", r=NCORE, hp=NHPA, dt=4, p=128)
        gate_g4B = gate_gB.rearrange("(r hp dt p) f -> r hp dt p f", r=NCORE, hp=NHPB, dt=4, p=128)
        up_g4A = up_gA.rearrange("(r hp dt p) f -> r hp dt p f", r=NCORE, hp=NHPA, dt=4, p=128)
        up_g4B = up_gB.rearrange("(r hp dt p) f -> r hp dt p f", r=NCORE, hp=NHPB, dt=4, p=128)
        down_g4A = down_gA.rearrange("(r ht dc p) f -> r ht dc p f", r=NCORE, ht=11, dc=4, p=128)
        down_g4B = down_gB.rearrange("(r ht dc p) f -> r ht dc p f", r=NCORE, ht=11, dc=4, p=128)
        x1t_g3aA = x1t_gaA.rearrange("(r d) f -> r d f", r=NCORE)
        x1t_g3aB = x1t_gaB.rearrange("(r d) f -> r d f", r=NCORE)
        x1t_g3bA = x1t_gbA.rearrange("(r d) f -> r d f", r=NCORE)
        x1t_g3bB = x1t_gbB.rearrange("(r d) f -> r d f", r=NCORE)
        x1t_ag2a = x1t_agin_a.rearrange("(dt p) f -> dt p f", p=128)
        x1t_ag2b = x1t_agin_b.rearrange("(dt p) f -> dt p f", p=128)
        wo_ag2 = wo_agin.rearrange("(dt cc p) f -> dt cc p f", dt=4, cc=8, p=128)
        gate_ag2 = gate_agin.rearrange("(hp dt p) f -> hp dt p f", hp=NHP, dt=4, p=128)
        up_ag2 = up_agin.rearrange("(hp dt p) f -> hp dt p f", hp=NHP, dt=4, p=128)
        down_ag2A = down_aginA.rearrange("(ht dc p) f -> ht dc p f", ht=11, dc=4, p=128)
        down_ag2B = down_aginB.rearrange("(ht dc p) f -> ht dc p f", ht=11, dc=4, p=128)

        SCONST = self.sem("sconst")
        SCONST2 = self.sem("sconst2")
        SPREG = self.sem("spreg")
        SPREGQ = self.sem("spregq")
        SPREGW = self.sem("spregw")
        SPREGT = self.sem("spregt")
        SPRES = self.sem("spres")
        SPRES1 = self.sem("spres1")
        SXs = [self.sem(f"sx{i}") for i in range(2)]
        SX1T = self.sem("sx1t")
        SX1TB = self.sem("sx1tb")
        SCCs = [self.sem(f"scc{i}") for i in range(8)]
        SQW = self.sem("sqw")
        SQs = [self.sem(f"sq{i}") for i in range(8)]
        SLORA = self.sem("slora")
        STT = self.sem("stt")
        STE = self.sem("ste")
        SVE = self.sem("sve")
        SSE = self.sem("sse")
        SEV = self.sem("sev")
        SPV = self.sem("spv")
        SEXP = self.sem("sexp")
        SMS = self.sem("sms")
        SAGs = [self.sem(f"sag{i}") for i in range(4)]
        SWOs = [self.sem(f"swo{i}") for i in range(8)]
        SWOT = self.sem("swot")
        SAT = self.sem("sat")
        SFWs = [self.sem(f"sfw{i}") for i in range(8)]
        SFDs = [self.sem(f"sfd{i}") for i in range(8)]
        SFT = self.sem("sft")
        SO4W = self.sem("so4w")
        SO4W2 = self.sem("so4w2")
        SO4L = self.sem("so4l")
        SO4R = self.sem("so4r")
        SX2 = self.sem("sx2")
        SX2Rs = [self.sem(f"sx2r{i}") for i in range(2)]
        SOUT = self.sem("sout")
        SSTAs = [self.sem(f"ssta{i}") for i in range(4)]
        SSRs = [self.sem(f"ssr{i}") for i in range(2)]

        pid = nc.partition_id()

        ident = self.sbuf("c_ident", [128, 128], BF16)
        onesc = self.sbuf("c_onesc", [128, 1], BF16)
        onesr = self.sbuf("c_onesr", [1, 128], BF16)
        onesrf = self.sbuf("c_onesrf", [1, 128], F32)
        st_small = self.sbuf("st_small", [128, 32], F32)
        stats_st = self.sbuf("stats_st", [128, 16], F32)

        cpool = self.sbuf("cpool", [128, 8], F32)
        for _i, _v in enumerate([0.0, 1.0, EPS, -0.5, -1.0, INV_SQRT_HD, 2.0, 0.5]):
            nc.const_aps.register(F32, _v, cpool[:, _i:_i + 1])

        ps_big = self.psum("ps_big", [128, 2048])
        ps_a = self.psum("ps_a", [128, 512])
        ps_b = self.psum("ps_b", [128, 512])
        ps_c = self.psum("ps_c", [128, 512])
        psT = self.psum("psT", [128, 1024], BF16)

        def act(out, in_, func, bias=0.0, scale=1.0, accum=None, sem=None):
            sem = sem or SSE
            def f(e, out=out, in_=in_, func=func, bias=bias, scale=scale,
                  accum=accum, s=sem.h):
                e.activation(out, in_, func, bias=bias, scale=scale,
                             accum_out=accum).then_inc(s, 1)
            on("scalar", f)
            return sem.inc()

        def vop(name, *args, sem=None):
            sem = sem or SVE
            def f(e, name=name, args=args, s=sem.h):
                getattr(e, name)(*args).then_inc(s, 1)
            on("vector", f)
            return sem.inc()

        def mm(out, lhsT, rhs, start, stop, sem=None):
            def f(e, out=out, l=lhsT, r=rhs, st=start, sp=stop,
                  s=(sem.h if sem else None)):
                ins = e.matmul(out, l, r, start=st, stop=sp)
                if s is not None:
                    ins.then_inc(s, 1)
            on("tensor", f)
            return sem.inc() if sem else None

        def trans(out, in_, sem=None):
            npt = in_.shape[0]
            idap = ident[:] if npt >= 128 else ident[0:npt, 0:npt]
            def f(e, out=out, in_=in_, idap=idap, s=(sem.h if sem else None)):
                ins = e.transpose(out, in_, idap)
                if s is not None:
                    ins.then_inc(s, 1)
            on("tensor", f)
            return sem.inc() if sem else None

        def ln_stats(src_ap, n_free, base, prewait=None):
            c = lambda k: st_small[:, base + k:base + k + 1]
            nm, rstd, ssum, ssq, var, tt2 = (c(0), c(1), c(2), c(3), c(5), c(6))
            v1 = vop("reduce_sum", ssum, src_ap, AX.X)
            if prewait is not None:
                wt("scalar", prewait)
            s1 = act(zrow[:, 0:n_free], src_ap, AF.Square, accum=ssq)
            w("vector", SVE, v1)
            v2 = vop("tensor_scalar_mul", nm, ssum, -1.0 / n_free)
            w("vector", SVE, v2)
            v3 = vop("tensor_mul", tt2, nm, nm)
            w("vector", SSE, s1)
            v4 = vop("tensor_scalar_mul", var, ssq, 1.0 / n_free)
            w("vector", SVE, v4)
            v5 = vop("tensor_sub", var, var, tt2)
            w("scalar", SVE, v5)
            s2 = act(var, var, AF.Ln, bias=EPS)
            w("scalar", SSE, s2)
            s3 = act(rstd, var, AF.Exp, scale=-0.5)
            return (SSE, s3), nm, rstd

        def _tap(name, src_ap, tok):
            if name not in taps:
                return
            o = nc.dram_tensor("tap_" + name, list(src_ap.shape), src_ap.dtype,
                               kind="ExternalOutput").ap()
            for _t in (tok if isinstance(tok, list) else [tok]):
                wt("sync", _t)
            dma("sync", o, src_ap, SOUT)

        def finish():
            w("sync", SOUT, SOUT.n)
            return self.emit()

        for dst, src in ((ident, ident_in), (onesc, onesc_in), (onesr, onesr_in),
                         (onesrf, onesrf_in), (cpool, fconst_in)):
            dma("sync", dst[:], src, SCONST)
        CONSTS1 = SCONST.n
        w("scalar", SCONST, CONSTS1)
        w("vector", SCONST, CONSTS1)

        qkvw = self.sbuf("qkvw", [128, 32 * WCOLS], BF16)
        for dtt in range(32):
            dma("gpsimd", qkvw[:, dtt * WCOLS:(dtt + 1) * WCOLS],
                wqkv_in[dtt * 128:(dtt + 1) * 128, :], SQW)
        QKVW_L = SQW.n
        c_wo = dma("gpsimd", wo_bf, wo_in, SPREGW)
        for _d, _s in ((ga_bfd, ga_in), (ua_bfd, ua_in), (da_bfd, da_in),
                       (gb_bfd, gb_in), (ub_bfd, ub_in), (db_bfd, db_in)):
            dma("gpsimd", _d, _s, SPREG)
        LORA_CASTS = SPREG.n

        perm = self.sbuf("c_perm", [128, 128], BF16)
        ropec = self.sbuf("c_ropec", [128, S], BF16)
        ropes = self.sbuf("c_ropes", [128, S], BF16)
        maskt = self.sbuf("c_masks", [128, 4 * 512], BF16)
        for dst, src in ((perm, perm_in), (ropec, ropec_in), (ropes, ropes_in),
                         (maskt, masks_in)):
            dma("sync", dst[:], src, SCONST2)
        CONSTS2 = SCONST2.n

        SW = 5632
        stripe = self.sbuf("stripe", [128, 2 * SW], BF16)
        xrow = self.sbuf("xrow", [128, 2 * D], F32)
        tstage = self.sbuf("tstage", [128, 1024], BF16)
        x1row = self.sbuf("x1row", [128, D], BF16)
        zrow = self.sbuf("zrow", [128, D], BF16)
        gaT = self.sbuf("gaT_p", [128, 32 * RK], BF16)
        uaT = self.sbuf("uaT_p", [128, 32 * RK], BF16)
        daT = self.sbuf("daT_p", [128, NHT * RK], BF16)
        lorapk = self.sbuf("lorapk_p", [128, DH], BF16)
        lora_b1 = self.sbuf("lora_b1", [16, 2 * 1024], BF16)
        lora_b2 = self.sbuf("lora_b2", [128, NHT * RK], BF16)


        x1t_ag2ap = x1t_agin_a.rearrange("(dt p) f -> p dt f", p=128)
        x1t_ag2bp = x1t_agin_b.rearrange("(dt p) f -> p dt f", p=128)
        tst3 = tstage.ap().rearrange("p (k f) -> p k f", k=8)
        xfree_v = [None, None]
        xfree_s = [None, None]
        sta_tok = [None] * 4
        psT_tok = None
        for rt in range(8):
            sl = rt % 2
            wt("sync", xfree_v[sl])
            wt("sync", xfree_s[sl])
            lx = dma("sync", xrow[:, sl * D:(sl + 1) * D],
                     x_in[rt * 128:(rt + 1) * 128, :], SXs[sl])
            w("vector", SXs[sl], lx)
            w("scalar", SXs[sl], lx)
            nv = vop("tensor_copy", x1row[:], xrow[:, sl * D:(sl + 1) * D])
            if sta_tok[rt % 4] is not None:
                w("vector", SSTAs[rt % 4], sta_tok[rt % 4])
                w("scalar", SSTAs[rt % 4], sta_tok[rt % 4])
            tok_ln, nm1, rstd1 = ln_stats(xrow[:, sl * D:(sl + 1) * D], D,
                                          (rt % 4) * 8)
            xfree_v[sl] = (SVE, SVE.n)
            xfree_s[sl] = (SSE, SSE.n)
            wt("sync", tok_ln)
            dma("sync", stats_agin[rt * 128:(rt + 1) * 128, 0:2],
                st_small[:, (rt % 4) * 8:(rt % 4) * 8 + 2], SSTAs[rt % 4])
            sta_tok[rt % 4] = SSTAs[rt % 4].n
            for g in range(4):
                w("tensor", SVE, nv)
                wt("tensor", psT_tok)
                if rt == 0 and g == 0:
                    w("tensor", SCONST, CONSTS1)
                for k in range(8):
                    dtt = g * 8 + k
                    tk = trans(psT[:, k * 128:(k + 1) * 128],
                               x1row[:, dtt * 128:(dtt + 1) * 128],
                               sem=STE if k == 7 else None)
                w("vector", STE, tk)
                w("vector", SX1T, SX1T.n)
                w("vector", SX1TB, SX1TB.n)
                ev = vop("tensor_copy", tstage[:], psT[:])
                psT_tok = (SVE, ev)
                ag2p = x1t_ag2ap if rt < 4 else x1t_ag2bp
                sem1t = SX1T if rt < 4 else SX1TB
                qeng = "sync" if g % 2 == 0 else "scalar"
                w(qeng, SVE, ev)
                dma(qeng, ag2p[:, g * 8:(g + 1) * 8,
                               (rt % 4) * 128:(rt % 4 + 1) * 128],
                    tst3, sem1t)
            if rt == 3:
                w("gpsimd", SX1T, SX1T.n)
                on("gpsimd", lambda e: e.collective_compute(
                    "AllGather", ALU.bypass, replica_groups=[list(range(NCORE))],
                    ins=[x1t_agin_a[0:2048, :]], outs=[x1t_gaA]).then_inc(SCCs[0].h, 1))
                CC_X1TA1 = (SCCs[0], SCCs[0].inc())
                on("gpsimd", lambda e: e.collective_compute(
                    "AllGather", ALU.bypass, replica_groups=[list(range(NCORE))],
                    ins=[x1t_agin_a[2048:4096, :]], outs=[x1t_gaB]).then_inc(SCCs[0].h, 1))
                CC_X1TA2 = (SCCs[0], SCCs[0].inc())
            if rt == 7:
                for _s in SSTAs:
                    w("gpsimd", _s, _s.n)
                on("gpsimd", lambda e: e.collective_compute(
                    "AllGather", ALU.bypass, replica_groups=[list(range(NCORE))],
                    ins=[stats_agin], outs=[stats_g]).then_inc(SCCs[6].h, 1))
                CC_STATS = (SCCs[6], SCCs[6].inc())
                w("gpsimd", SX1TB, SX1TB.n)
                on("gpsimd", lambda e: e.collective_compute(
                    "AllGather", ALU.bypass, replica_groups=[list(range(NCORE))],
                    ins=[x1t_agin_b[0:2048, :]], outs=[x1t_gbA]).then_inc(SCCs[6].h, 1))
                CC_X1TB1 = (SCCs[6], SCCs[6].inc())
                on("gpsimd", lambda e: e.collective_compute(
                    "AllGather", ALU.bypass, replica_groups=[list(range(NCORE))],
                    ins=[x1t_agin_b[2048:4096, :]], outs=[x1t_gbB]).then_inc(SCCs[6].h, 1))
                CC_X1TB2 = (SCCs[6], SCCs[6].inc())
        c_gate = dma("gpsimd", gate_bf, gate_in, SPREG)
        CAST_GATE = SPREG.n
        c_up = dma("gpsimd", up_bf, up_in, SPREG)
        CAST_UP = SPREG.n
        c_down = dma("gpsimd", down_bf, down_in, SPREG)
        CASTS_ALL = SPREG.n
        w("gpsimd", SPREGW, c_wo)
        for dtl in range(4):
            for cc in range(8):
                dma("gpsimd", wo_ag2[dtl, cc],
                    wo_bf[dtl * 128:(dtl + 1) * 128, cc * 512:(cc + 1) * 512],
                    SPREGT)
        WO_TILES = SPREGT.n

        psT_tok2 = psT_tok
        lf_tok = None
        lf2_tok = None

        def lora_a(inp, n_dt, dstT):
            nonlocal psT_tok2, lf_tok
            ngg = (n_dt + 7) // 8
            ch_tok = [None, None]
            if lf_tok is None:
                w("sync", SPREG, LORA_CASTS)
            for gg in range(ngg):
                kk = min(8, n_dt - gg * 8)
                par = gg % 2
                wt("sync", ch_tok[par])
                if gg == 0:
                    wt("sync", lf_tok)
                lv = dma("sync", lora_b1[0:16, par * 1024:par * 1024 + kk * 128],
                         inp[:, gg * 1024:gg * 1024 + kk * 128], SLORA)
                w("tensor", SLORA, lv)
                wt("tensor", psT_tok2)
                for k in range(kk):
                    tk = trans(psT[:, k * RK:(k + 1) * RK],
                               lora_b1[0:16, par * 1024 + k * 128:
                                       par * 1024 + (k + 1) * 128],
                               sem=STE if k == kk - 1 else None)
                ch_tok[par] = (STE, tk)
                w("vector", STE, tk)
                ev = vop("tensor_copy", dstT[:, gg * 8 * RK:gg * 8 * RK + kk * RK],
                         psT[:, 0:kk * RK])
                psT_tok2 = (SVE, ev)
                lf_tok = (STE, tk)

        def lora_b(inp, n_ht, dst_row):
            nonlocal psT_tok2, lf2_tok
            wt("sync", lf2_tok)
            wt("sync", lf_tok)
            lv = dma("sync",
                     lora_b2.ap().rearrange("p (ht j) -> p ht j", j=RK)[:, 0:n_ht, :],
                     inp.rearrange("(ht p) j -> p ht j", p=128), SLORA)
            w("tensor", SLORA, lv)
            for gg in range((n_ht + 7) // 8):
                kk = min(8, n_ht - gg * 8)
                wt("tensor", psT_tok2)
                for k in range(kk):
                    ht = gg * 8 + k
                    tk = trans(psT[0:16, k * 128:(k + 1) * 128],
                               lora_b2[:, ht * RK:(ht + 1) * RK],
                               sem=STE if k == kk - 1 else None)
                w("vector", STE, tk)
                ev = vop("tensor_copy",
                         lorapk[dst_row:dst_row + 16,
                                gg * 1024:gg * 1024 + kk * 128],
                         psT[0:16, 0:kk * 128])
                psT_tok2 = (SVE, ev)
                lf2_tok = (STE, tk)

        lora_a(ga_bfd, 32, gaT)
        lora_a(ua_bfd, 32, uaT)
        lora_a(da_bfd, NHT, daT)
        lora_b(gb_bfd, NHT, 0)
        lora_b(ub_bfd, NHT, 32)
        lora_b(db_bfd, 32, 64)
        psT_tok = psT_tok2
        w("sync", SVE, SVE.n)
        for _d, _t in ((gaT_d, gaT), (uaT_d, uaT), (daT_d, daT),
                       (lorapk_d, lorapk)):
            dma("sync", _d, _t[:], SLORA)
        LORA_STASH = SLORA.n

        s_par = [0]
        s_tok = [None, None]

        def stripe_job(src_bf, width, writes, cast_wait):
            p = s_par[0]
            s_par[0] ^= 1
            base = p * SW
            sem_p = SPRES if p == 0 else SPRES1
            w("scalar", SPREG, cast_wait)
            wt("scalar", s_tok[p])
            def tdma(e, src_bf=src_bf, width=width, base=base, sem_p=sem_p):
                e.dma_start(out=stripe[:, base:base + width], in_=src_bf,
                            transpose=True).then_inc(sem_p.h, 16)
            on("scalar", tdma)
            w("scalar", sem_p, sem_p.inc(16))
            last = None
            for dst, off, wdt in writes:
                last = dma("scalar", dst, stripe[:, base + off:base + off + wdt], sem_p)
            s_tok[p] = (sem_p, last)

        for dtl in range(4):
            for hplo, hphi in ((0, NHPA), (NHPA, NHP)):
                stripe_job(gate_bf[hplo * 256:hphi * 256,
                                   dtl * 128:(dtl + 1) * 128],
                           (hphi - hplo) * 256,
                           [(gate_ag2[hp, dtl], (hp - hplo) * 256, 256)
                            for hp in range(hplo, hphi)],
                           CAST_GATE)
        GATE_S, GATE_S1 = SPRES.n, SPRES1.n
        for dtl in range(4):
            for hplo, hphi in ((0, NHPA), (NHPA, NHP)):
                stripe_job(up_bf[hplo * 256:hphi * 256,
                                 dtl * 128:(dtl + 1) * 128],
                           (hphi - hplo) * 256,
                           [(up_ag2[hp, dtl], (hp - hplo) * 256, 256)
                            for hp in range(hplo, hphi)],
                           CAST_UP)
        UP_S, UP_S1 = SPRES.n, SPRES1.n
        for htl in range(11):
            for half in range(2):
                ag2 = down_ag2A if half == 0 else down_ag2B
                stripe_job(down_bf[half * 2048:(half + 1) * 2048,
                                   htl * 128:(htl + 1) * 128], 2048,
                           [(ag2[htl, dcl], dcl * 512, 512) for dcl in range(4)],
                           CASTS_ALL)
        PREPASS_S = SPRES.n
        PREPASS_S1 = SPRES1.n

        SPLIT_R = NHPA * 4 * 128

        def emit_gather(agin, lo, hi, out, scc):
            on("gpsimd", lambda e, agin=agin, lo=lo, hi=hi, out=out, scc=scc:
                e.collective_compute(
                    "AllGather", ALU.bypass, replica_groups=[list(range(NCORE))],
                    ins=[agin[lo:hi, :]], outs=[out]).then_inc(scc.h, 1))
            return (scc, scc.inc())

        def emit_gate_ags():
            w("gpsimd", SPRES, GATE_S)
            w("gpsimd", SPRES1, GATE_S1)
            ra = emit_gather(gate_agin, 0, SPLIT_R, gate_gA, SCCs[2])
            rb = emit_gather(gate_agin, SPLIT_R, NHP * 4 * 128, gate_gB, SCCs[2])
            return ra, rb

        def emit_up_ags():
            w("gpsimd", SPRES, UP_S)
            w("gpsimd", SPRES1, UP_S1)
            ra = emit_gather(up_agin, 0, SPLIT_R, up_gA, SCCs[3])
            return ra

        def emit_upb_ag():
            return emit_gather(up_agin, SPLIT_R, NHP * 4 * 128, up_gB, SCCs[3])

        def emit_weight_ags_a():
            w("gpsimd", SPREGT, WO_TILES)
            on("gpsimd", lambda e: e.collective_compute(
                "AllGather", ALU.bypass, replica_groups=[list(range(NCORE))],
                ins=[wo_agin], outs=[wo_g]).then_inc(SCCs[1].h, 1))
            return (SCCs[1], SCCs[1].inc())

        def emit_down_ag():
            w("gpsimd", SPRES, PREPASS_S)
            w("gpsimd", SPRES1, PREPASS_S1)
            on("gpsimd", lambda e: e.collective_compute(
                "AllGather", ALU.bypass, replica_groups=[list(range(NCORE))],
                ins=[down_aginA], outs=[down_gA]).then_inc(SCCs[4].h, 1))
            ra = (SCCs[4], SCCs[4].inc())
            on("gpsimd", lambda e: e.collective_compute(
                "AllGather", ALU.bypass, replica_groups=[list(range(NCORE))],
                ins=[down_aginB], outs=[down_gB]).then_inc(SCCs[4].h, 1))
            rb = (SCCs[4], SCCs[4].inc())
            return ra, rb


        if self.stop <= 1:
            return finish()

        on("gpsimd", lambda e: e.wait_ge(STT.h, 256))
        CC_WO = emit_weight_ags_a()
        CC_GATEA, CC_GATEB = emit_gate_ags()

        self.free_through("xrow")
        mxs = self.sbuf("mxs", [128, 33], BF16)
        bias_bh = self.sbuf("bias_bh", [128, 1], F32)
        ptst = self.sbuf("ptst", [128, 2 * 1024], BF16)
        rcln = self.sbuf("rcln", [1, 2 * 512], F32)
        rcb = self.sbuf("rcb", [1, 2 * 512], BF16)
        recipB = self.sbuf("recipB", [128, 2 * 512], BF16)
        aoraw = self.sbuf("aoraw", [128, 2 * 512], BF16)
        sums_sb = self.sbuf("sums_sb", [1, 2 * 512], F32)
        aostage = self.sbuf("aostage", [128, 2 * 512], BF16)
        smax2 = self.sbuf("smax2", [1, 2], F32)
        x1ring = self.sbuf("x1ring", [128, 8 * 512], BF16)
        qts = self.sbuf("qts", [128, 4 * S], BF16)
        kts = self.sbuf("kts", [128, S], BF16)
        vts = self.sbuf("vts", [128, S], BF16)
        ev_q = self.sbuf("ev_q", [128, 7 * 512], BF16)
        rtmp1 = self.sbuf("rtmp1", [128, 512], F32)
        rtmp2 = self.sbuf("rtmp2", [128, 512], F32)
        rsh = self.sbuf("rsh", [128, 512], BF16)
        stat_rows = self.sbuf("stat_rows", [1, 2 * 1024], F32)
        svw = stat_rows.ap().rearrange("p (s t o) -> p s o t", s=2, o=2)
        rows_bf = self.sbuf("rows_bf", [1, 2 * 1536], BF16)
        rstdB = self.sbuf("rstdB", [128, 2 * 512], F32)
        wsum_sb = self.sbuf("wsum_sb", [1, WCOLS], BF16)

        w("sync", SVE, SVE.n)
        w("sync", SSE, SSE.n)
        w("sync", STE, STE.n)
        w("sync", SX1T, SX1T.n)
        w("sync", SX1TB, SX1TB.n)
        w("vector", SX1T, SX1T.n)
        w("vector", SX1TB, SX1TB.n)
        w("vector", STE, STE.n)
        w("vector", SLORA, LORA_STASH)
        w("tensor", SQW, QKVW_L)
        w("tensor", SCONST2, CONSTS2)

        for dtt in range(32):
            mm(ps_c[0:1, 0:512], onesc[:],
               qkvw[:, dtt * WCOLS:dtt * WCOLS + 512],
               start=(dtt == 0), stop=(dtt == 31),
               sem=SPV if dtt == 31 else None)
        wsA = SPV.n
        w("vector", SPV, wsA)
        e1 = vop("tensor_copy", wsum_sb[0:1, 0:512], ps_c[0:1, 0:512])
        w("tensor", SVE, e1)
        for dtt in range(32):
            mm(ps_c[0:1, 0:256], onesc[:],
               qkvw[:, dtt * WCOLS + 512:dtt * WCOLS + 768],
               start=(dtt == 0), stop=(dtt == 31),
               sem=SPV if dtt == 31 else None)
        wsB = SPV.n
        w("vector", SPV, wsB)
        e2 = vop("tensor_copy", wsum_sb[0:1, 512:768], ps_c[0:1, 0:256])

        evac_tok = [None] * 6
        evq_rd_tok = [[] for _ in range(7)]
        pending_v = [None, None]
        rope_vc = [None]
        psc_tok = (SVE, e2)
        srow_free = [None, None]
        SHUF_MASK = [i ^ 1 for i in range(32)]
        aost_tok = [None] * 4
        aoraw_tok = [None, None]
        sums_tok = [None, None]
        psab_tok2 = [None]
        pending_div = [None]
        ptst_tok = [None, None]
        pshalf_tok = [None, None]
        psab_tok = None
        qts_tok = [None]
        x1t_seen = set()
        tile_i = 0

        GRC_ORDER = [0, 2, 4, 6, 1, 3, 5, 7]
        for b in range(B):
            for idx in range(8):
                rcb_i = GRC_ORDER[idx]
                grc = b * 8 + rcb_i
                rank, col = grc // 2, (grc % 2) * 512
                if b == 1 and idx == 0:
                    wt("vector", qts_tok[0])
                for dtt in range(32):
                    sl = tile_i % 8
                    if tile_i >= 8:
                        w("sync", STT, tile_i - 7)
                    if grc % 2 == 0:
                        g3, cctok = ((x1t_g3aA, CC_X1TA1) if dtt < 16
                                     else (x1t_g3aB, CC_X1TA2))
                    else:
                        g3, cctok = ((x1t_g3bA, CC_X1TB1) if dtt < 16
                                     else (x1t_g3bB, CC_X1TB2))
                    if id(cctok) not in x1t_seen:
                        x1t_seen.add(id(cctok))
                        wt("sync", cctok)
                    lv = dma("sync", x1ring[:, sl * 512:(sl + 1) * 512],
                             g3[rank, (dtt % 16) * 128:(dtt % 16 + 1) * 128, :],
                             SQs[sl])
                    w("tensor", SQs[sl], lv)
                    for wti in range(6):
                        if dtt == 0:
                            wt("tensor", evac_tok[wti])
                        dst = (ps_big[:, wti * 512:(wti + 1) * 512] if wti < 4
                               else (ps_a[:] if wti == 4 else ps_b[:]))
                        mm(dst,
                           qkvw[:, dtt * WCOLS + wti * 128:dtt * WCOLS + (wti + 1) * 128],
                           x1ring[:, sl * 512:(sl + 1) * 512],
                           start=(dtt == 0), stop=False,
                           sem=STT if wti == 5 else None)
                    tile_i += 1
                ssl = idx % 2
                sb0 = ssl * 1024
                rb0 = ssl * 1536
                wt("sync", srow_free[ssl])
                if b == 0 and idx == 0:
                    wt("sync", CC_STATS)
                g0 = rank * OWN + (grc % 2) * 512
                l2 = dma("sync", stat_rows[0:1, sb0:sb0 + 1024],
                         stats_g[g0:g0 + 512, 0:2].rearrange(
                             "(a t) o -> a (t o)", a=1), SSRs[ssl])
                w("vector", SSRs[ssl], l2)
                nm_v = svw[0:1, ssl, 0, :]
                rs_v = svw[0:1, ssl, 1, :]
                nmb = vop("tensor_copy", rows_bf[0:1, rb0:rb0 + 512], nm_v)
                rhi = vop("tensor_copy", rows_bf[0:1, rb0 + 512:rb0 + 1024],
                          rs_v)
                w("vector", SVE, rhi)
                rlo = vop("tensor_sub", rows_bf[0:1, rb0 + 1024:rb0 + 1536],
                          rs_v, rows_bf[0:1, rb0 + 512:rb0 + 1024])
                srow_free[ssl] = (SVE, rlo)
                wt("tensor", psc_tok)
                w("tensor", SVE, rlo)
                mm(ps_c[:], onesr[:], rows_bf[0:1, rb0 + 512:rb0 + 1024],
                   start=True, stop=False)
                bmv = mm(ps_c[:], onesr[:], rows_bf[0:1, rb0 + 1024:rb0 + 1536],
                         start=False, stop=True, sem=SPV)
                w("vector", SPV, bmv)
                cb1 = vop("tensor_copy", rstdB[:, 0:512], ps_c[:])
                w("vector", SVE, cb1)
                cb2 = vop("tensor_scalar_mul", rstdB[:, 512:1024],
                          rstdB[:, 0:512], INV_SQRT_HD)
                psc_tok = (SVE, cb1)
                corr = None
                for wti in range(6):
                    dst = (ps_big[:, wti * 512:(wti + 1) * 512] if wti < 4
                           else (ps_a[:] if wti == 4 else ps_b[:]))
                    corr = mm(dst, wsum_sb[0:1, wti * 128:(wti + 1) * 128],
                              rows_bf[0:1, rb0:rb0 + 512],
                              start=False, stop=True,
                              sem=SPV if wti == 5 else None)
                qkv_done = STT.n
                tcol = rcb_i * 512
                vpar = idx % 2
                for wti in range(6):
                    slot = wti if wti < 5 else 5 + vpar
                    w("vector", STT, qkv_done)
                    w("vector", SPV, corr)
                    if wti == 0:
                        w("vector", SVE, cb2)
                    for _tk in evq_rd_tok[slot]:
                        wt("vector", _tk)
                    src = (ps_big[:, wti * 512:(wti + 1) * 512] if wti < 4
                           else (ps_a[:] if wti == 4 else ps_b[:]))
                    evv = vop("tensor_mul",
                              ev_q[:, slot * 512:(slot + 1) * 512], src,
                              rstdB[:, 512:1024] if wti < 4 else rstdB[:, 0:512])
                    evac_tok[wti] = (SVE, evv)
                vc = rope_vc[0]
                for j in range(5):
                    evq = ev_q[:, j * 512:(j + 1) * 512]
                    wt("vector", evac_tok[j])
                    if vc is not None:
                        w("vector", SVE, vc)
                    sh = vop("stream_shuffle", rsh[:], evq, SHUF_MASK)
                    va = vop("tensor_mul", rtmp1[:], evq, ropec[:, tcol:tcol + 512])
                    w("vector", SVE, sh)
                    vb = vop("tensor_mul", rtmp2[:], rsh[:], ropes[:, tcol:tcol + 512])
                    w("vector", SVE, vb)
                    dst = (qts[:, j * S + tcol:j * S + tcol + 512] if j < 4
                           else kts[:, tcol:tcol + 512])
                    vc = vop("tensor_add", dst, rtmp1[:], rtmp2[:])
                    rope_vc[0] = vc
                    evq_rd_tok[j] = [(SVE, va)]
                def emit_vtrans():
                    nonlocal psT_tok
                    vsl, tc = pending_v[0]
                    w("tensor", SVE, pending_v[1])
                    wt("tensor", psT_tok)
                    for k in range(4):
                        tk = trans(psT[:, k * 128:(k + 1) * 128],
                                   ev_q[:, vsl * 512 + k * 128:
                                        vsl * 512 + (k + 1) * 128],
                                   sem=STE if k == 3 else None)
                    w("vector", STE, tk)
                    ev2 = vop("tensor_copy", vts[:, tc:tc + 512], psT[:, 0:512])
                    psT_tok = (SVE, ev2)
                    evq_rd_tok[vsl] = [(STE, tk)]
                pending_v_new = ((5 + vpar, tcol), evac_tok[5][1])
                if pending_v[0] is not None:
                    emit_vtrans()
                pending_v[0] = pending_v_new[0]
                pending_v[1] = pending_v_new[1]

            if pending_v[0] is not None:
                vslf, tcf = pending_v[0]
                w("tensor", SVE, pending_v[1])
                wt("tensor", psT_tok)
                for k in range(4):
                    tkf = trans(psT[:, k * 128:(k + 1) * 128],
                                ev_q[:, vslf * 512 + k * 128:
                                     vslf * 512 + (k + 1) * 128],
                                sem=STE if k == 3 else None)
                w("vector", STE, tkf)
                ev2f = vop("tensor_copy", vts[:, tcf:tcf + 512], psT[:, 0:512])
                psT_tok = (SVE, ev2f)
                evq_rd_tok[vslf] = [(STE, tkf)]
                pending_v[0] = None

            if b == 1:
                w("gpsimd", STT, 512)
                for _s in SAGs[0:2]:
                    w("gpsimd", _s, _s.n)
                on("gpsimd", lambda e: e.collective_compute(
                    "AllGather", ALU.bypass, replica_groups=[list(range(NCORE))],
                    ins=[at_agin_a], outs=[atg2[0:NCORE * QC, :]]).then_inc(SCCs[5].h, 1))
                CC_ATA = (SCCs[5], SCCs[5].inc())
                CC_UPA = emit_up_ags()
                CC_UPB = emit_upb_ag()
            for h in range(4):
                bh = b * 4 + h
                p1_tok = [pshalf_tok[0], pshalf_tok[1]]
                for qt in range(32):
                    par = qt % 2
                    wt("tensor", p1_tok[par])
                    step = 16 if qt < 16 else 32
                    cnt = (qt + 1) * 128 // step
                    pv = mm(ps_big[:, par * 1024:par * 1024 + cnt],
                            qts[:, h * S + qt * 128:h * S + (qt + 1) * 128],
                            kts[:, 0:(qt + 1) * 128:step],
                            start=True, stop=True, sem=SPV)
                    w("vector", SPV, pv)
                    mv = vop("reduce_max", mxs[:, qt:qt + 1],
                             ps_big[:, par * 1024:par * 1024 + cnt], AX.X)
                    p1_tok[par] = (SVE, mv)
                pshalf_tok = p1_tok
                w("vector", SVE, SVE.n)
                mv1 = vop("reduce_max", mxs[:, 32:33], mxs[:, 0:32], AX.X)
                wt("tensor", psT_tok)
                w("tensor", SVE, mv1)
                tv1 = trans(psT[0:1, 0:128], mxs[:, 32:33], sem=STE)
                w("vector", STE, tv1)
                mv2 = vop("reduce_max", smax2[0:1, 0:1], psT[0:1, 0:128], AX.X)
                psT_tok = (SVE, mv2)
                w("vector", SVE, mv2)
                mv3 = vop("tensor_scalar_mul", smax2[0:1, 1:2], smax2[0:1, 0:1], -1.0)
                wt("tensor", psc_tok)
                w("tensor", SVE, mv3)
                bb = mm(ps_c[:, 0:1], onesrf[:], smax2[0:1, 1:2],
                        start=True, stop=True, sem=SPV)
                w("scalar", SPV, bb)
                bs = act(bias_bh[:], ps_c[:, 0:1], AF.Copy)
                psc_tok = (SSE, bs)
                BIAS_OK = bs

                for qc in range(8):
                    ntt = 4 * (qc + 1)
                    qrhs = qts[:, h * S + qc * 512:h * S + (qc + 1) * 512]
                    ng = ntt // 2

                    def emit_S(g):
                        par = g % 2
                        half = lambda j, par=par: ps_big[:, par * 1024 + j * 512:
                                                         par * 1024 + (j + 1) * 512]
                        wt("tensor", pshalf_tok[par])
                        sv = None
                        for j in range(2):
                            tt = 2 * g + j
                            sv = mm(half(j), kts[:, tt * 128:(tt + 1) * 128], qrhs,
                                    start=True, stop=True,
                                    sem=SPV if j == 1 else None)
                        if (2 * g + 1) >= 4 * qc:
                            w("vector", SPV, sv)
                            msv = None
                            for j in range(2):
                                tt = 2 * g + j
                                if tt >= 4 * qc:
                                    msv = vop("tensor_add", half(j), half(j),
                                              maskt[:, (tt - 4 * qc) * 512:
                                                    (tt - 4 * qc + 1) * 512], sem=SMS)
                            w("scalar", SMS, msv)
                        else:
                            w("scalar", SPV, sv)
                        w("scalar", SSE, BIAS_OK)
                        wt("scalar", ptst_tok[par])
                        ex = act(ptst[:, par * 1024:(par + 1) * 1024],
                                 ps_big[:, par * 1024:(par + 1) * 1024],
                                 AF.Exp, bias=bias_bh[:], sem=SEXP)
                        pshalf_tok[par] = (SEXP, ex)

                    def emit_PV(g):
                        par = g % 2
                        w("tensor", SEXP, pshalf_tok[par][1])
                        if g == 0:
                            wt("tensor", psab_tok)
                            wt("tensor", psab_tok2[0])
                        last = None
                        for j in range(2):
                            tt = 2 * g + j
                            mm(ps_a[:], vts[:, tt * 128:(tt + 1) * 128],
                               ptst[:, par * 1024 + j * 512:par * 1024 + (j + 1) * 512],
                               start=(tt == 0), stop=(tt == ntt - 1))
                            last = mm(ps_b[0:1, :], onesc[:],
                                      ptst[:, par * 1024 + j * 512:
                                           par * 1024 + (j + 1) * 512],
                                      start=(tt == 0), stop=(tt == ntt - 1),
                                      sem=SPV if tt == ntt - 1 else None)
                        ptst_tok[par] = (SPV, SPV.n)
                        return last

                    emit_S(0)
                    last_pv = None
                    for g in range(1, ng):
                        emit_S(g)
                        last_pv = emit_PV(g - 1)
                    last_pv = emit_PV(ng - 1)

                    dsl = (bh * 8 + qc) % 2
                    wt("vector", aoraw_tok[dsl])
                    w("vector", SPV, last_pv)
                    eva = vop("tensor_copy", aoraw[:, dsl * 512:(dsl + 1) * 512],
                              ps_a[:])
                    w("scalar", SPV, last_pv)
                    wt("scalar", sums_tok[dsl])
                    evs = act(sums_sb[0:1, dsl * 512:(dsl + 1) * 512], ps_b[0:1, :],
                              AF.Copy, sem=SSE)
                    psab_tok = (SSE, evs)
                    psab_tok2[0] = (SVE, eva)
                    w("scalar", SSE, evs)
                    l1 = act(rcln[0:1, dsl * 512:(dsl + 1) * 512],
                             sums_sb[0:1, dsl * 512:(dsl + 1) * 512],
                             AF.Ln, sem=SSE)
                    w("scalar", SSE, l1)
                    l2 = act(rcb[0:1, dsl * 512:(dsl + 1) * 512],
                             rcln[0:1, dsl * 512:(dsl + 1) * 512],
                             AF.Exp, scale=-1.0, sem=SSE)
                    sums_tok[dsl] = (SSE, l1)

                    def emit_divide(dsl=dsl, bh=bh, qc=qc, b=b, h=h,
                                    eva=eva, l2=l2):
                        nonlocal psc_tok
                        sl2 = dsl
                        wt("tensor", psc_tok)
                        w("tensor", SSE, l2)
                        bc = mm(ps_c[:], onesr[:], rcb[0:1, sl2 * 512:(sl2 + 1) * 512],
                                start=True, stop=True, sem=SPV)
                        w("scalar", SPV, bc)
                        cb = act(recipB[:, sl2 * 512:(sl2 + 1) * 512], ps_c[:],
                                 AF.Copy, sem=SSE)
                        psc_tok = (SSE, cb)
                        asl = (bh * 8 + qc) % 2
                        wt("vector", aost_tok[asl])
                        w("vector", SSE, cb)
                        w("vector", SVE, eva)
                        dv = vop("tensor_mul", aostage[:, asl * 512:(asl + 1) * 512],
                                 aoraw[:, sl2 * 512:(sl2 + 1) * 512],
                                 recipB[:, sl2 * 512:(sl2 + 1) * 512])
                        aoraw_tok[sl2] = (SVE, dv)
                        w("sync", SVE, dv)
                        agin = at_agin_a if b == 0 else at_agin_b
                        av = dma("sync",
                                 agin[h * 128:(h + 1) * 128,
                                      qc * 512:(qc + 1) * 512],
                                 aostage[:, asl * 512:(asl + 1) * 512], SAGs[asl])
                        aost_tok[asl] = (SAGs[asl], av)
                    if pending_div[0] is not None:
                        pending_div[0]()
                    pending_div[0] = emit_divide
            if pending_div[0] is not None:
                pending_div[0]()
                pending_div[0] = None
            if b == 1:
                for _s in SAGs[0:2]:
                    w("gpsimd", _s, _s.n)
                on("gpsimd", lambda e: e.collective_compute(
                    "AllGather", ALU.bypass, replica_groups=[list(range(NCORE))],
                    ins=[at_agin_b],
                    outs=[atg2[NCORE * QC:2 * NCORE * QC, :]]).then_inc(SCCs[7].h, 1))
                CC_ATB = (SCCs[7], SCCs[7].inc())
                CC_DOWNA, CC_DOWNB = emit_down_ag()
            qts_tok[0] = (SVE, SVE.n)

        if pending_div[0] is not None:
            pending_div[0]()
            pending_div[0] = None
        if self.stop <= 2:
            return finish()

        self.free_through("qkvw")
        w("vector", SPRES, PREPASS_S)
        w("vector", SPRES1, PREPASS_S1)
        xrow = self.sbuf("xrow_b", [128, 2 * D], F32)
        xfree = [None, None]
        tstage = self.sbuf("tstage_b", [128, 1024], BF16)
        x1row = self.sbuf("x1row_b", [128, D], BF16)
        zrow = self.sbuf("zrow_b", [128, D], BF16)
        gaT = self.sbuf("gaT", [128, 32 * RK], BF16)
        uaT = self.sbuf("uaT", [128, 32 * RK], BF16)
        daT = self.sbuf("daT", [128, NHT * RK], BF16)
        lorapk = self.sbuf("lorapk", [128, DH], BF16)
        xrpack = self.sbuf("xrpack", [128, 512], BF16)
        x2tmp = self.sbuf("x2tmp", [128, D], F32)
        x3t = self.sbuf("x3t", [128, 32 * 512], BF16)

        w("sync", SLORA, LORA_STASH)
        for _t, _d in ((gaT, gaT_d), (uaT, uaT_d), (daT, daT_d),
                       (lorapk, lorapk_d)):
            dma("sync", _t[:], _d, SLORA)
        LORA_RELOAD = SLORA.n
        w("tensor", SLORA, LORA_RELOAD)

        out1 = self.sbuf("out1", [128, 4 * D], BF16)
        ats_q = self.sbuf("ats_q", [128, 32 * 512], BF16)
        woring = self.sbuf("woring", [128, 8 * 512], BF16)

        wo_bank_tok = [None] * 4
        out1_free = None
        x2tmp_tok = []
        zrow_tok = None
        x3t_free = None

        x3t_v = x3t.ap().rearrange("p (dt rtl f) -> p dt rtl f", dt=32, rtl=4)

        def emit_x3_transposes(rtl, nx3):
            nonlocal psT_tok
            for g in range(4):
                w("tensor", SVE, nx3)
                wt("tensor", psT_tok)
                for k in range(8):
                    dtt = g * 8 + k
                    tk = trans(psT[:, k * 128:(k + 1) * 128],
                               x1row[:, dtt * 128:(dtt + 1) * 128],
                               sem=STE if k == 7 else None)
                w("vector", STE, tk)
                wt("vector", x3t_free)
                ev = vop("tensor_copy",
                         x3t_v[:, g * 8:(g + 1) * 8, rtl, :],
                         psT.ap().rearrange("p (k f) -> p k f", k=8))
                psT_tok = (SVE, ev)
            return psT_tok
        wotile_i = 0
        ats_tok = None
        for rtq in range(2):
            wt("sync", CC_ATA)
            wt("sync", CC_ATB)
            wt("sync", ats_tok)
            atv = None
            at_g3 = atg2.rearrange("(x p) r -> p x r", p=128)
            ats_q3 = ats_q.ap().rearrange("p (dt r) -> p dt r", dt=32)
            xoff = (pid // 4) * 32
            coff = (pid % 4) * 1024 + rtq * 512
            for dg in range(4):
                atv = dma("sync", ats_q3[:, dg * 8:(dg + 1) * 8, :],
                          at_g3[:, bass.ds(xoff + dg * 8, 8),
                                bass.ds(coff, 512)], SAT)
            for cc in range(8):
                for dtt in range(32):
                    sl = wotile_i % 8
                    qeng = "sync" if sl % 2 == 0 else "gpsimd"
                    if wotile_i >= 8:
                        w(qeng, SWOT, wotile_i - 7)
                    if wotile_i < 2:
                        wt(qeng, CC_WO)
                    lv = dma(qeng, woring[:, sl * 512:(sl + 1) * 512],
                             wo_g4[dtt // 4, dtt % 4, cc], SWOs[sl])
                    w("tensor", SWOs[sl], lv)
                    if dtt == 0:
                        w("tensor", SAT, atv)
                        for rtl in range(4):
                            wt("tensor", wo_bank_tok[rtl])
                    for rtl in range(4):
                        mm(ps_big[:, rtl * 512:(rtl + 1) * 512],
                           ats_q[:, dtt * 512 + rtl * 128:dtt * 512 + (rtl + 1) * 128],
                           woring[:, sl * 512:(sl + 1) * 512],
                           start=(dtt == 0), stop=(dtt == 31),
                           sem=SWOT if rtl == 3 else None)
                    wotile_i += 1
                for rtl in range(4):
                    w("scalar", SWOT, SWOT.n)
                    wt("scalar", out1_free)
                    evv = act(out1[:, rtl * D + cc * 512:rtl * D + (cc + 1) * 512],
                              ps_big[:, rtl * 512:(rtl + 1) * 512], AF.Copy, sem=SEV)
                    wo_bank_tok[rtl] = (SEV, evv)
            WO_EVAC = SEV.n
            ats_tok = (SWOT, SWOT.n)
            for rtl in range(4):
                rt = rtq * 4 + rtl
                sl = rt % 2
                wt("sync", xfree[sl])
                lx = dma("sync", xrow[:, sl * D:(sl + 1) * D],
                         x_in[rt * 128:(rt + 1) * 128, :], SXs[sl])
                w("vector", SEV, WO_EVAC)
                w("scalar", SEV, WO_EVAC)
                o1 = out1[:, rtl * D:(rtl + 1) * D]
                tok_ln, nm2, r2 = ln_stats(o1, D, 0 if sl == 0 else 8,
                                           prewait=zrow_tok)
                wt("vector", tok_ln)
                z2 = vop("tensor_scalar", zrow[:], o1, nm2, r2, ALU.add, ALU.mult)
                w("vector", SVE, z2)
                w("vector", SXs[sl], lx)
                for _tk in x2tmp_tok:
                    wt("vector", _tk)
                xv = vop("tensor_add", x2tmp[:], zrow[:], xrow[:, sl * D:(sl + 1) * D])
                zrow_tok = (SVE, xv)
                xfree[sl] = (SVE, xv)
                w("sync", SVE, xv)
                x2dma = dma("sync", x2_dram[rt * 128:(rt + 1) * 128, :], x2tmp[:], SX2)
                tok3, nm3, r3 = ln_stats(x2tmp[:], D, 16 if sl == 0 else 24,
                                         prewait=zrow_tok)
                w("vector", SSE, tok3[1])
                c1 = vop("tensor_copy", stats_st[:, 2 * rt:2 * rt + 1], r3)
                c2 = vop("tensor_copy", stats_st[:, 2 * rt + 1:2 * rt + 2], nm3)
                if rtq == 0:
                    nx3 = vop("tensor_scalar", x1row[:], x2tmp[:], nm3, r3,
                              ALU.add, ALU.mult)
                    x2tmp_tok = [(SVE, nx3), (SX2, x2dma)]
                    psT_tok = emit_x3_transposes(rtl, nx3)
                else:
                    x2tmp_tok = [(SVE, c2), (SX2, x2dma)]
            out1_free = (SSE, SSE.n)
        X2_ALL = SX2.n

        self.free_through("out1")
        NGSL = 8
        NDSL = 6
        hidt = self.sbuf("hidt", [128, NHP * 512], BF16)
        guringA = self.sbuf("guringA", [128, NGSL * 1024], BF16)
        guringB = self.sbuf("guringB", [128, NGSL * 1024], BF16)
        dwring = self.sbuf("dwring", [128, NDSL * 512], BF16)

        x2r_tok = [None, None]
        zrow_tok2 = [zrow_tok]
        banks_tok = [None] * 4
        gu_slot_tok = [None] * NGSL
        dw_slot_tok = [None] * NDSL
        ts_tok = [None, None]
        x1row_tok = [None]
        o4r_prev = [0]
        gu_i = [0]
        dw_i = [0]
        dw_seen = set()
        gu_seen = set()

        def ffn_rc(rc, post_gu1=None):
            nonlocal psc_tok, x3t_free, x2tmp_tok
            XV = SVE.n
            for aT, dst_row in ((gaT, 0), (uaT, 32)):
                wt("tensor", psc_tok)
                w("tensor", SVE, XV)
                for dtt in range(32):
                    xv = mm(ps_c[0:16, :], aT[:, dtt * RK:(dtt + 1) * RK],
                            x3t[:, dtt * 512:(dtt + 1) * 512],
                            start=(dtt == 0), stop=(dtt == 31),
                            sem=SPV if dtt == 31 else None)
                w("scalar", SPV, xv)
                evv = act(xrpack[dst_row:dst_row + 16, :], ps_c[0:16, :], AF.Copy,
                          sem=SSE)
                psc_tok = (SSE, evv)
            XR_OK = psc_tok[1]

            hid_tok = [None] * NHT
            slot_of_rank = [0] * NCORE
            hidr_done = None
            pending_hidr = [None]

            def gu_half(half):
                nonlocal hidr_done
                for hl_ in range(NHP):
                    ht = half * NHP + hl_
                    hl = ht % NHP
                    hp, hoff = ht // 2, (ht % 2) * 128
                    par = ht % 2
                    if ht % 2 == 0:
                        used = []
                        for r in range(NCORE):
                            sl = gu_i[0] % NGSL
                            wt("sync", gu_slot_tok[sl])
                            wt("gpsimd", gu_slot_tok[sl])
                            guk = "A" if hp < NHPA else "B"
                            if ("sync", guk) not in gu_seen:
                                gu_seen.add(("sync", guk))
                                wt("sync", CC_GATEA if guk == "A" else CC_GATEB)
                            if ("gpsimd", guk) not in gu_seen:
                                gu_seen.add(("gpsimd", guk))
                                wt("gpsimd", CC_UPA if guk == "A" else CC_UPB)
                            gsrc = (gate_g4A[r, hp] if hp < NHPA
                                    else gate_g4B[r, hp - NHPA])
                            usrc = (up_g4A[r, hp] if hp < NHPA
                                    else up_g4B[r, hp - NHPA])
                            gA = guringA.ap().rearrange(
                                "p (s dt f) -> p s dt f", s=NGSL, dt=4)
                            gB = guringB.ap().rearrange(
                                "p (s dt f) -> p s dt f", s=NGSL, dt=4)
                            dma("sync", gA[:, sl, :, :],
                                gsrc.rearrange("dt p f -> p dt f"),
                                SFWs[sl])
                            lvs = dma("gpsimd", gB[:, sl, :, :],
                                      usrc.rearrange("dt p f -> p dt f"),
                                      SFWs[sl])
                            used.append((sl, lvs))
                            slot_of_rank[r] = sl
                            gu_i[0] += 1
                        for sl_, lvs_ in used:
                            w("tensor", SFWs[sl_], lvs_)
                    bg = ps_big[:, (par * 2) * 512:(par * 2 + 1) * 512]
                    bu = ps_big[:, (par * 2 + 1) * 512:(par * 2 + 2) * 512]
                    wt("tensor", banks_tok[par * 2])
                    wt("tensor", banks_tok[par * 2 + 1])
                    if ht == 0:
                        w("tensor", SSE, XR_OK)
                    for dtt in range(32):
                        r, dtl = dtt // 4, dtt % 4
                        off = slot_of_rank[r] * 1024 + dtl * 256 + hoff
                        mm(bg, guringA[:, off:off + 128],
                           x3t[:, dtt * 512:(dtt + 1) * 512],
                           start=(dtt == 0), stop=False)
                        s_l = mm(bu, guringB[:, off:off + 128],
                                 x3t[:, dtt * 512:(dtt + 1) * 512],
                                 start=(dtt == 0), stop=False,
                                 sem=SFT if (dtl == 3 and ht % 2 == 1) else None)
                        if dtl == 3 and ht % 2 == 1:
                            gu_slot_tok[slot_of_rank[r]] = (SFT, s_l)
                    mm(bg, lorapk[0:16, ht * 128:(ht + 1) * 128], xrpack[0:16, :],
                       start=False, stop=True)
                    gl = mm(bu, lorapk[32:48, ht * 128:(ht + 1) * 128],
                            xrpack[32:48, :], start=False, stop=True, sem=SFT)
                    w("scalar", SFT, gl)
                    ts_ = tstage[:, par * 512:(par + 1) * 512]
                    wt("scalar", ts_tok[par])
                    ts_tok[par] = None
                    sv = act(ts_, bg, AF.Silu, sem=SSE)
                    banks_tok[par * 2] = (SSE, sv)
                    w("vector", SSE, sv)
                    hv = vop("tensor_mul", hidt[:, hl * 512:(hl + 1) * 512], bu, ts_)
                    banks_tok[par * 2 + 1] = (SVE, hv)
                    hid_tok[ht] = (SVE, hv)
                    w("tensor", SVE, hv)
                    hidr_done = mm(ps_b[0:16, :], daT[:, ht * RK:(ht + 1) * RK],
                                   hidt[:, hl * 512:(hl + 1) * 512],
                                   start=(ht == 0), stop=(ht == NHT - 1),
                                   sem=SPV if ht == NHT - 1 else None)

            def down_half(half, first):
                for dc in range(8):
                    for rtl in range(4):
                        wt("tensor", banks_tok[rtl])
                    if not first:
                        w("tensor", SSE, HIDR_EV)
                        for rtl in range(4):
                            mm(ps_big[:, rtl * 512:(rtl + 1) * 512],
                               xrpack[64:80, rtl * 128:(rtl + 1) * 128],
                               lorapk[64:80, dc * 512:(dc + 1) * 512],
                               start=True, stop=False)
                    for hl in range(NHP):
                        ht = half * NHP + hl
                        sl = dw_i[0] % NDSL
                        qeng = "sync" if sl % 2 == 0 else "gpsimd"
                        wt(qeng, dw_slot_tok[sl])
                        src = (down_g4A[ht // 11, ht % 11, dc] if dc < 4
                               else down_g4B[ht // 11, ht % 11, dc - 4])
                        key = (qeng, "A" if dc < 4 else "B")
                        if key not in dw_seen:
                            dw_seen.add(key)
                            wt(qeng, CC_DOWNA if dc < 4 else CC_DOWNB)
                        lv = dma(qeng, dwring[:, sl * 512:(sl + 1) * 512],
                                 src, SFDs[sl])
                        dw_i[0] += 1
                        w("tensor", SFDs[sl], lv)
                        wt("tensor", hid_tok[ht])
                        dv = None
                        for rtl in range(4):
                            dv = mm(ps_big[:, rtl * 512:(rtl + 1) * 512],
                                    hidt[:, hl * 512 + rtl * 128:
                                         hl * 512 + (rtl + 1) * 128],
                                    dwring[:, sl * 512:(sl + 1) * 512],
                                    start=(first and hl == 0), stop=(hl == NHP - 1),
                                    sem=SFT if rtl == 3 else None)
                        dw_slot_tok[sl] = (SFT, dv)
                    for rtl in range(4):
                        chunk = out4_dram[rtl * 128:(rtl + 1) * 128,
                                          dc * 512:(dc + 1) * 512]
                        if first:
                            if dc == 0 and rtl == 0:
                                w("sync", SO4R, o4r_prev[0])
                            w("scalar", SFT, SFT.n)
                            wt("scalar", ts_tok[0])
                            evv = act(tstage[:, 0:512],
                                      ps_big[:, rtl * 512:(rtl + 1) * 512],
                                      AF.Copy, sem=SEV)
                            banks_tok[rtl] = (SEV, evv)
                            w("sync", SEV, evv)
                            wv = dma("sync", chunk, tstage[:, 0:512], SO4W)
                            ts_tok[0] = (SO4W, wv)
                        else:
                            wt("sync", ts_tok[0])
                            if dc == 0 and rtl == 0:
                                w("sync", SO4W, SO4W.n)
                            lv2 = dma("sync", tstage[:, 0:512], chunk, SO4L)
                            w("vector", SO4L, lv2)
                            w("vector", SFT, SFT.n)
                            wt("vector", ts_tok[1])
                            evv = vop("tensor_add", tstage[:, 512:1024],
                                      ps_big[:, rtl * 512:(rtl + 1) * 512],
                                      tstage[:, 0:512])
                            banks_tok[rtl] = (SVE, evv)
                            ts_tok[0] = (SVE, evv)
                            w("sync", SVE, evv)
                            wv = dma("sync", chunk, tstage[:, 512:1024], SO4W2)
                            ts_tok[1] = (SO4W2, wv)

            gu_half(0)
            down_half(0, True)
            gu_half(1)
            x3t_free = (SFT, SFT.n)
            w("scalar", SPV, hidr_done)
            HIDR_EV = act(xrpack[64:80, :], ps_b[0:16, :], AF.Copy, sem=SSE)
            down_half(1, False)

            for rtl in range(4):
                rt = rc * 4 + rtl
                sl = rtl % 2
                if rtl == 0:
                    w("sync", SO4W2, SO4W2.n)
                wt("sync", x1row_tok[0])
                l4 = dma("sync", x1row[:], out4_dram[rtl * 128:(rtl + 1) * 128, :],
                         SO4R)
                wt("sync", x2r_tok[sl])
                w("sync", SX2, X2_ALL)
                lx = dma("sync", xrow[:, sl * D:(sl + 1) * D],
                         x2_dram[rt * 128:(rt + 1) * 128, :], SX2Rs[sl])
                w("vector", SO4R, l4)
                w("scalar", SO4R, l4)
                tok_ln, nm4, r4 = ln_stats(x1row[:], D, 0 if sl == 0 else 8,
                                           prewait=zrow_tok2[0])
                wt("vector", tok_ln)
                z4 = vop("tensor_scalar", zrow[:], x1row[:], nm4, r4,
                         ALU.add, ALU.mult)
                x1row_tok[0] = (SVE, z4)
                w("vector", SVE, z4)
                w("vector", SX2Rs[sl], lx)
                for _tk in x2tmp_tok:
                    wt("vector", _tk)
                ov = vop("tensor_add", x2tmp[:], zrow[:], xrow[:, sl * D:(sl + 1) * D])
                zrow_tok2[0] = (SVE, ov)
                x2r_tok[sl] = (SVE, ov)
                w("sync", SVE, ov)
                odma = dma("sync", out_ap[rt * 128:(rt + 1) * 128, :], x2tmp[:], SOUT)
                x2tmp_tok = [(SOUT, odma)]
            o4r_prev[0] = SO4R.n

        ffn_rc(0)

        for rtl in range(4):
            rt = 4 + rtl
            sl = rtl % 2
            wt("sync", x2r_tok[sl])
            w("sync", SX2, X2_ALL)
            lx = dma("sync", xrow[:, sl * D:(sl + 1) * D],
                     x2_dram[rt * 128:(rt + 1) * 128, :], SX2Rs[sl])
            w("vector", SX2Rs[sl], lx)
            wt("vector", x1row_tok[0])
            nx3 = vop("tensor_scalar", x1row[:], xrow[:, sl * D:(sl + 1) * D],
                      stats_st[:, 2 * rt + 1:2 * rt + 2],
                      stats_st[:, 2 * rt:2 * rt + 1], ALU.add, ALU.mult)
            x1row_tok[0] = (SVE, nx3)
            x2r_tok[sl] = (SVE, nx3)
            psT_tok = emit_x3_transposes(rtl, nx3)
        ffn_rc(1)

        return finish()


def _shard_inputs(inputs, consts):
    x = np.asarray(inputs["L0_attn_x"]).reshape(R, D)
    wqkv = np.asarray(inputs["L0_attn_w_qkv"])
    wo = np.asarray(inputs["L0_attn_w_o"])
    gate = np.asarray(inputs["L0_ffn_gate_w"])
    up = np.asarray(inputs["L0_ffn_up_w"])
    down = np.asarray(inputs["L0_ffn_down_w"])
    in_maps = []
    for c in range(NCORE):
        qcols = wqkv[:, c * 512:(c + 1) * 512]
        kcols = wqkv[:, 4096 + c * 128:4096 + (c + 1) * 128]
        vcols = wqkv[:, 5120 + c * 128:5120 + (c + 1) * 128]
        dn = down[:, c * 1408:(c + 1) * 1408]
        if dn.shape[1] < 1408:
            dn = np.concatenate(
                [dn, np.zeros((D, 1408 - dn.shape[1]), np.float32)], axis=1)
        m = {
            "x_rows": np.ascontiguousarray(x[c * OWN:(c + 1) * OWN]),
            "wqkv_sl": np.ascontiguousarray(np.concatenate([qcols, kcols, vcols], axis=1)),
            "wo_sl": np.ascontiguousarray(wo[c * 512:(c + 1) * 512]),
            "gate_sl": np.ascontiguousarray(gate[:, c * 512:(c + 1) * 512]),
            "up_sl": np.ascontiguousarray(up[:, c * 512:(c + 1) * 512]),
            "down_sl": np.ascontiguousarray(dn),
            "gate_a": np.asarray(inputs["L0_ffn_gate_a"]),
            "gate_b": np.asarray(inputs["L0_ffn_gate_b"]),
            "up_a": np.asarray(inputs["L0_ffn_up_a"]),
            "up_b": np.asarray(inputs["L0_ffn_up_b"]),
            "down_a": np.asarray(inputs["L0_ffn_down_a"]),
            "down_b": np.asarray(inputs["L0_ffn_down_b"]),
        }
        m.update(consts)
        in_maps.append(m)
    return in_maps


def kernel(**inputs):
    consts = _host_consts()
    in_maps = _shard_inputs(inputs, consts)
    b = Builder()
    nc = b.build()
    res = run_bass_kernel_spmd(nc, in_maps, core_ids=list(range(NCORE)))
    out = np.concatenate([r["out_rows"] for r in res.results], axis=0)
    return out.reshape(B, S, D)


# version 23
# speedup vs baseline: 1.0371x; 1.0371x over previous
import math
import numpy as np
import ml_dtypes
import concourse.bass as bass
import concourse.mybir as mybir
from concourse.bass_utils import run_bass_kernel_spmd

F32 = mybir.dt.float32
BF16 = mybir.dt.bfloat16
AF = mybir.ActivationFunctionType
ALU = mybir.AluOpType
AX = mybir.AxisListType

NCORE = 8
B = 2
S = 4096
R = B * S
D = 4096
HD = 128
QC = 512
WCOLS = 768
DH = 11008
NHT = DH // 128
NHP = NHT // 2
RK = 16
OWN = R // NCORE
EPS = 1e-5
INV_SQRT_HD = 1.0 / math.sqrt(HD)
NEG_BIG = -30000.0

ENGS = ("sync", "tensor", "vector", "scalar", "gpsimd")


def _host_consts():
    c = {}
    c["ident"] = np.eye(128, dtype=ml_dtypes.bfloat16)
    p = np.zeros((128, 128), dtype=np.float32)
    for i in range(128):
        p[i ^ 1, i] = 1.0
    c["perm"] = p.astype(ml_dtypes.bfloat16)
    inv_freq = 1.0 / (10000.0 ** (np.arange(0, HD, 2, dtype=np.float64) / HD))
    ang = np.arange(S, dtype=np.float64)[None, :] * inv_freq[:, None]
    cos, sin = np.cos(ang), np.sin(ang)
    Ct = np.empty((128, S), np.float32)
    St = np.empty((128, S), np.float32)
    Ct[0::2] = cos
    Ct[1::2] = cos
    St[0::2] = -sin
    St[1::2] = sin
    c["rope_c"] = Ct.astype(ml_dtypes.bfloat16)
    c["rope_s"] = St.astype(ml_dtypes.bfloat16)
    masks = np.zeros((128, 4 * 512), np.float32)
    for k in range(4):
        tloc = 128 * k + np.arange(128)[:, None]
        masks[:, k * 512:(k + 1) * 512] = np.where(
            tloc > np.arange(512)[None, :], NEG_BIG, 0.0)
    c["masks"] = masks.astype(ml_dtypes.bfloat16)
    c["ones_col"] = np.ones((128, 1), dtype=ml_dtypes.bfloat16)
    c["ones_row"] = np.ones((1, 128), dtype=ml_dtypes.bfloat16)
    c["ones_row_f"] = np.ones((1, 128), dtype=np.float32)
    c["fconst"] = np.tile(np.array([[0.0, 1.0, EPS, -0.5, -1.0, INV_SQRT_HD, 2.0, 0.5]],
                           dtype=np.float32), (128, 1))
    return c


class Sem:
    def __init__(self, handle):
        self.h = handle
        self.n = 0

    def inc(self, k=1):
        self.n += k
        return self.n


class Builder:
    def __init__(self, taps=(), stop=99):
        self.nc = bass.Bass("TRN2", target_bir_lowering=False)
        self.taps = set(taps)
        self.stop = stop
        self.plan = {e: [] for e in ENGS}
        self.stack = []
        self.t = {}

    def on(self, eng, fn):
        self.plan[eng].append(fn)

    def w(self, eng, sem, val):
        assert val <= sem.n, f"wait on future value {val} > {sem.n}"
        if val > 0:
            self.on(eng, lambda e, s=sem.h, v=val: e.wait_ge(s, v))

    def wt(self, eng, tok):
        if tok is not None:
            self.w(eng, tok[0], tok[1])

    def dma(self, eng, out, in_, sem):
        def f(e, out=out, in_=in_, s=sem.h):
            e.dma_start(out=out, in_=in_).then_inc(s, 16)
        self.on(eng, f)
        return sem.inc(16)

    def sem(self, name):
        g = self.nc.semaphore(name)
        self.stack.append((name, g))
        return Sem(g.__enter__())

    def sbuf(self, name, shape, dtyp):
        g = self.nc.sbuf_tensor(name, shape, dtyp)
        self.stack.append((name, g))
        t = g.__enter__()
        self.t[name] = t
        return t

    def psum(self, name, shape, dtyp=F32):
        g = self.nc.psum_tensor(name, shape, dtyp)
        self.stack.append((name, g))
        t = g.__enter__()
        self.t[name] = t
        return t

    def free_through(self, name):
        while self.stack:
            n, g = self.stack.pop()
            g.__exit__(None, None, None)
            if n == name:
                return
        raise RuntimeError(f"{name} not on stack")

    def emit(self):
        nc = self.nc
        with nc.Block() as block:
            @block.sync
            def _(e):
                for fn in self.plan["sync"]:
                    fn(e)

            @block.tensor
            def _(e):
                for fn in self.plan["tensor"]:
                    fn(e)

            @block.vector
            def _(e):
                for fn in self.plan["vector"]:
                    fn(e)

            @block.scalar
            def _(e):
                for fn in self.plan["scalar"]:
                    fn(e)

            @block.gpsimd
            def _(e):
                for fn in self.plan["gpsimd"]:
                    fn(e)
        while self.stack:
            _, g = self.stack.pop()
            g.__exit__(None, None, None)
        return nc

    def build(self):
        nc = self.nc
        taps = self.taps
        on, w, wt, dma = self.on, self.w, self.wt, self.dma

        def ein(name, shape, dtyp=F32):
            return nc.dram_tensor(name, shape, dtyp, kind="ExternalInput").ap()

        x_in = ein("x_rows", [OWN, D])
        wqkv_in = ein("wqkv_sl", [D, WCOLS])
        wo_in = ein("wo_sl", [512, D])
        gate_in = ein("gate_sl", [DH, 512])
        up_in = ein("up_sl", [DH, 512])
        down_in = ein("down_sl", [D, 1408])
        ga_in = ein("gate_a", [RK, D])
        gb_in = ein("gate_b", [DH, RK])
        ua_in = ein("up_a", [RK, D])
        ub_in = ein("up_b", [DH, RK])
        da_in = ein("down_a", [RK, DH])
        db_in = ein("down_b", [D, RK])
        ident_in = ein("ident", [128, 128], BF16)
        perm_in = ein("perm", [128, 128], BF16)
        ropec_in = ein("rope_c", [128, S], BF16)
        ropes_in = ein("rope_s", [128, S], BF16)
        masks_in = ein("masks", [128, 4 * 512], BF16)
        onesc_in = ein("ones_col", [128, 1], BF16)
        onesr_in = ein("ones_row", [1, 128], BF16)
        onesrf_in = ein("ones_row_f", [1, 128], F32)
        fconst_in = ein("fconst", [128, 8], F32)

        out_ap = nc.dram_tensor("out_rows", [OWN, D], F32, kind="ExternalOutput").ap()

        def idr(name, shape, dtyp):
            return nc.dram_tensor(name, shape, dtyp, kind="Internal").ap()

        wo_bf = idr("wo_bf", [512, D], BF16)
        gate_bf = idr("gate_bf", [DH, 512], BF16)
        up_bf = idr("up_bf", [DH, 512], BF16)
        down_bf = idr("down_bf", [D, 1408], BF16)
        x1t_agin_a = idr("x1t_agin_a", [D, 512], BF16)
        x1t_agin_b = idr("x1t_agin_b", [D, 512], BF16)
        x1t_gaA = idr("x1t_gaA", [NCORE * 2048, 512], BF16)
        x1t_gaB = idr("x1t_gaB", [NCORE * 2048, 512], BF16)
        x1t_gbA = idr("x1t_gbA", [NCORE * 2048, 512], BF16)
        x1t_gbB = idr("x1t_gbB", [NCORE * 2048, 512], BF16)
        wo_agin = idr("wo_agin", [4 * 8 * 128, 512], BF16)
        wo_g = idr("wo_g", [NCORE * 4 * 8 * 128, 512], BF16)
        NHPA, NHPB = 22, 21
        gate_agin = idr("gate_agin", [NHP * 4 * 128, 256], BF16)
        gate_gA = idr("gate_gA", [NCORE * NHPA * 4 * 128, 256], BF16)
        gate_gB = idr("gate_gB", [NCORE * NHPB * 4 * 128, 256], BF16)
        up_agin = idr("up_agin", [NHP * 4 * 128, 256], BF16)
        up_gA = idr("up_gA", [NCORE * NHPA * 4 * 128, 256], BF16)
        up_gB = idr("up_gB", [NCORE * NHPB * 4 * 128, 256], BF16)
        down_aginA = idr("down_aginA", [11 * 4 * 128, 512], BF16)
        down_aginB = idr("down_aginB", [11 * 4 * 128, 512], BF16)
        down_gA = idr("down_gA", [NCORE * 11 * 4 * 128, 512], BF16)
        down_gB = idr("down_gB", [NCORE * 11 * 4 * 128, 512], BF16)
        out4_dram = idr("out4_dram", [512, D], BF16)
        ga_bfd = idr("ga_bfd", [RK, D], BF16)
        ua_bfd = idr("ua_bfd", [RK, D], BF16)
        da_bfd = idr("da_bfd", [RK, DH], BF16)
        gb_bfd = idr("gb_bfd", [DH, RK], BF16)
        ub_bfd = idr("ub_bfd", [DH, RK], BF16)
        db_bfd = idr("db_bfd", [D, RK], BF16)
        at_agin_a = idr("at_agin_a", [QC, S], BF16)
        at_agin_b = idr("at_agin_b", [QC, S], BF16)
        atg2 = idr("atg2", [2 * NCORE * QC, S], BF16)
        x2_dram = idr("x2_dram", [OWN, D], F32)
        gaT_d = idr("gaT_d", [128, 32 * RK], BF16)
        uaT_d = idr("uaT_d", [128, 32 * RK], BF16)
        daT_d = idr("daT_d", [128, NHT * RK], BF16)
        stats_agin = idr("stats_agin", [OWN, 2], F32)
        stats_g = idr("stats_g", [NCORE * OWN, 2], F32)

        wo_g4 = wo_g.rearrange("(r dt cc p) f -> r dt cc p f", r=NCORE, dt=4, cc=8, p=128)
        gate_g4A = gate_gA.rearrange("(r hp dt p) f -> r hp dt p f", r=NCORE, hp=NHPA, dt=4, p=128)
        gate_g4B = gate_gB.rearrange("(r hp dt p) f -> r hp dt p f", r=NCORE, hp=NHPB, dt=4, p=128)
        up_g4A = up_gA.rearrange("(r hp dt p) f -> r hp dt p f", r=NCORE, hp=NHPA, dt=4, p=128)
        up_g4B = up_gB.rearrange("(r hp dt p) f -> r hp dt p f", r=NCORE, hp=NHPB, dt=4, p=128)
        down_g4A = down_gA.rearrange("(r ht dc p) f -> r ht dc p f", r=NCORE, ht=11, dc=4, p=128)
        down_g4B = down_gB.rearrange("(r ht dc p) f -> r ht dc p f", r=NCORE, ht=11, dc=4, p=128)
        x1t_g3aA = x1t_gaA.rearrange("(r d) f -> r d f", r=NCORE)
        x1t_g3aB = x1t_gaB.rearrange("(r d) f -> r d f", r=NCORE)
        x1t_g3bA = x1t_gbA.rearrange("(r d) f -> r d f", r=NCORE)
        x1t_g3bB = x1t_gbB.rearrange("(r d) f -> r d f", r=NCORE)
        x1t_ag2a = x1t_agin_a.rearrange("(dt p) f -> dt p f", p=128)
        x1t_ag2b = x1t_agin_b.rearrange("(dt p) f -> dt p f", p=128)
        wo_ag2 = wo_agin.rearrange("(dt cc p) f -> dt cc p f", dt=4, cc=8, p=128)
        gate_ag2 = gate_agin.rearrange("(hp dt p) f -> hp dt p f", hp=NHP, dt=4, p=128)
        up_ag2 = up_agin.rearrange("(hp dt p) f -> hp dt p f", hp=NHP, dt=4, p=128)
        down_ag2A = down_aginA.rearrange("(ht dc p) f -> ht dc p f", ht=11, dc=4, p=128)
        down_ag2B = down_aginB.rearrange("(ht dc p) f -> ht dc p f", ht=11, dc=4, p=128)

        SCONST = self.sem("sconst")
        SCONST2 = self.sem("sconst2")
        SPREG = self.sem("spreg")
        SPREGQ = self.sem("spregq")
        SPREGW = self.sem("spregw")
        SPREGT = self.sem("spregt")
        SPRES = self.sem("spres")
        SPRES1 = self.sem("spres1")
        SXs = [self.sem(f"sx{i}") for i in range(2)]
        SX1T = self.sem("sx1t")
        SX1TB = self.sem("sx1tb")
        SCCs = [self.sem(f"scc{i}") for i in range(8)]
        SQW = self.sem("sqw")
        SQs = [self.sem(f"sq{i}") for i in range(8)]
        SLORA = self.sem("slora")
        STT = self.sem("stt")
        STE = self.sem("ste")
        SVE = self.sem("sve")
        SSE = self.sem("sse")
        SEV = self.sem("sev")
        SPV = self.sem("spv")
        SEXP = self.sem("sexp")
        SMS = self.sem("sms")
        SAGs = [self.sem(f"sag{i}") for i in range(4)]
        SWOs = [self.sem(f"swo{i}") for i in range(8)]
        SWOT = self.sem("swot")
        SAT = self.sem("sat")
        SFWs = [self.sem(f"sfw{i}") for i in range(8)]
        SFDs = [self.sem(f"sfd{i}") for i in range(8)]
        SFT = self.sem("sft")
        SO4W = self.sem("so4w")
        SO4W2 = self.sem("so4w2")
        SO4L = self.sem("so4l")
        SO4R = self.sem("so4r")
        SX2 = self.sem("sx2")
        SX2Rs = [self.sem(f"sx2r{i}") for i in range(2)]
        SOUT = self.sem("sout")
        SSTAs = [self.sem(f"ssta{i}") for i in range(4)]
        SSRs = [self.sem(f"ssr{i}") for i in range(2)]

        pid = nc.partition_id()

        ident = self.sbuf("c_ident", [128, 128], BF16)
        onesc = self.sbuf("c_onesc", [128, 1], BF16)
        onesr = self.sbuf("c_onesr", [1, 128], BF16)
        onesrf = self.sbuf("c_onesrf", [1, 128], F32)
        st_small = self.sbuf("st_small", [128, 32], F32)
        stats_st = self.sbuf("stats_st", [128, 16], F32)

        cpool = self.sbuf("cpool", [128, 8], F32)
        for _i, _v in enumerate([0.0, 1.0, EPS, -0.5, -1.0, INV_SQRT_HD, 2.0, 0.5]):
            nc.const_aps.register(F32, _v, cpool[:, _i:_i + 1])

        ps_big = self.psum("ps_big", [128, 2048])
        ps_a = self.psum("ps_a", [128, 512])
        ps_b = self.psum("ps_b", [128, 512])
        ps_c = self.psum("ps_c", [128, 512])
        psT = self.psum("psT", [128, 1024], BF16)

        def act(out, in_, func, bias=0.0, scale=1.0, accum=None, sem=None):
            sem = sem or SSE
            def f(e, out=out, in_=in_, func=func, bias=bias, scale=scale,
                  accum=accum, s=sem.h):
                e.activation(out, in_, func, bias=bias, scale=scale,
                             accum_out=accum).then_inc(s, 1)
            on("scalar", f)
            return sem.inc()

        def vop(name, *args, sem=None):
            sem = sem or SVE
            def f(e, name=name, args=args, s=sem.h):
                getattr(e, name)(*args).then_inc(s, 1)
            on("vector", f)
            return sem.inc()

        def mm(out, lhsT, rhs, start, stop, sem=None):
            def f(e, out=out, l=lhsT, r=rhs, st=start, sp=stop,
                  s=(sem.h if sem else None)):
                ins = e.matmul(out, l, r, start=st, stop=sp)
                if s is not None:
                    ins.then_inc(s, 1)
            on("tensor", f)
            return sem.inc() if sem else None

        def trans(out, in_, sem=None):
            npt = in_.shape[0]
            idap = ident[:] if npt >= 128 else ident[0:npt, 0:npt]
            def f(e, out=out, in_=in_, idap=idap, s=(sem.h if sem else None)):
                ins = e.transpose(out, in_, idap)
                if s is not None:
                    ins.then_inc(s, 1)
            on("tensor", f)
            return sem.inc() if sem else None

        def ln_stats(src_ap, n_free, base, prewait=None):
            c = lambda k: st_small[:, base + k:base + k + 1]
            nm, rstd, ssum, ssq, var, tt2 = (c(0), c(1), c(2), c(3), c(5), c(6))
            v1 = vop("reduce_sum", ssum, src_ap, AX.X)
            if prewait is not None:
                wt("scalar", prewait)
            s1 = act(zrow[:, 0:n_free], src_ap, AF.Square, accum=ssq)
            w("vector", SVE, v1)
            v2 = vop("tensor_scalar_mul", nm, ssum, -1.0 / n_free)
            w("vector", SVE, v2)
            v3 = vop("tensor_mul", tt2, nm, nm)
            w("vector", SSE, s1)
            v4 = vop("tensor_scalar_mul", var, ssq, 1.0 / n_free)
            w("vector", SVE, v4)
            v5 = vop("tensor_sub", var, var, tt2)
            w("scalar", SVE, v5)
            s2 = act(var, var, AF.Ln, bias=EPS)
            w("scalar", SSE, s2)
            s3 = act(rstd, var, AF.Exp, scale=-0.5)
            return (SSE, s3), nm, rstd

        def _tap(name, src_ap, tok):
            if name not in taps:
                return
            o = nc.dram_tensor("tap_" + name, list(src_ap.shape), src_ap.dtype,
                               kind="ExternalOutput").ap()
            for _t in (tok if isinstance(tok, list) else [tok]):
                wt("sync", _t)
            dma("sync", o, src_ap, SOUT)

        def finish():
            w("sync", SOUT, SOUT.n)
            return self.emit()

        for dst, src in ((ident, ident_in), (onesc, onesc_in), (onesr, onesr_in),
                         (onesrf, onesrf_in), (cpool, fconst_in)):
            dma("sync", dst[:], src, SCONST)
        CONSTS1 = SCONST.n
        w("scalar", SCONST, CONSTS1)
        w("vector", SCONST, CONSTS1)

        qkvw = self.sbuf("qkvw", [128, 32 * WCOLS], BF16)
        for dtt in range(32):
            dma("gpsimd", qkvw[:, dtt * WCOLS:(dtt + 1) * WCOLS],
                wqkv_in[dtt * 128:(dtt + 1) * 128, :], SQW)
        QKVW_L = SQW.n
        c_wo = dma("gpsimd", wo_bf, wo_in, SPREGW)
        for _d, _s in ((ga_bfd, ga_in), (ua_bfd, ua_in), (da_bfd, da_in),
                       (gb_bfd, gb_in), (ub_bfd, ub_in), (db_bfd, db_in)):
            dma("gpsimd", _d, _s, SPREG)
        LORA_CASTS = SPREG.n

        perm = self.sbuf("c_perm", [128, 128], BF16)
        ropec = self.sbuf("c_ropec", [128, S], BF16)
        ropes = self.sbuf("c_ropes", [128, S], BF16)
        maskt = self.sbuf("c_masks", [128, 4 * 512], BF16)
        for dst, src in ((perm, perm_in), (ropec, ropec_in), (ropes, ropes_in),
                         (maskt, masks_in)):
            dma("sync", dst[:], src, SCONST2)
        CONSTS2 = SCONST2.n

        SW = 5632
        stripe = self.sbuf("stripe", [128, 2 * SW], BF16)
        xrow = self.sbuf("xrow", [128, 2 * D], F32)
        tstage = self.sbuf("tstage", [128, 1024], BF16)
        x1row = self.sbuf("x1row", [128, D], BF16)
        zrow = self.sbuf("zrow", [128, D], BF16)
        gaT = self.sbuf("gaT_p", [128, 32 * RK], BF16)
        uaT = self.sbuf("uaT_p", [128, 32 * RK], BF16)
        daT = self.sbuf("daT_p", [128, NHT * RK], BF16)
        lora_b1 = self.sbuf("lora_b1", [16, 2 * 1024], BF16)


        x1t_ag2ap = x1t_agin_a.rearrange("(dt p) f -> p dt f", p=128)
        x1t_ag2bp = x1t_agin_b.rearrange("(dt p) f -> p dt f", p=128)
        tst3 = tstage.ap().rearrange("p (k f) -> p k f", k=8)
        xfree_v = [None, None]
        xfree_s = [None, None]
        sta_tok = [None] * 4
        psT_tok = None
        for rt in range(8):
            sl = rt % 2
            wt("sync", xfree_v[sl])
            wt("sync", xfree_s[sl])
            lx = dma("sync", xrow[:, sl * D:(sl + 1) * D],
                     x_in[rt * 128:(rt + 1) * 128, :], SXs[sl])
            w("vector", SXs[sl], lx)
            w("scalar", SXs[sl], lx)
            nv = vop("tensor_copy", x1row[:], xrow[:, sl * D:(sl + 1) * D])
            if sta_tok[rt % 4] is not None:
                w("vector", SSTAs[rt % 4], sta_tok[rt % 4])
                w("scalar", SSTAs[rt % 4], sta_tok[rt % 4])
            tok_ln, nm1, rstd1 = ln_stats(xrow[:, sl * D:(sl + 1) * D], D,
                                          (rt % 4) * 8)
            xfree_v[sl] = (SVE, SVE.n)
            xfree_s[sl] = (SSE, SSE.n)
            wt("sync", tok_ln)
            dma("sync", stats_agin[rt * 128:(rt + 1) * 128, 0:2],
                st_small[:, (rt % 4) * 8:(rt % 4) * 8 + 2], SSTAs[rt % 4])
            sta_tok[rt % 4] = SSTAs[rt % 4].n
            for g in range(4):
                w("tensor", SVE, nv)
                wt("tensor", psT_tok)
                if rt == 0 and g == 0:
                    w("tensor", SCONST, CONSTS1)
                for k in range(8):
                    dtt = g * 8 + k
                    tk = trans(psT[:, k * 128:(k + 1) * 128],
                               x1row[:, dtt * 128:(dtt + 1) * 128],
                               sem=STE if k == 7 else None)
                w("vector", STE, tk)
                w("vector", SX1T, SX1T.n)
                w("vector", SX1TB, SX1TB.n)
                ev = vop("tensor_copy", tstage[:], psT[:])
                psT_tok = (SVE, ev)
                ag2p = x1t_ag2ap if rt < 4 else x1t_ag2bp
                sem1t = SX1T if rt < 4 else SX1TB
                qeng = "sync" if g % 2 == 0 else "scalar"
                w(qeng, SVE, ev)
                dma(qeng, ag2p[:, g * 8:(g + 1) * 8,
                               (rt % 4) * 128:(rt % 4 + 1) * 128],
                    tst3, sem1t)
            if rt == 3:
                w("gpsimd", SX1T, SX1T.n)
                on("gpsimd", lambda e: e.collective_compute(
                    "AllGather", ALU.bypass, replica_groups=[list(range(NCORE))],
                    ins=[x1t_agin_a[0:2048, :]], outs=[x1t_gaA]).then_inc(SCCs[0].h, 1))
                CC_X1TA1 = (SCCs[0], SCCs[0].inc())
                on("gpsimd", lambda e: e.collective_compute(
                    "AllGather", ALU.bypass, replica_groups=[list(range(NCORE))],
                    ins=[x1t_agin_a[2048:4096, :]], outs=[x1t_gaB]).then_inc(SCCs[0].h, 1))
                CC_X1TA2 = (SCCs[0], SCCs[0].inc())
            if rt == 7:
                for _s in SSTAs:
                    w("gpsimd", _s, _s.n)
                on("gpsimd", lambda e: e.collective_compute(
                    "AllGather", ALU.bypass, replica_groups=[list(range(NCORE))],
                    ins=[stats_agin], outs=[stats_g]).then_inc(SCCs[6].h, 1))
                CC_STATS = (SCCs[6], SCCs[6].inc())
                w("gpsimd", SX1TB, SX1TB.n)
                on("gpsimd", lambda e: e.collective_compute(
                    "AllGather", ALU.bypass, replica_groups=[list(range(NCORE))],
                    ins=[x1t_agin_b[0:2048, :]], outs=[x1t_gbA]).then_inc(SCCs[6].h, 1))
                CC_X1TB1 = (SCCs[6], SCCs[6].inc())
                on("gpsimd", lambda e: e.collective_compute(
                    "AllGather", ALU.bypass, replica_groups=[list(range(NCORE))],
                    ins=[x1t_agin_b[2048:4096, :]], outs=[x1t_gbB]).then_inc(SCCs[6].h, 1))
                CC_X1TB2 = (SCCs[6], SCCs[6].inc())
        c_gate = dma("gpsimd", gate_bf, gate_in, SPREG)
        CAST_GATE = SPREG.n
        c_up = dma("gpsimd", up_bf, up_in, SPREG)
        CAST_UP = SPREG.n
        c_down = dma("gpsimd", down_bf, down_in, SPREG)
        CASTS_ALL = SPREG.n
        w("gpsimd", SPREGW, c_wo)
        for dtl in range(4):
            for cc in range(8):
                dma("gpsimd", wo_ag2[dtl, cc],
                    wo_bf[dtl * 128:(dtl + 1) * 128, cc * 512:(cc + 1) * 512],
                    SPREGT)
        WO_TILES = SPREGT.n

        psT_tok2 = psT_tok
        lf_tok = None
        lf2_tok = None

        def lora_a(inp, n_dt, dstT):
            nonlocal psT_tok2, lf_tok
            ngg = (n_dt + 7) // 8
            ch_tok = [None, None]
            if lf_tok is None:
                w("sync", SPREG, LORA_CASTS)
            for gg in range(ngg):
                kk = min(8, n_dt - gg * 8)
                par = gg % 2
                wt("sync", ch_tok[par])
                if gg == 0:
                    wt("sync", lf_tok)
                lv = dma("sync", lora_b1[0:16, par * 1024:par * 1024 + kk * 128],
                         inp[:, gg * 1024:gg * 1024 + kk * 128], SLORA)
                w("tensor", SLORA, lv)
                wt("tensor", psT_tok2)
                for k in range(kk):
                    tk = trans(psT[:, k * RK:(k + 1) * RK],
                               lora_b1[0:16, par * 1024 + k * 128:
                                       par * 1024 + (k + 1) * 128],
                               sem=STE if k == kk - 1 else None)
                ch_tok[par] = (STE, tk)
                w("vector", STE, tk)
                ev = vop("tensor_copy", dstT[:, gg * 8 * RK:gg * 8 * RK + kk * RK],
                         psT[:, 0:kk * RK])
                psT_tok2 = (SVE, ev)
                lf_tok = (STE, tk)

        lora_a(ga_bfd, 32, gaT)
        lora_a(ua_bfd, 32, uaT)
        lora_a(da_bfd, NHT, daT)
        psT_tok = psT_tok2
        w("sync", SVE, SVE.n)
        for _d, _t in ((gaT_d, gaT), (uaT_d, uaT), (daT_d, daT)):
            dma("sync", _d, _t[:], SLORA)
        LORA_STASH = SLORA.n

        s_par = [0]
        s_tok = [None, None]

        def stripe_job(src_bf, width, writes, cast_wait):
            p = s_par[0]
            s_par[0] ^= 1
            base = p * SW
            sem_p = SPRES if p == 0 else SPRES1
            w("scalar", SPREG, cast_wait)
            wt("scalar", s_tok[p])
            def tdma(e, src_bf=src_bf, width=width, base=base, sem_p=sem_p):
                e.dma_start(out=stripe[:, base:base + width], in_=src_bf,
                            transpose=True).then_inc(sem_p.h, 16)
            on("scalar", tdma)
            w("scalar", sem_p, sem_p.inc(16))
            last = None
            for dst, off, wdt in writes:
                last = dma("scalar", dst, stripe[:, base + off:base + off + wdt], sem_p)
            s_tok[p] = (sem_p, last)

        for dtl in range(4):
            for hplo, hphi in ((0, NHPA), (NHPA, NHP)):
                stripe_job(gate_bf[hplo * 256:hphi * 256,
                                   dtl * 128:(dtl + 1) * 128],
                           (hphi - hplo) * 256,
                           [(gate_ag2[hp, dtl], (hp - hplo) * 256, 256)
                            for hp in range(hplo, hphi)],
                           CAST_GATE)
        GATE_S, GATE_S1 = SPRES.n, SPRES1.n
        for dtl in range(4):
            for hplo, hphi in ((0, NHPA), (NHPA, NHP)):
                stripe_job(up_bf[hplo * 256:hphi * 256,
                                 dtl * 128:(dtl + 1) * 128],
                           (hphi - hplo) * 256,
                           [(up_ag2[hp, dtl], (hp - hplo) * 256, 256)
                            for hp in range(hplo, hphi)],
                           CAST_UP)
        UP_S, UP_S1 = SPRES.n, SPRES1.n
        for htl in range(11):
            for half in range(2):
                ag2 = down_ag2A if half == 0 else down_ag2B
                stripe_job(down_bf[half * 2048:(half + 1) * 2048,
                                   htl * 128:(htl + 1) * 128], 2048,
                           [(ag2[htl, dcl], dcl * 512, 512) for dcl in range(4)],
                           CASTS_ALL)
        PREPASS_S = SPRES.n
        PREPASS_S1 = SPRES1.n

        SPLIT_R = NHPA * 4 * 128

        def emit_gather(agin, lo, hi, out, scc):
            on("gpsimd", lambda e, agin=agin, lo=lo, hi=hi, out=out, scc=scc:
                e.collective_compute(
                    "AllGather", ALU.bypass, replica_groups=[list(range(NCORE))],
                    ins=[agin[lo:hi, :]], outs=[out]).then_inc(scc.h, 1))
            return (scc, scc.inc())

        def emit_gate_ags():
            w("gpsimd", SPRES, GATE_S)
            w("gpsimd", SPRES1, GATE_S1)
            ra = emit_gather(gate_agin, 0, SPLIT_R, gate_gA, SCCs[2])
            rb = emit_gather(gate_agin, SPLIT_R, NHP * 4 * 128, gate_gB, SCCs[2])
            return ra, rb

        def emit_up_ags():
            w("gpsimd", SPRES, UP_S)
            w("gpsimd", SPRES1, UP_S1)
            ra = emit_gather(up_agin, 0, SPLIT_R, up_gA, SCCs[3])
            return ra

        def emit_upb_ag():
            return emit_gather(up_agin, SPLIT_R, NHP * 4 * 128, up_gB, SCCs[3])

        def emit_weight_ags_a():
            w("gpsimd", SPREGT, WO_TILES)
            on("gpsimd", lambda e: e.collective_compute(
                "AllGather", ALU.bypass, replica_groups=[list(range(NCORE))],
                ins=[wo_agin], outs=[wo_g]).then_inc(SCCs[1].h, 1))
            return (SCCs[1], SCCs[1].inc())

        def emit_down_ag():
            w("gpsimd", SPRES, PREPASS_S)
            w("gpsimd", SPRES1, PREPASS_S1)
            on("gpsimd", lambda e: e.collective_compute(
                "AllGather", ALU.bypass, replica_groups=[list(range(NCORE))],
                ins=[down_aginA], outs=[down_gA]).then_inc(SCCs[4].h, 1))
            ra = (SCCs[4], SCCs[4].inc())
            on("gpsimd", lambda e: e.collective_compute(
                "AllGather", ALU.bypass, replica_groups=[list(range(NCORE))],
                ins=[down_aginB], outs=[down_gB]).then_inc(SCCs[4].h, 1))
            rb = (SCCs[4], SCCs[4].inc())
            return ra, rb


        if self.stop <= 1:
            return finish()

        on("gpsimd", lambda e: e.wait_ge(STT.h, 256))
        CC_WO = emit_weight_ags_a()
        CC_GATEA, CC_GATEB = emit_gate_ags()

        self.free_through("xrow")
        mxs = self.sbuf("mxs", [128, 33], BF16)
        bias_bh = self.sbuf("bias_bh", [128, 1], F32)
        ptst = self.sbuf("ptst", [128, 2 * 1024], BF16)
        rcln = self.sbuf("rcln", [1, 2 * 512], F32)
        rcb = self.sbuf("rcb", [1, 2 * 512], BF16)
        recipB = self.sbuf("recipB", [128, 2 * 512], BF16)
        aoraw = self.sbuf("aoraw", [128, 2 * 512], BF16)
        sums_sb = self.sbuf("sums_sb", [1, 2 * 512], F32)
        aostage = self.sbuf("aostage", [128, 2 * 512], BF16)
        smax2 = self.sbuf("smax2", [1, 2], F32)
        x1ring = self.sbuf("x1ring", [128, 8 * 512], BF16)
        qts = self.sbuf("qts", [128, 4 * S], BF16)
        kts = self.sbuf("kts", [128, S], BF16)
        vts = self.sbuf("vts", [128, S], BF16)
        ev_q = self.sbuf("ev_q", [128, 7 * 512], BF16)
        rtmp1 = self.sbuf("rtmp1", [128, 512], F32)
        rtmp2 = self.sbuf("rtmp2", [128, 512], F32)
        rsh = self.sbuf("rsh", [128, 512], BF16)
        stat_rows = self.sbuf("stat_rows", [1, 2 * 1024], F32)
        svw = stat_rows.ap().rearrange("p (s t o) -> p s o t", s=2, o=2)
        rows_bf = self.sbuf("rows_bf", [1, 2 * 1536], BF16)
        rstdB = self.sbuf("rstdB", [128, 2 * 512], F32)
        wsum_sb = self.sbuf("wsum_sb", [1, WCOLS], BF16)

        w("sync", SVE, SVE.n)
        w("sync", SSE, SSE.n)
        w("sync", STE, STE.n)
        w("sync", SX1T, SX1T.n)
        w("sync", SX1TB, SX1TB.n)
        w("vector", SX1T, SX1T.n)
        w("vector", SX1TB, SX1TB.n)
        w("vector", STE, STE.n)
        w("vector", SLORA, LORA_STASH)
        w("tensor", SQW, QKVW_L)
        w("tensor", SCONST2, CONSTS2)

        for dtt in range(32):
            mm(ps_c[0:1, 0:512], onesc[:],
               qkvw[:, dtt * WCOLS:dtt * WCOLS + 512],
               start=(dtt == 0), stop=(dtt == 31),
               sem=SPV if dtt == 31 else None)
        wsA = SPV.n
        w("vector", SPV, wsA)
        e1 = vop("tensor_copy", wsum_sb[0:1, 0:512], ps_c[0:1, 0:512])
        w("tensor", SVE, e1)
        for dtt in range(32):
            mm(ps_c[0:1, 0:256], onesc[:],
               qkvw[:, dtt * WCOLS + 512:dtt * WCOLS + 768],
               start=(dtt == 0), stop=(dtt == 31),
               sem=SPV if dtt == 31 else None)
        wsB = SPV.n
        w("vector", SPV, wsB)
        e2 = vop("tensor_copy", wsum_sb[0:1, 512:768], ps_c[0:1, 0:256])

        evac_tok = [None] * 6
        evq_rd_tok = [[] for _ in range(7)]
        pending_v = [None, None]
        rope_vc = [None]
        psc_tok = (SVE, e2)
        srow_free = [None, None]
        SHUF_MASK = [i ^ 1 for i in range(32)]
        aost_tok = [None] * 4
        aoraw_tok = [None, None]
        sums_tok = [None, None]
        psab_tok2 = [None]
        pending_div = [None]
        ptst_tok = [None, None]
        pshalf_tok = [None, None]
        psab_tok = None
        qts_tok = [None]
        x1t_seen = set()
        tile_i = 0

        GRC_ORDER = [0, 2, 4, 6, 1, 3, 5, 7]
        for b in range(B):
            for idx in range(8):
                rcb_i = GRC_ORDER[idx]
                grc = b * 8 + rcb_i
                rank, col = grc // 2, (grc % 2) * 512
                if b == 1 and idx == 0:
                    wt("vector", qts_tok[0])
                for dtt in range(32):
                    sl = tile_i % 8
                    if tile_i >= 8:
                        w("sync", STT, tile_i - 7)
                    if grc % 2 == 0:
                        g3, cctok = ((x1t_g3aA, CC_X1TA1) if dtt < 16
                                     else (x1t_g3aB, CC_X1TA2))
                    else:
                        g3, cctok = ((x1t_g3bA, CC_X1TB1) if dtt < 16
                                     else (x1t_g3bB, CC_X1TB2))
                    if id(cctok) not in x1t_seen:
                        x1t_seen.add(id(cctok))
                        wt("sync", cctok)
                    lv = dma("sync", x1ring[:, sl * 512:(sl + 1) * 512],
                             g3[rank, (dtt % 16) * 128:(dtt % 16 + 1) * 128, :],
                             SQs[sl])
                    w("tensor", SQs[sl], lv)
                    for wti in range(6):
                        if dtt == 0:
                            wt("tensor", evac_tok[wti])
                        dst = (ps_big[:, wti * 512:(wti + 1) * 512] if wti < 4
                               else (ps_a[:] if wti == 4 else ps_b[:]))
                        mm(dst,
                           qkvw[:, dtt * WCOLS + wti * 128:dtt * WCOLS + (wti + 1) * 128],
                           x1ring[:, sl * 512:(sl + 1) * 512],
                           start=(dtt == 0), stop=False,
                           sem=STT if wti == 5 else None)
                    tile_i += 1
                ssl = idx % 2
                sb0 = ssl * 1024
                rb0 = ssl * 1536
                wt("sync", srow_free[ssl])
                if b == 0 and idx == 0:
                    wt("sync", CC_STATS)
                g0 = rank * OWN + (grc % 2) * 512
                l2 = dma("sync", stat_rows[0:1, sb0:sb0 + 1024],
                         stats_g[g0:g0 + 512, 0:2].rearrange(
                             "(a t) o -> a (t o)", a=1), SSRs[ssl])
                w("vector", SSRs[ssl], l2)
                nm_v = svw[0:1, ssl, 0, :]
                rs_v = svw[0:1, ssl, 1, :]
                nmb = vop("tensor_copy", rows_bf[0:1, rb0:rb0 + 512], nm_v)
                rhi = vop("tensor_copy", rows_bf[0:1, rb0 + 512:rb0 + 1024],
                          rs_v)
                w("vector", SVE, rhi)
                rlo = vop("tensor_sub", rows_bf[0:1, rb0 + 1024:rb0 + 1536],
                          rs_v, rows_bf[0:1, rb0 + 512:rb0 + 1024])
                srow_free[ssl] = (SVE, rlo)
                wt("tensor", psc_tok)
                w("tensor", SVE, rlo)
                mm(ps_c[:], onesr[:], rows_bf[0:1, rb0 + 512:rb0 + 1024],
                   start=True, stop=False)
                bmv = mm(ps_c[:], onesr[:], rows_bf[0:1, rb0 + 1024:rb0 + 1536],
                         start=False, stop=True, sem=SPV)
                w("vector", SPV, bmv)
                cb1 = vop("tensor_copy", rstdB[:, 0:512], ps_c[:])
                w("vector", SVE, cb1)
                cb2 = vop("tensor_scalar_mul", rstdB[:, 512:1024],
                          rstdB[:, 0:512], INV_SQRT_HD)
                psc_tok = (SVE, cb1)
                corr = None
                for wti in range(6):
                    dst = (ps_big[:, wti * 512:(wti + 1) * 512] if wti < 4
                           else (ps_a[:] if wti == 4 else ps_b[:]))
                    corr = mm(dst, wsum_sb[0:1, wti * 128:(wti + 1) * 128],
                              rows_bf[0:1, rb0:rb0 + 512],
                              start=False, stop=True,
                              sem=SPV if wti == 5 else None)
                qkv_done = STT.n
                tcol = rcb_i * 512
                vpar = idx % 2
                for wti in range(6):
                    slot = wti if wti < 5 else 5 + vpar
                    w("vector", STT, qkv_done)
                    w("vector", SPV, corr)
                    if wti == 0:
                        w("vector", SVE, cb2)
                    for _tk in evq_rd_tok[slot]:
                        wt("vector", _tk)
                    src = (ps_big[:, wti * 512:(wti + 1) * 512] if wti < 4
                           else (ps_a[:] if wti == 4 else ps_b[:]))
                    evv = vop("tensor_mul",
                              ev_q[:, slot * 512:(slot + 1) * 512], src,
                              rstdB[:, 512:1024] if wti < 4 else rstdB[:, 0:512])
                    evac_tok[wti] = (SVE, evv)
                vc = rope_vc[0]
                for j in range(5):
                    evq = ev_q[:, j * 512:(j + 1) * 512]
                    wt("vector", evac_tok[j])
                    if vc is not None:
                        w("vector", SVE, vc)
                    sh = vop("stream_shuffle", rsh[:], evq, SHUF_MASK)
                    va = vop("tensor_mul", rtmp1[:], evq, ropec[:, tcol:tcol + 512])
                    w("vector", SVE, sh)
                    vb = vop("tensor_mul", rtmp2[:], rsh[:], ropes[:, tcol:tcol + 512])
                    w("vector", SVE, vb)
                    dst = (qts[:, j * S + tcol:j * S + tcol + 512] if j < 4
                           else kts[:, tcol:tcol + 512])
                    vc = vop("tensor_add", dst, rtmp1[:], rtmp2[:])
                    rope_vc[0] = vc
                    evq_rd_tok[j] = [(SVE, va)]
                def emit_vtrans():
                    nonlocal psT_tok
                    vsl, tc = pending_v[0]
                    w("tensor", SVE, pending_v[1])
                    wt("tensor", psT_tok)
                    for k in range(4):
                        tk = trans(psT[:, k * 128:(k + 1) * 128],
                                   ev_q[:, vsl * 512 + k * 128:
                                        vsl * 512 + (k + 1) * 128],
                                   sem=STE if k == 3 else None)
                    w("vector", STE, tk)
                    ev2 = vop("tensor_copy", vts[:, tc:tc + 512], psT[:, 0:512])
                    psT_tok = (SVE, ev2)
                    evq_rd_tok[vsl] = [(STE, tk)]
                pending_v_new = ((5 + vpar, tcol), evac_tok[5][1])
                if pending_v[0] is not None:
                    emit_vtrans()
                pending_v[0] = pending_v_new[0]
                pending_v[1] = pending_v_new[1]

            if pending_v[0] is not None:
                vslf, tcf = pending_v[0]
                w("tensor", SVE, pending_v[1])
                wt("tensor", psT_tok)
                for k in range(4):
                    tkf = trans(psT[:, k * 128:(k + 1) * 128],
                                ev_q[:, vslf * 512 + k * 128:
                                     vslf * 512 + (k + 1) * 128],
                                sem=STE if k == 3 else None)
                w("vector", STE, tkf)
                ev2f = vop("tensor_copy", vts[:, tcf:tcf + 512], psT[:, 0:512])
                psT_tok = (SVE, ev2f)
                evq_rd_tok[vslf] = [(STE, tkf)]
                pending_v[0] = None

            if b == 1:
                w("gpsimd", STT, 512)
                for _s in SAGs[0:2]:
                    w("gpsimd", _s, _s.n)
                on("gpsimd", lambda e: e.collective_compute(
                    "AllGather", ALU.bypass, replica_groups=[list(range(NCORE))],
                    ins=[at_agin_a], outs=[atg2[0:NCORE * QC, :]]).then_inc(SCCs[5].h, 1))
                CC_ATA = (SCCs[5], SCCs[5].inc())
                CC_UPA = emit_up_ags()
                CC_UPB = emit_upb_ag()
            for h in range(4):
                bh = b * 4 + h
                p1_tok = [pshalf_tok[0], pshalf_tok[1]]
                for qt in range(32):
                    par = qt % 2
                    wt("tensor", p1_tok[par])
                    step = 16 if qt < 16 else 32
                    cnt = (qt + 1) * 128 // step
                    pv = mm(ps_big[:, par * 1024:par * 1024 + cnt],
                            qts[:, h * S + qt * 128:h * S + (qt + 1) * 128],
                            kts[:, 0:(qt + 1) * 128:step],
                            start=True, stop=True, sem=SPV)
                    w("vector", SPV, pv)
                    mv = vop("reduce_max", mxs[:, qt:qt + 1],
                             ps_big[:, par * 1024:par * 1024 + cnt], AX.X)
                    p1_tok[par] = (SVE, mv)
                pshalf_tok = p1_tok
                w("vector", SVE, SVE.n)
                mv1 = vop("reduce_max", mxs[:, 32:33], mxs[:, 0:32], AX.X)
                wt("tensor", psT_tok)
                w("tensor", SVE, mv1)
                tv1 = trans(psT[0:1, 0:128], mxs[:, 32:33], sem=STE)
                w("vector", STE, tv1)
                mv2 = vop("reduce_max", smax2[0:1, 0:1], psT[0:1, 0:128], AX.X)
                psT_tok = (SVE, mv2)
                w("vector", SVE, mv2)
                mv3 = vop("tensor_scalar_mul", smax2[0:1, 1:2], smax2[0:1, 0:1], -1.0)
                wt("tensor", psc_tok)
                w("tensor", SVE, mv3)
                bb = mm(ps_c[:, 0:1], onesrf[:], smax2[0:1, 1:2],
                        start=True, stop=True, sem=SPV)
                w("scalar", SPV, bb)
                bs = act(bias_bh[:], ps_c[:, 0:1], AF.Copy)
                psc_tok = (SSE, bs)
                BIAS_OK = bs

                for qc in range(8):
                    ntt = 4 * (qc + 1)
                    qrhs = qts[:, h * S + qc * 512:h * S + (qc + 1) * 512]
                    ng = ntt // 2

                    def emit_S(g):
                        par = g % 2
                        half = lambda j, par=par: ps_big[:, par * 1024 + j * 512:
                                                         par * 1024 + (j + 1) * 512]
                        wt("tensor", pshalf_tok[par])
                        sv = None
                        for j in range(2):
                            tt = 2 * g + j
                            sv = mm(half(j), kts[:, tt * 128:(tt + 1) * 128], qrhs,
                                    start=True, stop=True,
                                    sem=SPV if j == 1 else None)
                        if (2 * g + 1) >= 4 * qc:
                            w("vector", SPV, sv)
                            msv = None
                            for j in range(2):
                                tt = 2 * g + j
                                if tt >= 4 * qc:
                                    msv = vop("tensor_add", half(j), half(j),
                                              maskt[:, (tt - 4 * qc) * 512:
                                                    (tt - 4 * qc + 1) * 512], sem=SMS)
                            w("scalar", SMS, msv)
                        else:
                            w("scalar", SPV, sv)
                        w("scalar", SSE, BIAS_OK)
                        wt("scalar", ptst_tok[par])
                        ex = act(ptst[:, par * 1024:(par + 1) * 1024],
                                 ps_big[:, par * 1024:(par + 1) * 1024],
                                 AF.Exp, bias=bias_bh[:], sem=SEXP)
                        pshalf_tok[par] = (SEXP, ex)

                    def emit_PV(g):
                        par = g % 2
                        w("tensor", SEXP, pshalf_tok[par][1])
                        if g == 0:
                            wt("tensor", psab_tok)
                            wt("tensor", psab_tok2[0])
                        last = None
                        for j in range(2):
                            tt = 2 * g + j
                            mm(ps_a[:], vts[:, tt * 128:(tt + 1) * 128],
                               ptst[:, par * 1024 + j * 512:par * 1024 + (j + 1) * 512],
                               start=(tt == 0), stop=(tt == ntt - 1))
                            last = mm(ps_b[0:1, :], onesc[:],
                                      ptst[:, par * 1024 + j * 512:
                                           par * 1024 + (j + 1) * 512],
                                      start=(tt == 0), stop=(tt == ntt - 1),
                                      sem=SPV if tt == ntt - 1 else None)
                        ptst_tok[par] = (SPV, SPV.n)
                        return last

                    emit_S(0)
                    last_pv = None
                    for g in range(1, ng):
                        emit_S(g)
                        last_pv = emit_PV(g - 1)
                    last_pv = emit_PV(ng - 1)

                    dsl = (bh * 8 + qc) % 2
                    wt("vector", aoraw_tok[dsl])
                    w("vector", SPV, last_pv)
                    eva = vop("tensor_copy", aoraw[:, dsl * 512:(dsl + 1) * 512],
                              ps_a[:])
                    w("scalar", SPV, last_pv)
                    wt("scalar", sums_tok[dsl])
                    evs = act(sums_sb[0:1, dsl * 512:(dsl + 1) * 512], ps_b[0:1, :],
                              AF.Copy, sem=SSE)
                    psab_tok = (SSE, evs)
                    psab_tok2[0] = (SVE, eva)
                    w("scalar", SSE, evs)
                    l1 = act(rcln[0:1, dsl * 512:(dsl + 1) * 512],
                             sums_sb[0:1, dsl * 512:(dsl + 1) * 512],
                             AF.Ln, sem=SSE)
                    w("scalar", SSE, l1)
                    l2 = act(rcb[0:1, dsl * 512:(dsl + 1) * 512],
                             rcln[0:1, dsl * 512:(dsl + 1) * 512],
                             AF.Exp, scale=-1.0, sem=SSE)
                    sums_tok[dsl] = (SSE, l1)

                    def emit_divide(dsl=dsl, bh=bh, qc=qc, b=b, h=h,
                                    eva=eva, l2=l2):
                        nonlocal psc_tok
                        sl2 = dsl
                        wt("tensor", psc_tok)
                        w("tensor", SSE, l2)
                        bc = mm(ps_c[:], onesr[:], rcb[0:1, sl2 * 512:(sl2 + 1) * 512],
                                start=True, stop=True, sem=SPV)
                        w("scalar", SPV, bc)
                        cb = act(recipB[:, sl2 * 512:(sl2 + 1) * 512], ps_c[:],
                                 AF.Copy, sem=SSE)
                        psc_tok = (SSE, cb)
                        asl = (bh * 8 + qc) % 2
                        wt("vector", aost_tok[asl])
                        w("vector", SSE, cb)
                        w("vector", SVE, eva)
                        dv = vop("tensor_mul", aostage[:, asl * 512:(asl + 1) * 512],
                                 aoraw[:, sl2 * 512:(sl2 + 1) * 512],
                                 recipB[:, sl2 * 512:(sl2 + 1) * 512])
                        aoraw_tok[sl2] = (SVE, dv)
                        w("sync", SVE, dv)
                        agin = at_agin_a if b == 0 else at_agin_b
                        av = dma("sync",
                                 agin[h * 128:(h + 1) * 128,
                                      qc * 512:(qc + 1) * 512],
                                 aostage[:, asl * 512:(asl + 1) * 512], SAGs[asl])
                        aost_tok[asl] = (SAGs[asl], av)
                    if pending_div[0] is not None:
                        pending_div[0]()
                    pending_div[0] = emit_divide
            if pending_div[0] is not None:
                pending_div[0]()
                pending_div[0] = None
            if b == 1:
                for _s in SAGs[0:2]:
                    w("gpsimd", _s, _s.n)
                on("gpsimd", lambda e: e.collective_compute(
                    "AllGather", ALU.bypass, replica_groups=[list(range(NCORE))],
                    ins=[at_agin_b],
                    outs=[atg2[NCORE * QC:2 * NCORE * QC, :]]).then_inc(SCCs[7].h, 1))
                CC_ATB = (SCCs[7], SCCs[7].inc())
                CC_DOWNA, CC_DOWNB = emit_down_ag()
            qts_tok[0] = (SVE, SVE.n)

        if pending_div[0] is not None:
            pending_div[0]()
            pending_div[0] = None
        if self.stop <= 2:
            return finish()

        self.free_through("qkvw")
        w("vector", SPRES, PREPASS_S)
        w("vector", SPRES1, PREPASS_S1)
        xrow = self.sbuf("xrow_b", [128, 2 * D], F32)
        xfree = [None, None]
        tstage = self.sbuf("tstage_b", [128, 1024], BF16)
        x1row = self.sbuf("x1row_b", [128, D], BF16)
        zrow = self.sbuf("zrow_b", [128, D], BF16)
        gaT = self.sbuf("gaT", [128, 32 * RK], BF16)
        uaT = self.sbuf("uaT", [128, 32 * RK], BF16)
        daT = self.sbuf("daT", [128, NHT * RK], BF16)
        lorapk = self.sbuf("lorapk", [128, DH], BF16)
        xrpack = self.sbuf("xrpack", [128, 512], BF16)
        x2tmp = self.sbuf("x2tmp", [128, D], F32)
        x3t = self.sbuf("x3t", [128, 32 * 512], BF16)

        lora_b2 = self.sbuf("lora_b2", [128, NHT * RK], BF16)
        w("sync", SLORA, LORA_STASH)
        for _t, _d in ((gaT, gaT_d), (uaT, uaT_d), (daT, daT_d)):
            dma("sync", _t[:], _d, SLORA)
        LORA_RELOAD = SLORA.n

        psT_tok3 = psT_tok
        lfb_tok = None

        def lora_b(inp, n_ht, dst_row):
            nonlocal psT_tok3, lfb_tok
            wt("sync", lfb_tok)
            lv = dma("sync",
                     lora_b2.ap().rearrange("p (ht j) -> p ht j", j=RK)[:, 0:n_ht, :],
                     inp.rearrange("(ht p) j -> p ht j", p=128), SLORA)
            w("tensor", SLORA, lv)
            for gg in range((n_ht + 7) // 8):
                kk = min(8, n_ht - gg * 8)
                wt("tensor", psT_tok3)
                for k in range(kk):
                    ht = gg * 8 + k
                    tk = trans(psT[0:16, k * 128:(k + 1) * 128],
                               lora_b2[:, ht * RK:(ht + 1) * RK],
                               sem=STE if k == kk - 1 else None)
                w("vector", STE, tk)
                ev = vop("tensor_copy",
                         lorapk[dst_row:dst_row + 16,
                                gg * 1024:gg * 1024 + kk * 128],
                         psT[0:16, 0:kk * 128])
                psT_tok3 = (SVE, ev)
                lfb_tok = (STE, tk)

        lora_b(gb_bfd, NHT, 0)
        lora_b(ub_bfd, NHT, 32)
        lora_b(db_bfd, 32, 64)
        psT_tok = psT_tok3

        out1 = self.sbuf("out1", [128, 4 * D], BF16)
        ats_q = self.sbuf("ats_q", [128, 32 * 512], BF16)
        woring = self.sbuf("woring", [128, 8 * 512], BF16)

        wo_bank_tok = [None] * 4
        out1_free = None
        x2tmp_tok = []
        zrow_tok = None
        x3t_free = None

        x3t_v = x3t.ap().rearrange("p (dt rtl f) -> p dt rtl f", dt=32, rtl=4)

        def emit_x3_transposes(rtl, nx3):
            nonlocal psT_tok
            for g in range(4):
                w("tensor", SVE, nx3)
                wt("tensor", psT_tok)
                for k in range(8):
                    dtt = g * 8 + k
                    tk = trans(psT[:, k * 128:(k + 1) * 128],
                               x1row[:, dtt * 128:(dtt + 1) * 128],
                               sem=STE if k == 7 else None)
                w("vector", STE, tk)
                wt("vector", x3t_free)
                ev = vop("tensor_copy",
                         x3t_v[:, g * 8:(g + 1) * 8, rtl, :],
                         psT.ap().rearrange("p (k f) -> p k f", k=8))
                psT_tok = (SVE, ev)
            return psT_tok
        wotile_i = 0
        ats_tok = None
        for rtq in range(2):
            wt("sync", CC_ATA)
            wt("sync", CC_ATB)
            wt("sync", ats_tok)
            atv = None
            at_g3 = atg2.rearrange("(x p) r -> p x r", p=128)
            ats_q3 = ats_q.ap().rearrange("p (dt r) -> p dt r", dt=32)
            xoff = (pid // 4) * 32
            coff = (pid % 4) * 1024 + rtq * 512
            for dg in range(4):
                atv = dma("sync", ats_q3[:, dg * 8:(dg + 1) * 8, :],
                          at_g3[:, bass.ds(xoff + dg * 8, 8),
                                bass.ds(coff, 512)], SAT)
            for cc in range(8):
                for dtt in range(32):
                    sl = wotile_i % 8
                    qeng = "sync" if sl % 2 == 0 else "gpsimd"
                    if wotile_i >= 8:
                        w(qeng, SWOT, wotile_i - 7)
                    if wotile_i < 2:
                        wt(qeng, CC_WO)
                    lv = dma(qeng, woring[:, sl * 512:(sl + 1) * 512],
                             wo_g4[dtt // 4, dtt % 4, cc], SWOs[sl])
                    w("tensor", SWOs[sl], lv)
                    if dtt == 0:
                        w("tensor", SAT, atv)
                        for rtl in range(4):
                            wt("tensor", wo_bank_tok[rtl])
                    for rtl in range(4):
                        mm(ps_big[:, rtl * 512:(rtl + 1) * 512],
                           ats_q[:, dtt * 512 + rtl * 128:dtt * 512 + (rtl + 1) * 128],
                           woring[:, sl * 512:(sl + 1) * 512],
                           start=(dtt == 0), stop=(dtt == 31),
                           sem=SWOT if rtl == 3 else None)
                    wotile_i += 1
                for rtl in range(4):
                    w("scalar", SWOT, SWOT.n)
                    wt("scalar", out1_free)
                    evv = act(out1[:, rtl * D + cc * 512:rtl * D + (cc + 1) * 512],
                              ps_big[:, rtl * 512:(rtl + 1) * 512], AF.Copy, sem=SEV)
                    wo_bank_tok[rtl] = (SEV, evv)
            WO_EVAC = SEV.n
            ats_tok = (SWOT, SWOT.n)
            for rtl in range(4):
                rt = rtq * 4 + rtl
                sl = rt % 2
                wt("sync", xfree[sl])
                lx = dma("sync", xrow[:, sl * D:(sl + 1) * D],
                         x_in[rt * 128:(rt + 1) * 128, :], SXs[sl])
                w("vector", SEV, WO_EVAC)
                w("scalar", SEV, WO_EVAC)
                o1 = out1[:, rtl * D:(rtl + 1) * D]
                tok_ln, nm2, r2 = ln_stats(o1, D, 0 if sl == 0 else 8,
                                           prewait=zrow_tok)
                wt("vector", tok_ln)
                z2 = vop("tensor_scalar", zrow[:], o1, nm2, r2, ALU.add, ALU.mult)
                w("vector", SVE, z2)
                w("vector", SXs[sl], lx)
                for _tk in x2tmp_tok:
                    wt("vector", _tk)
                xv = vop("tensor_add", x2tmp[:], zrow[:], xrow[:, sl * D:(sl + 1) * D])
                zrow_tok = (SVE, xv)
                xfree[sl] = (SVE, xv)
                w("sync", SVE, xv)
                x2dma = dma("sync", x2_dram[rt * 128:(rt + 1) * 128, :], x2tmp[:], SX2)
                tok3, nm3, r3 = ln_stats(x2tmp[:], D, 16 if sl == 0 else 24,
                                         prewait=zrow_tok)
                w("vector", SSE, tok3[1])
                c1 = vop("tensor_copy", stats_st[:, 2 * rt:2 * rt + 1], r3)
                c2 = vop("tensor_copy", stats_st[:, 2 * rt + 1:2 * rt + 2], nm3)
                if rtq == 0:
                    nx3 = vop("tensor_scalar", x1row[:], x2tmp[:], nm3, r3,
                              ALU.add, ALU.mult)
                    x2tmp_tok = [(SVE, nx3), (SX2, x2dma)]
                    psT_tok = emit_x3_transposes(rtl, nx3)
                else:
                    x2tmp_tok = [(SVE, c2), (SX2, x2dma)]
            out1_free = (SSE, SSE.n)
        X2_ALL = SX2.n

        self.free_through("lora_b2")
        NGSL = 8
        NDSL = 6
        hidt = self.sbuf("hidt", [128, NHP * 512], BF16)
        guringA = self.sbuf("guringA", [128, NGSL * 1024], BF16)
        guringB = self.sbuf("guringB", [128, NGSL * 1024], BF16)
        dwring = self.sbuf("dwring", [128, NDSL * 512], BF16)

        x2r_tok = [None, None]
        zrow_tok2 = [zrow_tok]
        banks_tok = [None] * 4
        gu_slot_tok = [None] * NGSL
        dw_slot_tok = [None] * NDSL
        ts_tok = [None, None]
        x1row_tok = [None]
        o4r_prev = [0]
        gu_i = [0]
        dw_i = [0]
        dw_seen = set()
        gu_seen = set()

        w("tensor", SLORA, LORA_RELOAD)

        def ffn_rc(rc, post_gu1=None):
            nonlocal psc_tok, x3t_free, x2tmp_tok
            XV = SVE.n
            for aT, dst_row in ((gaT, 0), (uaT, 32)):
                wt("tensor", psc_tok)
                w("tensor", SVE, XV)
                for dtt in range(32):
                    xv = mm(ps_c[0:16, :], aT[:, dtt * RK:(dtt + 1) * RK],
                            x3t[:, dtt * 512:(dtt + 1) * 512],
                            start=(dtt == 0), stop=(dtt == 31),
                            sem=SPV if dtt == 31 else None)
                w("scalar", SPV, xv)
                evv = act(xrpack[dst_row:dst_row + 16, :], ps_c[0:16, :], AF.Copy,
                          sem=SSE)
                psc_tok = (SSE, evv)
            XR_OK = psc_tok[1]

            hid_tok = [None] * NHT
            slot_of_rank = [0] * NCORE
            hidr_done = None
            pending_hidr = [None]

            def gu_half(half):
                nonlocal hidr_done
                for hl_ in range(NHP):
                    ht = half * NHP + hl_
                    hl = ht % NHP
                    hp, hoff = ht // 2, (ht % 2) * 128
                    par = ht % 2
                    if ht % 2 == 0:
                        used = []
                        for r in range(NCORE):
                            sl = gu_i[0] % NGSL
                            wt("sync", gu_slot_tok[sl])
                            wt("gpsimd", gu_slot_tok[sl])
                            guk = "A" if hp < NHPA else "B"
                            if ("sync", guk) not in gu_seen:
                                gu_seen.add(("sync", guk))
                                wt("sync", CC_GATEA if guk == "A" else CC_GATEB)
                            if ("gpsimd", guk) not in gu_seen:
                                gu_seen.add(("gpsimd", guk))
                                wt("gpsimd", CC_UPA if guk == "A" else CC_UPB)
                            gsrc = (gate_g4A[r, hp] if hp < NHPA
                                    else gate_g4B[r, hp - NHPA])
                            usrc = (up_g4A[r, hp] if hp < NHPA
                                    else up_g4B[r, hp - NHPA])
                            gA = guringA.ap().rearrange(
                                "p (s dt f) -> p s dt f", s=NGSL, dt=4)
                            gB = guringB.ap().rearrange(
                                "p (s dt f) -> p s dt f", s=NGSL, dt=4)
                            dma("sync", gA[:, sl, :, :],
                                gsrc.rearrange("dt p f -> p dt f"),
                                SFWs[sl])
                            lvs = dma("gpsimd", gB[:, sl, :, :],
                                      usrc.rearrange("dt p f -> p dt f"),
                                      SFWs[sl])
                            used.append((sl, lvs))
                            slot_of_rank[r] = sl
                            gu_i[0] += 1
                        for sl_, lvs_ in used:
                            w("tensor", SFWs[sl_], lvs_)
                    bg = ps_big[:, (par * 2) * 512:(par * 2 + 1) * 512]
                    bu = ps_big[:, (par * 2 + 1) * 512:(par * 2 + 2) * 512]
                    wt("tensor", banks_tok[par * 2])
                    wt("tensor", banks_tok[par * 2 + 1])
                    if ht == 0:
                        w("tensor", SSE, XR_OK)
                    for dtt in range(32):
                        r, dtl = dtt // 4, dtt % 4
                        off = slot_of_rank[r] * 1024 + dtl * 256 + hoff
                        mm(bg, guringA[:, off:off + 128],
                           x3t[:, dtt * 512:(dtt + 1) * 512],
                           start=(dtt == 0), stop=False)
                        s_l = mm(bu, guringB[:, off:off + 128],
                                 x3t[:, dtt * 512:(dtt + 1) * 512],
                                 start=(dtt == 0), stop=False,
                                 sem=SFT if (dtl == 3 and ht % 2 == 1) else None)
                        if dtl == 3 and ht % 2 == 1:
                            gu_slot_tok[slot_of_rank[r]] = (SFT, s_l)
                    mm(bg, lorapk[0:16, ht * 128:(ht + 1) * 128], xrpack[0:16, :],
                       start=False, stop=True)
                    gl = mm(bu, lorapk[32:48, ht * 128:(ht + 1) * 128],
                            xrpack[32:48, :], start=False, stop=True, sem=SFT)
                    w("scalar", SFT, gl)
                    ts_ = tstage[:, par * 512:(par + 1) * 512]
                    wt("scalar", ts_tok[par])
                    ts_tok[par] = None
                    sv = act(ts_, bg, AF.Silu, sem=SSE)
                    banks_tok[par * 2] = (SSE, sv)
                    w("vector", SSE, sv)
                    hv = vop("tensor_mul", hidt[:, hl * 512:(hl + 1) * 512], bu, ts_)
                    banks_tok[par * 2 + 1] = (SVE, hv)
                    hid_tok[ht] = (SVE, hv)
                    w("tensor", SVE, hv)
                    hidr_done = mm(ps_b[0:16, :], daT[:, ht * RK:(ht + 1) * RK],
                                   hidt[:, hl * 512:(hl + 1) * 512],
                                   start=(ht == 0), stop=(ht == NHT - 1),
                                   sem=SPV if ht == NHT - 1 else None)

            def down_half(half, first):
                for dc in range(8):
                    for rtl in range(4):
                        wt("tensor", banks_tok[rtl])
                    if not first:
                        w("tensor", SSE, HIDR_EV)
                        for rtl in range(4):
                            mm(ps_big[:, rtl * 512:(rtl + 1) * 512],
                               xrpack[64:80, rtl * 128:(rtl + 1) * 128],
                               lorapk[64:80, dc * 512:(dc + 1) * 512],
                               start=True, stop=False)
                    for hl in range(NHP):
                        ht = half * NHP + hl
                        sl = dw_i[0] % NDSL
                        qeng = "sync" if sl % 2 == 0 else "gpsimd"
                        wt(qeng, dw_slot_tok[sl])
                        src = (down_g4A[ht // 11, ht % 11, dc] if dc < 4
                               else down_g4B[ht // 11, ht % 11, dc - 4])
                        key = (qeng, "A" if dc < 4 else "B")
                        if key not in dw_seen:
                            dw_seen.add(key)
                            wt(qeng, CC_DOWNA if dc < 4 else CC_DOWNB)
                        lv = dma(qeng, dwring[:, sl * 512:(sl + 1) * 512],
                                 src, SFDs[sl])
                        dw_i[0] += 1
                        w("tensor", SFDs[sl], lv)
                        wt("tensor", hid_tok[ht])
                        dv = None
                        for rtl in range(4):
                            dv = mm(ps_big[:, rtl * 512:(rtl + 1) * 512],
                                    hidt[:, hl * 512 + rtl * 128:
                                         hl * 512 + (rtl + 1) * 128],
                                    dwring[:, sl * 512:(sl + 1) * 512],
                                    start=(first and hl == 0), stop=(hl == NHP - 1),
                                    sem=SFT if rtl == 3 else None)
                        dw_slot_tok[sl] = (SFT, dv)
                    for rtl in range(4):
                        chunk = out4_dram[rtl * 128:(rtl + 1) * 128,
                                          dc * 512:(dc + 1) * 512]
                        if first:
                            if dc == 0 and rtl == 0:
                                w("sync", SO4R, o4r_prev[0])
                            w("scalar", SFT, SFT.n)
                            wt("scalar", ts_tok[0])
                            evv = act(tstage[:, 0:512],
                                      ps_big[:, rtl * 512:(rtl + 1) * 512],
                                      AF.Copy, sem=SEV)
                            banks_tok[rtl] = (SEV, evv)
                            w("sync", SEV, evv)
                            wv = dma("sync", chunk, tstage[:, 0:512], SO4W)
                            ts_tok[0] = (SO4W, wv)
                        else:
                            wt("sync", ts_tok[0])
                            if dc == 0 and rtl == 0:
                                w("sync", SO4W, SO4W.n)
                            lv2 = dma("sync", tstage[:, 0:512], chunk, SO4L)
                            w("vector", SO4L, lv2)
                            w("vector", SFT, SFT.n)
                            wt("vector", ts_tok[1])
                            evv = vop("tensor_add", tstage[:, 512:1024],
                                      ps_big[:, rtl * 512:(rtl + 1) * 512],
                                      tstage[:, 0:512])
                            banks_tok[rtl] = (SVE, evv)
                            ts_tok[0] = (SVE, evv)
                            w("sync", SVE, evv)
                            wv = dma("sync", chunk, tstage[:, 512:1024], SO4W2)
                            ts_tok[1] = (SO4W2, wv)

            gu_half(0)
            down_half(0, True)
            gu_half(1)
            x3t_free = (SFT, SFT.n)
            w("scalar", SPV, hidr_done)
            HIDR_EV = act(xrpack[64:80, :], ps_b[0:16, :], AF.Copy, sem=SSE)
            down_half(1, False)

            for rtl in range(4):
                rt = rc * 4 + rtl
                sl = rtl % 2
                if rtl == 0:
                    w("sync", SO4W2, SO4W2.n)
                wt("sync", x1row_tok[0])
                l4 = dma("sync", x1row[:], out4_dram[rtl * 128:(rtl + 1) * 128, :],
                         SO4R)
                wt("sync", x2r_tok[sl])
                w("sync", SX2, X2_ALL)
                lx = dma("sync", xrow[:, sl * D:(sl + 1) * D],
                         x2_dram[rt * 128:(rt + 1) * 128, :], SX2Rs[sl])
                w("vector", SO4R, l4)
                w("scalar", SO4R, l4)
                tok_ln, nm4, r4 = ln_stats(x1row[:], D, 0 if sl == 0 else 8,
                                           prewait=zrow_tok2[0])
                wt("vector", tok_ln)
                z4 = vop("tensor_scalar", zrow[:], x1row[:], nm4, r4,
                         ALU.add, ALU.mult)
                x1row_tok[0] = (SVE, z4)
                w("vector", SVE, z4)
                w("vector", SX2Rs[sl], lx)
                for _tk in x2tmp_tok:
                    wt("vector", _tk)
                ov = vop("tensor_add", x2tmp[:], zrow[:], xrow[:, sl * D:(sl + 1) * D])
                zrow_tok2[0] = (SVE, ov)
                x2r_tok[sl] = (SVE, ov)
                w("sync", SVE, ov)
                odma = dma("sync", out_ap[rt * 128:(rt + 1) * 128, :], x2tmp[:], SOUT)
                x2tmp_tok = [(SOUT, odma)]
            o4r_prev[0] = SO4R.n

        ffn_rc(0)

        for rtl in range(4):
            rt = 4 + rtl
            sl = rtl % 2
            wt("sync", x2r_tok[sl])
            w("sync", SX2, X2_ALL)
            lx = dma("sync", xrow[:, sl * D:(sl + 1) * D],
                     x2_dram[rt * 128:(rt + 1) * 128, :], SX2Rs[sl])
            w("vector", SX2Rs[sl], lx)
            wt("vector", x1row_tok[0])
            nx3 = vop("tensor_scalar", x1row[:], xrow[:, sl * D:(sl + 1) * D],
                      stats_st[:, 2 * rt + 1:2 * rt + 2],
                      stats_st[:, 2 * rt:2 * rt + 1], ALU.add, ALU.mult)
            x1row_tok[0] = (SVE, nx3)
            x2r_tok[sl] = (SVE, nx3)
            psT_tok = emit_x3_transposes(rtl, nx3)
        ffn_rc(1)

        return finish()


def _shard_inputs(inputs, consts):
    x = np.asarray(inputs["L0_attn_x"]).reshape(R, D)
    wqkv = np.asarray(inputs["L0_attn_w_qkv"])
    wo = np.asarray(inputs["L0_attn_w_o"])
    gate = np.asarray(inputs["L0_ffn_gate_w"])
    up = np.asarray(inputs["L0_ffn_up_w"])
    down = np.asarray(inputs["L0_ffn_down_w"])
    in_maps = []
    for c in range(NCORE):
        qcols = wqkv[:, c * 512:(c + 1) * 512]
        kcols = wqkv[:, 4096 + c * 128:4096 + (c + 1) * 128]
        vcols = wqkv[:, 5120 + c * 128:5120 + (c + 1) * 128]
        dn = down[:, c * 1408:(c + 1) * 1408]
        if dn.shape[1] < 1408:
            dn = np.concatenate(
                [dn, np.zeros((D, 1408 - dn.shape[1]), np.float32)], axis=1)
        m = {
            "x_rows": np.ascontiguousarray(x[c * OWN:(c + 1) * OWN]),
            "wqkv_sl": np.ascontiguousarray(np.concatenate([qcols, kcols, vcols], axis=1)),
            "wo_sl": np.ascontiguousarray(wo[c * 512:(c + 1) * 512]),
            "gate_sl": np.ascontiguousarray(gate[:, c * 512:(c + 1) * 512]),
            "up_sl": np.ascontiguousarray(up[:, c * 512:(c + 1) * 512]),
            "down_sl": np.ascontiguousarray(dn),
            "gate_a": np.asarray(inputs["L0_ffn_gate_a"]),
            "gate_b": np.asarray(inputs["L0_ffn_gate_b"]),
            "up_a": np.asarray(inputs["L0_ffn_up_a"]),
            "up_b": np.asarray(inputs["L0_ffn_up_b"]),
            "down_a": np.asarray(inputs["L0_ffn_down_a"]),
            "down_b": np.asarray(inputs["L0_ffn_down_b"]),
        }
        m.update(consts)
        in_maps.append(m)
    return in_maps


def kernel(**inputs):
    consts = _host_consts()
    in_maps = _shard_inputs(inputs, consts)
    b = Builder()
    nc = b.build()
    res = run_bass_kernel_spmd(nc, in_maps, core_ids=list(range(NCORE)))
    out = np.concatenate([r["out_rows"] for r in res.results], axis=0)
    return out.reshape(B, S, D)


# version 24
# speedup vs baseline: 1.0579x; 1.0200x over previous
import math
import numpy as np
import ml_dtypes
import concourse.bass as bass
import concourse.mybir as mybir
from concourse.bass_utils import run_bass_kernel_spmd

F32 = mybir.dt.float32
BF16 = mybir.dt.bfloat16
AF = mybir.ActivationFunctionType
ALU = mybir.AluOpType
AX = mybir.AxisListType

NCORE = 8
B = 2
S = 4096
R = B * S
D = 4096
HD = 128
QC = 512
WCOLS = 768
DH = 11008
NHT = DH // 128
NHP = NHT // 2
RK = 16
OWN = R // NCORE
EPS = 1e-5
INV_SQRT_HD = 1.0 / math.sqrt(HD)
NEG_BIG = -30000.0

ENGS = ("sync", "tensor", "vector", "scalar", "gpsimd")


def _host_consts():
    c = {}
    c["ident"] = np.eye(128, dtype=ml_dtypes.bfloat16)
    p = np.zeros((128, 128), dtype=np.float32)
    for i in range(128):
        p[i ^ 1, i] = 1.0
    c["perm"] = p.astype(ml_dtypes.bfloat16)
    inv_freq = 1.0 / (10000.0 ** (np.arange(0, HD, 2, dtype=np.float64) / HD))
    ang = np.arange(S, dtype=np.float64)[None, :] * inv_freq[:, None]
    cos, sin = np.cos(ang), np.sin(ang)
    Ct = np.empty((128, S), np.float32)
    St = np.empty((128, S), np.float32)
    Ct[0::2] = cos
    Ct[1::2] = cos
    St[0::2] = -sin
    St[1::2] = sin
    c["rope_c"] = Ct.astype(ml_dtypes.bfloat16)
    c["rope_s"] = St.astype(ml_dtypes.bfloat16)
    masks = np.zeros((128, 4 * 512), np.float32)
    for k in range(4):
        tloc = 128 * k + np.arange(128)[:, None]
        masks[:, k * 512:(k + 1) * 512] = np.where(
            tloc > np.arange(512)[None, :], NEG_BIG, 0.0)
    c["masks"] = masks.astype(ml_dtypes.bfloat16)
    c["ones_col"] = np.ones((128, 1), dtype=ml_dtypes.bfloat16)
    c["ones_row"] = np.ones((1, 128), dtype=ml_dtypes.bfloat16)
    c["ones_row_f"] = np.ones((1, 128), dtype=np.float32)
    c["fconst"] = np.tile(np.array([[0.0, 1.0, EPS, -0.5, -1.0, INV_SQRT_HD, 2.0, 0.5]],
                           dtype=np.float32), (128, 1))
    return c


class Sem:
    def __init__(self, handle):
        self.h = handle
        self.n = 0

    def inc(self, k=1):
        self.n += k
        return self.n


class Builder:
    def __init__(self, taps=(), stop=99):
        self.nc = bass.Bass("TRN2", target_bir_lowering=False)
        self.taps = set(taps)
        self.stop = stop
        self.plan = {e: [] for e in ENGS}
        self.stack = []
        self.t = {}

    def on(self, eng, fn):
        self.plan[eng].append(fn)

    def w(self, eng, sem, val):
        assert val <= sem.n, f"wait on future value {val} > {sem.n}"
        if val > 0:
            self.on(eng, lambda e, s=sem.h, v=val: e.wait_ge(s, v))

    def wt(self, eng, tok):
        if tok is not None:
            self.w(eng, tok[0], tok[1])

    def dma(self, eng, out, in_, sem):
        def f(e, out=out, in_=in_, s=sem.h):
            e.dma_start(out=out, in_=in_).then_inc(s, 16)
        self.on(eng, f)
        return sem.inc(16)

    def sem(self, name):
        g = self.nc.semaphore(name)
        self.stack.append((name, g))
        return Sem(g.__enter__())

    def sbuf(self, name, shape, dtyp):
        g = self.nc.sbuf_tensor(name, shape, dtyp)
        self.stack.append((name, g))
        t = g.__enter__()
        self.t[name] = t
        return t

    def psum(self, name, shape, dtyp=F32):
        g = self.nc.psum_tensor(name, shape, dtyp)
        self.stack.append((name, g))
        t = g.__enter__()
        self.t[name] = t
        return t

    def free_through(self, name):
        while self.stack:
            n, g = self.stack.pop()
            g.__exit__(None, None, None)
            if n == name:
                return
        raise RuntimeError(f"{name} not on stack")

    def emit(self):
        nc = self.nc
        with nc.Block() as block:
            @block.sync
            def _(e):
                for fn in self.plan["sync"]:
                    fn(e)

            @block.tensor
            def _(e):
                for fn in self.plan["tensor"]:
                    fn(e)

            @block.vector
            def _(e):
                for fn in self.plan["vector"]:
                    fn(e)

            @block.scalar
            def _(e):
                for fn in self.plan["scalar"]:
                    fn(e)

            @block.gpsimd
            def _(e):
                for fn in self.plan["gpsimd"]:
                    fn(e)
        while self.stack:
            _, g = self.stack.pop()
            g.__exit__(None, None, None)
        return nc

    def build(self):
        nc = self.nc
        taps = self.taps
        on, w, wt, dma = self.on, self.w, self.wt, self.dma

        def ein(name, shape, dtyp=F32):
            return nc.dram_tensor(name, shape, dtyp, kind="ExternalInput").ap()

        x_in = ein("x_rows", [OWN, D])
        wqkv_in = ein("wqkv_sl", [D, WCOLS])
        wo_in = ein("wo_sl", [512, D])
        gate_in = ein("gate_sl", [DH, 512])
        up_in = ein("up_sl", [DH, 512])
        down_in = ein("down_sl", [D, 1408])
        ga_in = ein("gate_a", [RK, D])
        gb_in = ein("gate_b", [DH, RK])
        ua_in = ein("up_a", [RK, D])
        ub_in = ein("up_b", [DH, RK])
        da_in = ein("down_a", [RK, DH])
        db_in = ein("down_b", [D, RK])
        ident_in = ein("ident", [128, 128], BF16)
        perm_in = ein("perm", [128, 128], BF16)
        ropec_in = ein("rope_c", [128, S], BF16)
        ropes_in = ein("rope_s", [128, S], BF16)
        masks_in = ein("masks", [128, 4 * 512], BF16)
        onesc_in = ein("ones_col", [128, 1], BF16)
        onesr_in = ein("ones_row", [1, 128], BF16)
        onesrf_in = ein("ones_row_f", [1, 128], F32)
        fconst_in = ein("fconst", [128, 8], F32)

        out_ap = nc.dram_tensor("out_rows", [OWN, D], F32, kind="ExternalOutput").ap()

        def idr(name, shape, dtyp):
            return nc.dram_tensor(name, shape, dtyp, kind="Internal").ap()

        gate_bf = idr("gate_bf", [DH, 512], BF16)
        up_bf = idr("up_bf", [DH, 512], BF16)
        down_bf = idr("down_bf", [D, 1408], BF16)
        x1t_agin_a = idr("x1t_agin_a", [D, 512], BF16)
        x1t_agin_b = idr("x1t_agin_b", [D, 512], BF16)
        x1t_gaA = idr("x1t_gaA", [NCORE * 2048, 512], BF16)
        x1t_gaB = idr("x1t_gaB", [NCORE * 2048, 512], BF16)
        x1t_gbA = idr("x1t_gbA", [NCORE * 2048, 512], BF16)
        x1t_gbB = idr("x1t_gbB", [NCORE * 2048, 512], BF16)
        wo_agin = idr("wo_agin", [4 * 8 * 128, 512], BF16)
        wo_g = idr("wo_g", [NCORE * 4 * 8 * 128, 512], BF16)
        NHPA, NHPB = 22, 21
        gate_agin = idr("gate_agin", [NHP * 4 * 128, 256], BF16)
        gate_gA = idr("gate_gA", [NCORE * NHPA * 4 * 128, 256], BF16)
        gate_gB = idr("gate_gB", [NCORE * NHPB * 4 * 128, 256], BF16)
        up_agin = idr("up_agin", [NHP * 4 * 128, 256], BF16)
        up_gA = idr("up_gA", [NCORE * NHPA * 4 * 128, 256], BF16)
        up_gB = idr("up_gB", [NCORE * NHPB * 4 * 128, 256], BF16)
        down_aginA = idr("down_aginA", [11 * 4 * 128, 512], BF16)
        down_aginB = idr("down_aginB", [11 * 4 * 128, 512], BF16)
        down_gA = idr("down_gA", [NCORE * 11 * 4 * 128, 512], BF16)
        down_gB = idr("down_gB", [NCORE * 11 * 4 * 128, 512], BF16)
        out4_dram = idr("out4_dram", [512, D], BF16)
        ga_bfd = idr("ga_bfd", [RK, D], BF16)
        ua_bfd = idr("ua_bfd", [RK, D], BF16)
        da_bfd = idr("da_bfd", [RK, DH], BF16)
        gb_bfd = idr("gb_bfd", [DH, RK], BF16)
        ub_bfd = idr("ub_bfd", [DH, RK], BF16)
        db_bfd = idr("db_bfd", [D, RK], BF16)
        at_agin_a = idr("at_agin_a", [QC, S], BF16)
        at_agin_b = idr("at_agin_b", [QC, S], BF16)
        atg2 = idr("atg2", [2 * NCORE * QC, S], BF16)
        x2_dram = idr("x2_dram", [OWN, D], F32)
        gaT_d = idr("gaT_d", [128, 32 * RK], BF16)
        uaT_d = idr("uaT_d", [128, 32 * RK], BF16)
        daT_d = idr("daT_d", [128, NHT * RK], BF16)
        stats_agin = idr("stats_agin", [OWN, 2], F32)
        stats_g = idr("stats_g", [NCORE * OWN, 2], F32)

        wo_g4 = wo_g.rearrange("(r dt cc p) f -> r dt cc p f", r=NCORE, dt=4, cc=8, p=128)
        gate_g4A = gate_gA.rearrange("(r hp dt p) f -> r hp dt p f", r=NCORE, hp=NHPA, dt=4, p=128)
        gate_g4B = gate_gB.rearrange("(r hp dt p) f -> r hp dt p f", r=NCORE, hp=NHPB, dt=4, p=128)
        up_g4A = up_gA.rearrange("(r hp dt p) f -> r hp dt p f", r=NCORE, hp=NHPA, dt=4, p=128)
        up_g4B = up_gB.rearrange("(r hp dt p) f -> r hp dt p f", r=NCORE, hp=NHPB, dt=4, p=128)
        down_g4A = down_gA.rearrange("(r ht dc p) f -> r ht dc p f", r=NCORE, ht=11, dc=4, p=128)
        down_g4B = down_gB.rearrange("(r ht dc p) f -> r ht dc p f", r=NCORE, ht=11, dc=4, p=128)
        x1t_g3aA = x1t_gaA.rearrange("(r d) f -> r d f", r=NCORE)
        x1t_g3aB = x1t_gaB.rearrange("(r d) f -> r d f", r=NCORE)
        x1t_g3bA = x1t_gbA.rearrange("(r d) f -> r d f", r=NCORE)
        x1t_g3bB = x1t_gbB.rearrange("(r d) f -> r d f", r=NCORE)
        x1t_ag2a = x1t_agin_a.rearrange("(dt p) f -> dt p f", p=128)
        x1t_ag2b = x1t_agin_b.rearrange("(dt p) f -> dt p f", p=128)
        wo_ag2 = wo_agin.rearrange("(dt cc p) f -> dt cc p f", dt=4, cc=8, p=128)
        gate_ag2 = gate_agin.rearrange("(hp dt p) f -> hp dt p f", hp=NHP, dt=4, p=128)
        up_ag2 = up_agin.rearrange("(hp dt p) f -> hp dt p f", hp=NHP, dt=4, p=128)
        down_ag2A = down_aginA.rearrange("(ht dc p) f -> ht dc p f", ht=11, dc=4, p=128)
        down_ag2B = down_aginB.rearrange("(ht dc p) f -> ht dc p f", ht=11, dc=4, p=128)

        SCONST = self.sem("sconst")
        SCONST2 = self.sem("sconst2")
        SPREG = self.sem("spreg")
        SPREGQ = self.sem("spregq")
        SPREGW = self.sem("spregw")
        SPREGD = self.sem("spregd")
        SPREGT = self.sem("spregt")
        SPRES = self.sem("spres")
        SPRES1 = self.sem("spres1")
        SXs = [self.sem(f"sx{i}") for i in range(2)]
        SX1T = self.sem("sx1t")
        SX1TB = self.sem("sx1tb")
        SCCs = [self.sem(f"scc{i}") for i in range(8)]
        SQW = self.sem("sqw")
        SQs = [self.sem(f"sq{i}") for i in range(8)]
        SLORA = self.sem("slora")
        STT = self.sem("stt")
        STE = self.sem("ste")
        SVE = self.sem("sve")
        SSE = self.sem("sse")
        SEV = self.sem("sev")
        SPV = self.sem("spv")
        SEXP = self.sem("sexp")
        SMS = self.sem("sms")
        SAGs = [self.sem(f"sag{i}") for i in range(4)]
        SWOs = [self.sem(f"swo{i}") for i in range(8)]
        SWOT = self.sem("swot")
        SAT = self.sem("sat")
        SFWs = [self.sem(f"sfw{i}") for i in range(8)]
        SFDs = [self.sem(f"sfd{i}") for i in range(8)]
        SFT = self.sem("sft")
        SO4W = self.sem("so4w")
        SO4W2 = self.sem("so4w2")
        SO4L = self.sem("so4l")
        SO4R = self.sem("so4r")
        SX2 = self.sem("sx2")
        SX2Rs = [self.sem(f"sx2r{i}") for i in range(2)]
        SOUT = self.sem("sout")
        SSTAs = [self.sem(f"ssta{i}") for i in range(4)]
        SSRs = [self.sem(f"ssr{i}") for i in range(2)]

        pid = nc.partition_id()

        ident = self.sbuf("c_ident", [128, 128], BF16)
        onesc = self.sbuf("c_onesc", [128, 1], BF16)
        onesr = self.sbuf("c_onesr", [1, 128], BF16)
        onesrf = self.sbuf("c_onesrf", [1, 128], F32)
        st_small = self.sbuf("st_small", [128, 32], F32)
        stats_st = self.sbuf("stats_st", [128, 16], F32)

        cpool = self.sbuf("cpool", [128, 8], F32)
        for _i, _v in enumerate([0.0, 1.0, EPS, -0.5, -1.0, INV_SQRT_HD, 2.0, 0.5]):
            nc.const_aps.register(F32, _v, cpool[:, _i:_i + 1])

        ps_big = self.psum("ps_big", [128, 2048])
        ps_a = self.psum("ps_a", [128, 512])
        ps_b = self.psum("ps_b", [128, 512])
        ps_c = self.psum("ps_c", [128, 512])
        psT = self.psum("psT", [128, 1024], BF16)

        def act(out, in_, func, bias=0.0, scale=1.0, accum=None, sem=None):
            sem = sem or SSE
            def f(e, out=out, in_=in_, func=func, bias=bias, scale=scale,
                  accum=accum, s=sem.h):
                e.activation(out, in_, func, bias=bias, scale=scale,
                             accum_out=accum).then_inc(s, 1)
            on("scalar", f)
            return sem.inc()

        def vop(name, *args, sem=None):
            sem = sem or SVE
            def f(e, name=name, args=args, s=sem.h):
                getattr(e, name)(*args).then_inc(s, 1)
            on("vector", f)
            return sem.inc()

        def mm(out, lhsT, rhs, start, stop, sem=None):
            def f(e, out=out, l=lhsT, r=rhs, st=start, sp=stop,
                  s=(sem.h if sem else None)):
                ins = e.matmul(out, l, r, start=st, stop=sp)
                if s is not None:
                    ins.then_inc(s, 1)
            on("tensor", f)
            return sem.inc() if sem else None

        def trans(out, in_, sem=None):
            npt = in_.shape[0]
            idap = ident[:] if npt >= 128 else ident[0:npt, 0:npt]
            def f(e, out=out, in_=in_, idap=idap, s=(sem.h if sem else None)):
                ins = e.transpose(out, in_, idap)
                if s is not None:
                    ins.then_inc(s, 1)
            on("tensor", f)
            return sem.inc() if sem else None

        def ln_stats(src_ap, n_free, base, prewait=None):
            c = lambda k: st_small[:, base + k:base + k + 1]
            nm, rstd, ssum, ssq, var, tt2 = (c(0), c(1), c(2), c(3), c(5), c(6))
            v1 = vop("reduce_sum", ssum, src_ap, AX.X)
            if prewait is not None:
                wt("scalar", prewait)
            s1 = act(zrow[:, 0:n_free], src_ap, AF.Square, accum=ssq)
            w("vector", SVE, v1)
            v2 = vop("tensor_scalar_mul", nm, ssum, -1.0 / n_free)
            w("vector", SVE, v2)
            v3 = vop("tensor_mul", tt2, nm, nm)
            w("vector", SSE, s1)
            v4 = vop("tensor_scalar_mul", var, ssq, 1.0 / n_free)
            w("vector", SVE, v4)
            v5 = vop("tensor_sub", var, var, tt2)
            w("scalar", SVE, v5)
            s2 = act(var, var, AF.Ln, bias=EPS)
            w("scalar", SSE, s2)
            s3 = act(rstd, var, AF.Exp, scale=-0.5)
            return (SSE, s3), nm, rstd

        def _tap(name, src_ap, tok):
            if name not in taps:
                return
            o = nc.dram_tensor("tap_" + name, list(src_ap.shape), src_ap.dtype,
                               kind="ExternalOutput").ap()
            for _t in (tok if isinstance(tok, list) else [tok]):
                wt("sync", _t)
            dma("sync", o, src_ap, SOUT)

        def finish():
            w("sync", SOUT, SOUT.n)
            return self.emit()

        for dst, src in ((ident, ident_in), (onesc, onesc_in), (onesr, onesr_in),
                         (onesrf, onesrf_in), (cpool, fconst_in)):
            dma("sync", dst[:], src, SCONST)
        CONSTS1 = SCONST.n
        w("scalar", SCONST, CONSTS1)
        w("vector", SCONST, CONSTS1)

        qkvw = self.sbuf("qkvw", [128, 32 * WCOLS], BF16)
        for dtt in range(32):
            dma("gpsimd", qkvw[:, dtt * WCOLS:(dtt + 1) * WCOLS],
                wqkv_in[dtt * 128:(dtt + 1) * 128, :], SQW)
        QKVW_L = SQW.n
        CAST_GATE = dma("gpsimd", gate_bf, gate_in, SPREGW)
        for _d, _s in ((ga_bfd, ga_in), (ua_bfd, ua_in), (da_bfd, da_in),
                       (gb_bfd, gb_in), (ub_bfd, ub_in), (db_bfd, db_in)):
            dma("gpsimd", _d, _s, SPREG)
        LORA_CASTS = SPREG.n

        perm = self.sbuf("c_perm", [128, 128], BF16)
        ropec = self.sbuf("c_ropec", [128, S], BF16)
        ropes = self.sbuf("c_ropes", [128, S], BF16)
        maskt = self.sbuf("c_masks", [128, 4 * 512], BF16)
        for dst, src in ((perm, perm_in), (ropec, ropec_in), (ropes, ropes_in),
                         (maskt, masks_in)):
            dma("sync", dst[:], src, SCONST2)
        CONSTS2 = SCONST2.n

        SW = 5632
        stripe = self.sbuf("stripe", [128, 2 * SW], BF16)
        xrow = self.sbuf("xrow", [128, 2 * D], F32)
        tstage = self.sbuf("tstage", [128, 1024], BF16)
        x1row = self.sbuf("x1row", [128, D], BF16)
        zrow = self.sbuf("zrow", [128, D], BF16)
        gaT = self.sbuf("gaT_p", [128, 32 * RK], BF16)
        uaT = self.sbuf("uaT_p", [128, 32 * RK], BF16)
        daT = self.sbuf("daT_p", [128, NHT * RK], BF16)
        lora_b1 = self.sbuf("lora_b1", [16, 2 * 1024], BF16)


        x1t_ag2ap = x1t_agin_a.rearrange("(dt p) f -> p dt f", p=128)
        x1t_ag2bp = x1t_agin_b.rearrange("(dt p) f -> p dt f", p=128)
        tst3 = tstage.ap().rearrange("p (k f) -> p k f", k=8)
        xfree_v = [None, None]
        xfree_s = [None, None]
        sta_tok = [None] * 4
        psT_tok = None
        for rt in range(8):
            sl = rt % 2
            wt("sync", xfree_v[sl])
            wt("sync", xfree_s[sl])
            lx = dma("sync", xrow[:, sl * D:(sl + 1) * D],
                     x_in[rt * 128:(rt + 1) * 128, :], SXs[sl])
            w("vector", SXs[sl], lx)
            w("scalar", SXs[sl], lx)
            nv = vop("tensor_copy", x1row[:], xrow[:, sl * D:(sl + 1) * D])
            if sta_tok[rt % 4] is not None:
                w("vector", SSTAs[rt % 4], sta_tok[rt % 4])
                w("scalar", SSTAs[rt % 4], sta_tok[rt % 4])
            tok_ln, nm1, rstd1 = ln_stats(xrow[:, sl * D:(sl + 1) * D], D,
                                          (rt % 4) * 8)
            xfree_v[sl] = (SVE, SVE.n)
            xfree_s[sl] = (SSE, SSE.n)
            wt("sync", tok_ln)
            dma("sync", stats_agin[rt * 128:(rt + 1) * 128, 0:2],
                st_small[:, (rt % 4) * 8:(rt % 4) * 8 + 2], SSTAs[rt % 4])
            sta_tok[rt % 4] = SSTAs[rt % 4].n
            for g in range(4):
                w("tensor", SVE, nv)
                wt("tensor", psT_tok)
                if rt == 0 and g == 0:
                    w("tensor", SCONST, CONSTS1)
                for k in range(8):
                    dtt = g * 8 + k
                    tk = trans(psT[:, k * 128:(k + 1) * 128],
                               x1row[:, dtt * 128:(dtt + 1) * 128],
                               sem=STE if k == 7 else None)
                w("vector", STE, tk)
                w("vector", SX1T, SX1T.n)
                w("vector", SX1TB, SX1TB.n)
                ev = vop("tensor_copy", tstage[:], psT[:])
                psT_tok = (SVE, ev)
                ag2p = x1t_ag2ap if rt < 4 else x1t_ag2bp
                sem1t = SX1T if rt < 4 else SX1TB
                qeng = "sync" if g % 2 == 0 else "scalar"
                w(qeng, SVE, ev)
                dma(qeng, ag2p[:, g * 8:(g + 1) * 8,
                               (rt % 4) * 128:(rt % 4 + 1) * 128],
                    tst3, sem1t)
            if rt == 3:
                w("gpsimd", SX1T, SX1T.n)
                on("gpsimd", lambda e: e.collective_compute(
                    "AllGather", ALU.bypass, replica_groups=[list(range(NCORE))],
                    ins=[x1t_agin_a[0:2048, :]], outs=[x1t_gaA]).then_inc(SCCs[0].h, 1))
                CC_X1TA1 = (SCCs[0], SCCs[0].inc())
                on("gpsimd", lambda e: e.collective_compute(
                    "AllGather", ALU.bypass, replica_groups=[list(range(NCORE))],
                    ins=[x1t_agin_a[2048:4096, :]], outs=[x1t_gaB]).then_inc(SCCs[0].h, 1))
                CC_X1TA2 = (SCCs[0], SCCs[0].inc())
            if rt == 7:
                for _s in SSTAs:
                    w("gpsimd", _s, _s.n)
                on("gpsimd", lambda e: e.collective_compute(
                    "AllGather", ALU.bypass, replica_groups=[list(range(NCORE))],
                    ins=[stats_agin], outs=[stats_g]).then_inc(SCCs[6].h, 1))
                CC_STATS = (SCCs[6], SCCs[6].inc())
                w("gpsimd", SX1TB, SX1TB.n)
                on("gpsimd", lambda e: e.collective_compute(
                    "AllGather", ALU.bypass, replica_groups=[list(range(NCORE))],
                    ins=[x1t_agin_b[0:2048, :]], outs=[x1t_gbA]).then_inc(SCCs[6].h, 1))
                CC_X1TB1 = (SCCs[6], SCCs[6].inc())
                on("gpsimd", lambda e: e.collective_compute(
                    "AllGather", ALU.bypass, replica_groups=[list(range(NCORE))],
                    ins=[x1t_agin_b[2048:4096, :]], outs=[x1t_gbB]).then_inc(SCCs[6].h, 1))
                CC_X1TB2 = (SCCs[6], SCCs[6].inc())
        CAST_UP = dma("gpsimd", up_bf, up_in, SPREGQ)
        CASTS_ALL = dma("gpsimd", down_bf, down_in, SPREGD)
        for dtl in range(4):
            for cc in range(8):
                dma("gpsimd", wo_ag2[dtl, cc],
                    wo_in[dtl * 128:(dtl + 1) * 128, cc * 512:(cc + 1) * 512],
                    SPREGT)
        WO_TILES = SPREGT.n

        psT_tok2 = psT_tok
        lf_tok = None
        lf2_tok = None

        def lora_a(inp, n_dt, dstT):
            nonlocal psT_tok2, lf_tok
            ngg = (n_dt + 7) // 8
            ch_tok = [None, None]
            if lf_tok is None:
                w("sync", SPREG, LORA_CASTS)
            for gg in range(ngg):
                kk = min(8, n_dt - gg * 8)
                par = gg % 2
                wt("sync", ch_tok[par])
                if gg == 0:
                    wt("sync", lf_tok)
                lv = dma("sync", lora_b1[0:16, par * 1024:par * 1024 + kk * 128],
                         inp[:, gg * 1024:gg * 1024 + kk * 128], SLORA)
                w("tensor", SLORA, lv)
                wt("tensor", psT_tok2)
                for k in range(kk):
                    tk = trans(psT[:, k * RK:(k + 1) * RK],
                               lora_b1[0:16, par * 1024 + k * 128:
                                       par * 1024 + (k + 1) * 128],
                               sem=STE if k == kk - 1 else None)
                ch_tok[par] = (STE, tk)
                w("vector", STE, tk)
                ev = vop("tensor_copy", dstT[:, gg * 8 * RK:gg * 8 * RK + kk * RK],
                         psT[:, 0:kk * RK])
                psT_tok2 = (SVE, ev)
                lf_tok = (STE, tk)

        lora_a(ga_bfd, 32, gaT)
        lora_a(ua_bfd, 32, uaT)
        lora_a(da_bfd, NHT, daT)
        psT_tok = psT_tok2
        w("sync", SVE, SVE.n)
        for _d, _t in ((gaT_d, gaT), (uaT_d, uaT), (daT_d, daT)):
            dma("sync", _d, _t[:], SLORA)
        LORA_STASH = SLORA.n

        s_par = [0]
        s_tok = [None, None]

        stripe_jobs = []
        stripe_done = [0]
        smark = {}

        def stripe_job(src_bf, width, writes, cast_wait):
            stripe_jobs.append((src_bf, width, writes, cast_wait))

        def emit_stripe_job():
            if not stripe_jobs:
                return
            src_bf, width, writes, cast_wait = stripe_jobs.pop(0)
            p = s_par[0]
            s_par[0] ^= 1
            base = p * SW
            sem_p = SPRES if p == 0 else SPRES1
            if cast_wait is not None:
                w("sync", cast_wait[0], cast_wait[1])
            wt("sync", s_tok[p])
            def tdma(e, src_bf=src_bf, width=width, base=base, sem_p=sem_p):
                e.dma_start(out=stripe[:, base:base + width], in_=src_bf,
                            transpose=True).then_inc(sem_p.h, 16)
            on("sync", tdma)
            w("sync", sem_p, sem_p.inc(16))
            last = None
            for dst, off, wdt in writes:
                last = dma("sync", dst, stripe[:, base + off:base + off + wdt], sem_p)
            s_tok[p] = (sem_p, last)
            stripe_done[0] += 1
            if stripe_done[0] == N_GATE_JOBS:
                smark["gate"] = (SPRES.n, SPRES1.n)
            if stripe_done[0] == N_UP_JOBS:
                smark["up"] = (SPRES.n, SPRES1.n)
            if stripe_done[0] == N_ALL_JOBS:
                smark["all"] = (SPRES.n, SPRES1.n)

        for dtl in range(4):
            for hplo, hphi in ((0, NHPA), (NHPA, NHP)):
                stripe_job(gate_bf[hplo * 256:hphi * 256,
                                   dtl * 128:(dtl + 1) * 128],
                           (hphi - hplo) * 256,
                           [(gate_ag2[hp, dtl], (hp - hplo) * 256, 256)
                            for hp in range(hplo, hphi)],
                           (SPREGW, CAST_GATE))
        N_GATE_JOBS = len(stripe_jobs)
        for dtl in range(4):
            for hplo, hphi in ((0, NHPA), (NHPA, NHP)):
                stripe_job(up_bf[hplo * 256:hphi * 256,
                                 dtl * 128:(dtl + 1) * 128],
                           (hphi - hplo) * 256,
                           [(up_ag2[hp, dtl], (hp - hplo) * 256, 256)
                            for hp in range(hplo, hphi)],
                           (SPREGQ, CAST_UP))
        N_UP_JOBS = len(stripe_jobs)
        for htl in range(11):
            for half in range(2):
                ag2 = down_ag2A if half == 0 else down_ag2B
                stripe_job(down_bf[half * 2048:(half + 1) * 2048,
                                   htl * 128:(htl + 1) * 128], 2048,
                           [(ag2[htl, dcl], dcl * 512, 512) for dcl in range(4)],
                           (SPREGD, CASTS_ALL))
        N_ALL_JOBS = len(stripe_jobs)

        SPLIT_R = NHPA * 4 * 128

        def emit_gather(agin, lo, hi, out, scc):
            on("gpsimd", lambda e, agin=agin, lo=lo, hi=hi, out=out, scc=scc:
                e.collective_compute(
                    "AllGather", ALU.bypass, replica_groups=[list(range(NCORE))],
                    ins=[agin[lo:hi, :]], outs=[out]).then_inc(scc.h, 1))
            return (scc, scc.inc())

        def emit_gate_ags():
            w("gpsimd", SPRES, smark["gate"][0])
            w("gpsimd", SPRES1, smark["gate"][1])
            ra = emit_gather(gate_agin, 0, SPLIT_R, gate_gA, SCCs[2])
            rb = emit_gather(gate_agin, SPLIT_R, NHP * 4 * 128, gate_gB, SCCs[2])
            return ra, rb

        def emit_up_ags():
            w("gpsimd", SPRES, smark["up"][0])
            w("gpsimd", SPRES1, smark["up"][1])
            ra = emit_gather(up_agin, 0, SPLIT_R, up_gA, SCCs[3])
            return ra

        def emit_upb_ag():
            return emit_gather(up_agin, SPLIT_R, NHP * 4 * 128, up_gB, SCCs[3])

        def emit_weight_ags_a():
            w("gpsimd", SPREGT, WO_TILES)
            on("gpsimd", lambda e: e.collective_compute(
                "AllGather", ALU.bypass, replica_groups=[list(range(NCORE))],
                ins=[wo_agin], outs=[wo_g]).then_inc(SCCs[1].h, 1))
            return (SCCs[1], SCCs[1].inc())

        def emit_down_ag():
            w("gpsimd", SPRES, smark["all"][0])
            w("gpsimd", SPRES1, smark["all"][1])
            on("gpsimd", lambda e: e.collective_compute(
                "AllGather", ALU.bypass, replica_groups=[list(range(NCORE))],
                ins=[down_aginA], outs=[down_gA]).then_inc(SCCs[4].h, 1))
            ra = (SCCs[4], SCCs[4].inc())
            on("gpsimd", lambda e: e.collective_compute(
                "AllGather", ALU.bypass, replica_groups=[list(range(NCORE))],
                ins=[down_aginB], outs=[down_gB]).then_inc(SCCs[4].h, 1))
            rb = (SCCs[4], SCCs[4].inc())
            return ra, rb


        if self.stop <= 1:
            return finish()

        on("gpsimd", lambda e: e.wait_ge(STT.h, 256))
        CC_WO = emit_weight_ags_a()

        self.free_through("xrow")
        mxs = self.sbuf("mxs", [128, 33], BF16)
        bias_bh = self.sbuf("bias_bh", [128, 1], F32)
        ptst = self.sbuf("ptst", [128, 2 * 1024], BF16)
        rcln = self.sbuf("rcln", [1, 2 * 512], F32)
        rcb = self.sbuf("rcb", [1, 2 * 512], BF16)
        recipB = self.sbuf("recipB", [128, 2 * 512], BF16)
        aoraw = self.sbuf("aoraw", [128, 2 * 512], BF16)
        sums_sb = self.sbuf("sums_sb", [1, 2 * 512], F32)
        aostage = self.sbuf("aostage", [128, 2 * 512], BF16)
        smax2 = self.sbuf("smax2", [1, 2], F32)
        x1ring = self.sbuf("x1ring", [128, 8 * 512], BF16)
        qts = self.sbuf("qts", [128, 4 * S], BF16)
        kts = self.sbuf("kts", [128, S], BF16)
        vts = self.sbuf("vts", [128, S], BF16)
        ev_q = self.sbuf("ev_q", [128, 7 * 512], BF16)
        rtmp1 = self.sbuf("rtmp1", [128, 512], F32)
        rtmp2 = self.sbuf("rtmp2", [128, 512], F32)
        rsh = self.sbuf("rsh", [128, 512], BF16)
        stat_rows = self.sbuf("stat_rows", [1, 2 * 1024], F32)
        svw = stat_rows.ap().rearrange("p (s t o) -> p s o t", s=2, o=2)
        rows_bf = self.sbuf("rows_bf", [1, 2 * 1536], BF16)
        rstdB = self.sbuf("rstdB", [128, 2 * 512], F32)
        wsum_sb = self.sbuf("wsum_sb", [1, WCOLS], BF16)

        w("sync", SVE, SVE.n)
        w("sync", SSE, SSE.n)
        w("sync", STE, STE.n)
        w("sync", SX1T, SX1T.n)
        w("sync", SX1TB, SX1TB.n)
        w("vector", SX1T, SX1T.n)
        w("vector", SX1TB, SX1TB.n)
        w("vector", STE, STE.n)
        w("vector", SLORA, LORA_STASH)
        w("tensor", SQW, QKVW_L)
        w("tensor", SCONST2, CONSTS2)

        for dtt in range(32):
            mm(ps_c[0:1, 0:512], onesc[:],
               qkvw[:, dtt * WCOLS:dtt * WCOLS + 512],
               start=(dtt == 0), stop=(dtt == 31),
               sem=SPV if dtt == 31 else None)
        wsA = SPV.n
        w("vector", SPV, wsA)
        e1 = vop("tensor_copy", wsum_sb[0:1, 0:512], ps_c[0:1, 0:512])
        w("tensor", SVE, e1)
        for dtt in range(32):
            mm(ps_c[0:1, 0:256], onesc[:],
               qkvw[:, dtt * WCOLS + 512:dtt * WCOLS + 768],
               start=(dtt == 0), stop=(dtt == 31),
               sem=SPV if dtt == 31 else None)
        wsB = SPV.n
        w("vector", SPV, wsB)
        e2 = vop("tensor_copy", wsum_sb[0:1, 512:768], ps_c[0:1, 0:256])

        evac_tok = [None] * 6
        evq_rd_tok = [[] for _ in range(7)]
        pending_v = [None, None]
        rope_vc = [None]
        psc_tok = (SVE, e2)
        srow_free = [None, None]
        SHUF_MASK = [i ^ 1 for i in range(32)]
        aost_tok = [None] * 4
        aoraw_tok = [None, None]
        sums_tok = [None, None]
        psab_tok2 = [None]
        pending_div = [None]
        ptst_tok = [None, None]
        pshalf_tok = [None, None]
        psab_tok = None
        qts_tok = [None]
        x1t_seen = set()
        tile_i = 0

        GRC_ORDER = [0, 2, 4, 6, 1, 3, 5, 7]
        for b in range(B):
            for idx in range(8):
                rcb_i = GRC_ORDER[idx]
                grc = b * 8 + rcb_i
                rank, col = grc // 2, (grc % 2) * 512
                if b == 1 and idx == 0:
                    wt("vector", qts_tok[0])
                for dtt in range(32):
                    sl = tile_i % 8
                    if tile_i >= 8:
                        w("sync", STT, tile_i - 7)
                    if grc % 2 == 0:
                        g3, cctok = ((x1t_g3aA, CC_X1TA1) if dtt < 16
                                     else (x1t_g3aB, CC_X1TA2))
                    else:
                        g3, cctok = ((x1t_g3bA, CC_X1TB1) if dtt < 16
                                     else (x1t_g3bB, CC_X1TB2))
                    if id(cctok) not in x1t_seen:
                        x1t_seen.add(id(cctok))
                        wt("sync", cctok)
                    lv = dma("sync", x1ring[:, sl * 512:(sl + 1) * 512],
                             g3[rank, (dtt % 16) * 128:(dtt % 16 + 1) * 128, :],
                             SQs[sl])
                    w("tensor", SQs[sl], lv)
                    for wti in range(6):
                        if dtt == 0:
                            wt("tensor", evac_tok[wti])
                        dst = (ps_big[:, wti * 512:(wti + 1) * 512] if wti < 4
                               else (ps_a[:] if wti == 4 else ps_b[:]))
                        mm(dst,
                           qkvw[:, dtt * WCOLS + wti * 128:dtt * WCOLS + (wti + 1) * 128],
                           x1ring[:, sl * 512:(sl + 1) * 512],
                           start=(dtt == 0), stop=False,
                           sem=STT if wti == 5 else None)
                    tile_i += 1
                ssl = idx % 2
                sb0 = ssl * 1024
                rb0 = ssl * 1536
                wt("sync", srow_free[ssl])
                if b == 0 and idx == 0:
                    wt("sync", CC_STATS)
                g0 = rank * OWN + (grc % 2) * 512
                l2 = dma("sync", stat_rows[0:1, sb0:sb0 + 1024],
                         stats_g[g0:g0 + 512, 0:2].rearrange(
                             "(a t) o -> a (t o)", a=1), SSRs[ssl])
                w("vector", SSRs[ssl], l2)
                nm_v = svw[0:1, ssl, 0, :]
                rs_v = svw[0:1, ssl, 1, :]
                nmb = vop("tensor_copy", rows_bf[0:1, rb0:rb0 + 512], nm_v)
                rhi = vop("tensor_copy", rows_bf[0:1, rb0 + 512:rb0 + 1024],
                          rs_v)
                w("vector", SVE, rhi)
                rlo = vop("tensor_sub", rows_bf[0:1, rb0 + 1024:rb0 + 1536],
                          rs_v, rows_bf[0:1, rb0 + 512:rb0 + 1024])
                srow_free[ssl] = (SVE, rlo)
                wt("tensor", psc_tok)
                w("tensor", SVE, rlo)
                mm(ps_c[:], onesr[:], rows_bf[0:1, rb0 + 512:rb0 + 1024],
                   start=True, stop=False)
                bmv = mm(ps_c[:], onesr[:], rows_bf[0:1, rb0 + 1024:rb0 + 1536],
                         start=False, stop=True, sem=SPV)
                w("vector", SPV, bmv)
                cb1 = vop("tensor_copy", rstdB[:, 0:512], ps_c[:])
                w("vector", SVE, cb1)
                cb2 = vop("tensor_scalar_mul", rstdB[:, 512:1024],
                          rstdB[:, 0:512], INV_SQRT_HD)
                psc_tok = (SVE, cb1)
                corr = None
                for wti in range(6):
                    dst = (ps_big[:, wti * 512:(wti + 1) * 512] if wti < 4
                           else (ps_a[:] if wti == 4 else ps_b[:]))
                    corr = mm(dst, wsum_sb[0:1, wti * 128:(wti + 1) * 128],
                              rows_bf[0:1, rb0:rb0 + 512],
                              start=False, stop=True,
                              sem=SPV if wti == 5 else None)
                qkv_done = STT.n
                tcol = rcb_i * 512
                vpar = idx % 2
                for wti in range(6):
                    slot = wti if wti < 5 else 5 + vpar
                    w("vector", STT, qkv_done)
                    w("vector", SPV, corr)
                    if wti == 0:
                        w("vector", SVE, cb2)
                    for _tk in evq_rd_tok[slot]:
                        wt("vector", _tk)
                    src = (ps_big[:, wti * 512:(wti + 1) * 512] if wti < 4
                           else (ps_a[:] if wti == 4 else ps_b[:]))
                    evv = vop("tensor_mul",
                              ev_q[:, slot * 512:(slot + 1) * 512], src,
                              rstdB[:, 512:1024] if wti < 4 else rstdB[:, 0:512])
                    evac_tok[wti] = (SVE, evv)
                vc = rope_vc[0]
                for j in range(5):
                    evq = ev_q[:, j * 512:(j + 1) * 512]
                    wt("vector", evac_tok[j])
                    if vc is not None:
                        w("vector", SVE, vc)
                    sh = vop("stream_shuffle", rsh[:], evq, SHUF_MASK)
                    va = vop("tensor_mul", rtmp1[:], evq, ropec[:, tcol:tcol + 512])
                    w("vector", SVE, sh)
                    vb = vop("tensor_mul", rtmp2[:], rsh[:], ropes[:, tcol:tcol + 512])
                    w("vector", SVE, vb)
                    dst = (qts[:, j * S + tcol:j * S + tcol + 512] if j < 4
                           else kts[:, tcol:tcol + 512])
                    vc = vop("tensor_add", dst, rtmp1[:], rtmp2[:])
                    rope_vc[0] = vc
                    evq_rd_tok[j] = [(SVE, va)]
                def emit_vtrans():
                    nonlocal psT_tok
                    vsl, tc = pending_v[0]
                    w("tensor", SVE, pending_v[1])
                    wt("tensor", psT_tok)
                    for k in range(4):
                        tk = trans(psT[:, k * 128:(k + 1) * 128],
                                   ev_q[:, vsl * 512 + k * 128:
                                        vsl * 512 + (k + 1) * 128],
                                   sem=STE if k == 3 else None)
                    w("vector", STE, tk)
                    ev2 = vop("tensor_copy", vts[:, tc:tc + 512], psT[:, 0:512])
                    psT_tok = (SVE, ev2)
                    evq_rd_tok[vsl] = [(STE, tk)]
                pending_v_new = ((5 + vpar, tcol), evac_tok[5][1])
                if pending_v[0] is not None:
                    emit_vtrans()
                pending_v[0] = pending_v_new[0]
                pending_v[1] = pending_v_new[1]

            if pending_v[0] is not None:
                vslf, tcf = pending_v[0]
                w("tensor", SVE, pending_v[1])
                wt("tensor", psT_tok)
                for k in range(4):
                    tkf = trans(psT[:, k * 128:(k + 1) * 128],
                                ev_q[:, vslf * 512 + k * 128:
                                     vslf * 512 + (k + 1) * 128],
                                sem=STE if k == 3 else None)
                w("vector", STE, tkf)
                ev2f = vop("tensor_copy", vts[:, tcf:tcf + 512], psT[:, 0:512])
                psT_tok = (SVE, ev2f)
                evq_rd_tok[vslf] = [(STE, tkf)]
                pending_v[0] = None

            if b == 1:
                w("gpsimd", STT, 512)
                for _s in SAGs[0:2]:
                    w("gpsimd", _s, _s.n)
                on("gpsimd", lambda e: e.collective_compute(
                    "AllGather", ALU.bypass, replica_groups=[list(range(NCORE))],
                    ins=[at_agin_a], outs=[atg2[0:NCORE * QC, :]]).then_inc(SCCs[5].h, 1))
                CC_ATA = (SCCs[5], SCCs[5].inc())
                CC_UPA = emit_up_ags()
                CC_UPB = emit_upb_ag()
            for h in range(4):
                bh = b * 4 + h
                p1_tok = [pshalf_tok[0], pshalf_tok[1]]
                for qt in range(32):
                    par = qt % 2
                    wt("tensor", p1_tok[par])
                    step = 16 if qt < 16 else 32
                    cnt = (qt + 1) * 128 // step
                    pv = mm(ps_big[:, par * 1024:par * 1024 + cnt],
                            qts[:, h * S + qt * 128:h * S + (qt + 1) * 128],
                            kts[:, 0:(qt + 1) * 128:step],
                            start=True, stop=True, sem=SPV)
                    w("vector", SPV, pv)
                    mv = vop("reduce_max", mxs[:, qt:qt + 1],
                             ps_big[:, par * 1024:par * 1024 + cnt], AX.X)
                    p1_tok[par] = (SVE, mv)
                pshalf_tok = p1_tok
                w("vector", SVE, SVE.n)
                mv1 = vop("reduce_max", mxs[:, 32:33], mxs[:, 0:32], AX.X)
                wt("tensor", psT_tok)
                w("tensor", SVE, mv1)
                tv1 = trans(psT[0:1, 0:128], mxs[:, 32:33], sem=STE)
                w("vector", STE, tv1)
                mv2 = vop("reduce_max", smax2[0:1, 0:1], psT[0:1, 0:128], AX.X)
                psT_tok = (SVE, mv2)
                w("vector", SVE, mv2)
                mv3 = vop("tensor_scalar_mul", smax2[0:1, 1:2], smax2[0:1, 0:1], -1.0)
                wt("tensor", psc_tok)
                w("tensor", SVE, mv3)
                bb = mm(ps_c[:, 0:1], onesrf[:], smax2[0:1, 1:2],
                        start=True, stop=True, sem=SPV)
                w("scalar", SPV, bb)
                bs = act(bias_bh[:], ps_c[:, 0:1], AF.Copy)
                psc_tok = (SSE, bs)
                BIAS_OK = bs

                for qc in range(8):
                    ntt = 4 * (qc + 1)
                    qrhs = qts[:, h * S + qc * 512:h * S + (qc + 1) * 512]
                    ng = ntt // 2

                    def emit_S(g):
                        par = g % 2
                        half = lambda j, par=par: ps_big[:, par * 1024 + j * 512:
                                                         par * 1024 + (j + 1) * 512]
                        wt("tensor", pshalf_tok[par])
                        sv = None
                        for j in range(2):
                            tt = 2 * g + j
                            sv = mm(half(j), kts[:, tt * 128:(tt + 1) * 128], qrhs,
                                    start=True, stop=True,
                                    sem=SPV if j == 1 else None)
                        if (2 * g + 1) >= 4 * qc:
                            w("vector", SPV, sv)
                            msv = None
                            for j in range(2):
                                tt = 2 * g + j
                                if tt >= 4 * qc:
                                    msv = vop("tensor_add", half(j), half(j),
                                              maskt[:, (tt - 4 * qc) * 512:
                                                    (tt - 4 * qc + 1) * 512], sem=SMS)
                            w("scalar", SMS, msv)
                        else:
                            w("scalar", SPV, sv)
                        w("scalar", SSE, BIAS_OK)
                        wt("scalar", ptst_tok[par])
                        ex = act(ptst[:, par * 1024:(par + 1) * 1024],
                                 ps_big[:, par * 1024:(par + 1) * 1024],
                                 AF.Exp, bias=bias_bh[:], sem=SEXP)
                        pshalf_tok[par] = (SEXP, ex)

                    def emit_PV(g):
                        par = g % 2
                        w("tensor", SEXP, pshalf_tok[par][1])
                        if g == 0:
                            wt("tensor", psab_tok)
                            wt("tensor", psab_tok2[0])
                        last = None
                        for j in range(2):
                            tt = 2 * g + j
                            mm(ps_a[:], vts[:, tt * 128:(tt + 1) * 128],
                               ptst[:, par * 1024 + j * 512:par * 1024 + (j + 1) * 512],
                               start=(tt == 0), stop=(tt == ntt - 1))
                            last = mm(ps_b[0:1, :], onesc[:],
                                      ptst[:, par * 1024 + j * 512:
                                           par * 1024 + (j + 1) * 512],
                                      start=(tt == 0), stop=(tt == ntt - 1),
                                      sem=SPV if tt == ntt - 1 else None)
                        ptst_tok[par] = (SPV, SPV.n)
                        return last

                    emit_S(0)
                    last_pv = None
                    for g in range(1, ng):
                        emit_S(g)
                        last_pv = emit_PV(g - 1)
                    last_pv = emit_PV(ng - 1)

                    dsl = (bh * 8 + qc) % 2
                    wt("vector", aoraw_tok[dsl])
                    w("vector", SPV, last_pv)
                    eva = vop("tensor_copy", aoraw[:, dsl * 512:(dsl + 1) * 512],
                              ps_a[:])
                    w("scalar", SPV, last_pv)
                    wt("scalar", sums_tok[dsl])
                    evs = act(sums_sb[0:1, dsl * 512:(dsl + 1) * 512], ps_b[0:1, :],
                              AF.Copy, sem=SSE)
                    psab_tok = (SSE, evs)
                    psab_tok2[0] = (SVE, eva)
                    w("scalar", SSE, evs)
                    l1 = act(rcln[0:1, dsl * 512:(dsl + 1) * 512],
                             sums_sb[0:1, dsl * 512:(dsl + 1) * 512],
                             AF.Ln, sem=SSE)
                    w("scalar", SSE, l1)
                    l2 = act(rcb[0:1, dsl * 512:(dsl + 1) * 512],
                             rcln[0:1, dsl * 512:(dsl + 1) * 512],
                             AF.Exp, scale=-1.0, sem=SSE)
                    sums_tok[dsl] = (SSE, l1)

                    def emit_divide(dsl=dsl, bh=bh, qc=qc, b=b, h=h,
                                    eva=eva, l2=l2):
                        nonlocal psc_tok
                        sl2 = dsl
                        wt("tensor", psc_tok)
                        w("tensor", SSE, l2)
                        bc = mm(ps_c[:], onesr[:], rcb[0:1, sl2 * 512:(sl2 + 1) * 512],
                                start=True, stop=True, sem=SPV)
                        w("scalar", SPV, bc)
                        cb = act(recipB[:, sl2 * 512:(sl2 + 1) * 512], ps_c[:],
                                 AF.Copy, sem=SSE)
                        psc_tok = (SSE, cb)
                        asl = (bh * 8 + qc) % 2
                        wt("vector", aost_tok[asl])
                        w("vector", SSE, cb)
                        w("vector", SVE, eva)
                        dv = vop("tensor_mul", aostage[:, asl * 512:(asl + 1) * 512],
                                 aoraw[:, sl2 * 512:(sl2 + 1) * 512],
                                 recipB[:, sl2 * 512:(sl2 + 1) * 512])
                        aoraw_tok[sl2] = (SVE, dv)
                        w("sync", SVE, dv)
                        agin = at_agin_a if b == 0 else at_agin_b
                        av = dma("sync",
                                 agin[h * 128:(h + 1) * 128,
                                      qc * 512:(qc + 1) * 512],
                                 aostage[:, asl * 512:(asl + 1) * 512], SAGs[asl])
                        aost_tok[asl] = (SAGs[asl], av)
                        emit_stripe_job()
                    if pending_div[0] is not None:
                        pending_div[0]()
                    pending_div[0] = emit_divide
            if pending_div[0] is not None:
                pending_div[0]()
                pending_div[0] = None
            if b == 0:
                CC_GATEA, CC_GATEB = emit_gate_ags()
            if b == 1:
                for _s in SAGs[0:2]:
                    w("gpsimd", _s, _s.n)
                on("gpsimd", lambda e: e.collective_compute(
                    "AllGather", ALU.bypass, replica_groups=[list(range(NCORE))],
                    ins=[at_agin_b],
                    outs=[atg2[NCORE * QC:2 * NCORE * QC, :]]).then_inc(SCCs[7].h, 1))
                CC_ATB = (SCCs[7], SCCs[7].inc())
                while stripe_jobs:
                    emit_stripe_job()
                CC_DOWNA, CC_DOWNB = emit_down_ag()
            qts_tok[0] = (SVE, SVE.n)

        if pending_div[0] is not None:
            pending_div[0]()
            pending_div[0] = None
        if self.stop <= 2:
            return finish()

        self.free_through("qkvw")
        w("vector", SPRES, smark["all"][0])
        w("vector", SPRES1, smark["all"][1])
        xrow = self.sbuf("xrow_b", [128, 2 * D], F32)
        xfree = [None, None]
        tstage = self.sbuf("tstage_b", [128, 1024], BF16)
        x1row = self.sbuf("x1row_b", [128, D], BF16)
        zrow = self.sbuf("zrow_b", [128, D], BF16)
        gaT = self.sbuf("gaT", [128, 32 * RK], BF16)
        uaT = self.sbuf("uaT", [128, 32 * RK], BF16)
        daT = self.sbuf("daT", [128, NHT * RK], BF16)
        lorapk = self.sbuf("lorapk", [128, DH], BF16)
        xrpack = self.sbuf("xrpack", [128, 512], BF16)
        x2tmp = self.sbuf("x2tmp", [128, D], F32)
        x3t = self.sbuf("x3t", [128, 32 * 512], BF16)

        lora_b2 = self.sbuf("lora_b2", [128, NHT * RK], BF16)
        w("sync", SLORA, LORA_STASH)
        for _t, _d in ((gaT, gaT_d), (uaT, uaT_d), (daT, daT_d)):
            dma("sync", _t[:], _d, SLORA)
        LORA_RELOAD = SLORA.n

        psT_tok3 = psT_tok
        lfb_tok = None

        def lora_b(inp, n_ht, dst_row):
            nonlocal psT_tok3, lfb_tok
            wt("sync", lfb_tok)
            lv = dma("sync",
                     lora_b2.ap().rearrange("p (ht j) -> p ht j", j=RK)[:, 0:n_ht, :],
                     inp.rearrange("(ht p) j -> p ht j", p=128), SLORA)
            w("tensor", SLORA, lv)
            for gg in range((n_ht + 7) // 8):
                kk = min(8, n_ht - gg * 8)
                wt("tensor", psT_tok3)
                for k in range(kk):
                    ht = gg * 8 + k
                    tk = trans(psT[0:16, k * 128:(k + 1) * 128],
                               lora_b2[:, ht * RK:(ht + 1) * RK],
                               sem=STE if k == kk - 1 else None)
                w("vector", STE, tk)
                ev = vop("tensor_copy",
                         lorapk[dst_row:dst_row + 16,
                                gg * 1024:gg * 1024 + kk * 128],
                         psT[0:16, 0:kk * 128])
                psT_tok3 = (SVE, ev)
                lfb_tok = (STE, tk)

        lora_b(gb_bfd, NHT, 0)
        lora_b(ub_bfd, NHT, 32)
        lora_b(db_bfd, 32, 64)
        psT_tok = psT_tok3

        out1 = self.sbuf("out1", [128, 4 * D], BF16)
        ats_q = self.sbuf("ats_q", [128, 32 * 512], BF16)
        woring = self.sbuf("woring", [128, 8 * 512], BF16)

        wo_bank_tok = [None] * 4
        out1_free = None
        x2tmp_tok = []
        zrow_tok = None
        x3t_free = None

        x3t_v = x3t.ap().rearrange("p (dt rtl f) -> p dt rtl f", dt=32, rtl=4)

        def emit_x3_transposes(rtl, nx3):
            nonlocal psT_tok
            for g in range(4):
                w("tensor", SVE, nx3)
                wt("tensor", psT_tok)
                for k in range(8):
                    dtt = g * 8 + k
                    tk = trans(psT[:, k * 128:(k + 1) * 128],
                               x1row[:, dtt * 128:(dtt + 1) * 128],
                               sem=STE if k == 7 else None)
                w("vector", STE, tk)
                wt("vector", x3t_free)
                ev = vop("tensor_copy",
                         x3t_v[:, g * 8:(g + 1) * 8, rtl, :],
                         psT.ap().rearrange("p (k f) -> p k f", k=8))
                psT_tok = (SVE, ev)
            return psT_tok
        wotile_i = 0
        ats_tok = None
        for rtq in range(2):
            wt("sync", CC_ATA)
            wt("sync", CC_ATB)
            wt("sync", ats_tok)
            atv = None
            at_g3 = atg2.rearrange("(x p) r -> p x r", p=128)
            ats_q3 = ats_q.ap().rearrange("p (dt r) -> p dt r", dt=32)
            xoff = (pid // 4) * 32
            coff = (pid % 4) * 1024 + rtq * 512
            for dg in range(4):
                atv = dma("sync", ats_q3[:, dg * 8:(dg + 1) * 8, :],
                          at_g3[:, bass.ds(xoff + dg * 8, 8),
                                bass.ds(coff, 512)], SAT)
            for cc in range(8):
                for dtt in range(32):
                    sl = wotile_i % 8
                    qeng = "sync" if sl % 2 == 0 else "gpsimd"
                    if wotile_i >= 8:
                        w(qeng, SWOT, wotile_i - 7)
                    if wotile_i < 2:
                        wt(qeng, CC_WO)
                    lv = dma(qeng, woring[:, sl * 512:(sl + 1) * 512],
                             wo_g4[dtt // 4, dtt % 4, cc], SWOs[sl])
                    w("tensor", SWOs[sl], lv)
                    if dtt == 0:
                        w("tensor", SAT, atv)
                        for rtl in range(4):
                            wt("tensor", wo_bank_tok[rtl])
                    for rtl in range(4):
                        mm(ps_big[:, rtl * 512:(rtl + 1) * 512],
                           ats_q[:, dtt * 512 + rtl * 128:dtt * 512 + (rtl + 1) * 128],
                           woring[:, sl * 512:(sl + 1) * 512],
                           start=(dtt == 0), stop=(dtt == 31),
                           sem=SWOT if rtl == 3 else None)
                    wotile_i += 1
                for rtl in range(4):
                    w("scalar", SWOT, SWOT.n)
                    wt("scalar", out1_free)
                    evv = act(out1[:, rtl * D + cc * 512:rtl * D + (cc + 1) * 512],
                              ps_big[:, rtl * 512:(rtl + 1) * 512], AF.Copy, sem=SEV)
                    wo_bank_tok[rtl] = (SEV, evv)
            WO_EVAC = SEV.n
            ats_tok = (SWOT, SWOT.n)
            for rtl in range(4):
                rt = rtq * 4 + rtl
                sl = rt % 2
                wt("sync", xfree[sl])
                lx = dma("sync", xrow[:, sl * D:(sl + 1) * D],
                         x_in[rt * 128:(rt + 1) * 128, :], SXs[sl])
                w("vector", SEV, WO_EVAC)
                w("scalar", SEV, WO_EVAC)
                o1 = out1[:, rtl * D:(rtl + 1) * D]
                tok_ln, nm2, r2 = ln_stats(o1, D, 0 if sl == 0 else 8,
                                           prewait=zrow_tok)
                wt("vector", tok_ln)
                z2 = vop("tensor_scalar", zrow[:], o1, nm2, r2, ALU.add, ALU.mult)
                w("vector", SVE, z2)
                w("vector", SXs[sl], lx)
                for _tk in x2tmp_tok:
                    wt("vector", _tk)
                xv = vop("tensor_add", x2tmp[:], zrow[:], xrow[:, sl * D:(sl + 1) * D])
                zrow_tok = (SVE, xv)
                xfree[sl] = (SVE, xv)
                w("sync", SVE, xv)
                x2dma = dma("sync", x2_dram[rt * 128:(rt + 1) * 128, :], x2tmp[:], SX2)
                tok3, nm3, r3 = ln_stats(x2tmp[:], D, 16 if sl == 0 else 24,
                                         prewait=zrow_tok)
                w("vector", SSE, tok3[1])
                c1 = vop("tensor_copy", stats_st[:, 2 * rt:2 * rt + 1], r3)
                c2 = vop("tensor_copy", stats_st[:, 2 * rt + 1:2 * rt + 2], nm3)
                if rtq == 0:
                    nx3 = vop("tensor_scalar", x1row[:], x2tmp[:], nm3, r3,
                              ALU.add, ALU.mult)
                    x2tmp_tok = [(SVE, nx3), (SX2, x2dma)]
                    psT_tok = emit_x3_transposes(rtl, nx3)
                else:
                    x2tmp_tok = [(SVE, c2), (SX2, x2dma)]
            out1_free = (SSE, SSE.n)
        X2_ALL = SX2.n

        self.free_through("lora_b2")
        NGSL = 8
        NDSL = 6
        hidt = self.sbuf("hidt", [128, NHP * 512], BF16)
        guringA = self.sbuf("guringA", [128, NGSL * 1024], BF16)
        guringB = self.sbuf("guringB", [128, NGSL * 1024], BF16)
        dwring = self.sbuf("dwring", [128, NDSL * 512], BF16)

        x2r_tok = [None, None]
        zrow_tok2 = [zrow_tok]
        banks_tok = [None] * 4
        gu_slot_tok = [None] * NGSL
        dw_slot_tok = [None] * NDSL
        ts_tok = [None, None]
        x1row_tok = [None]
        o4r_prev = [0]
        gu_i = [0]
        dw_i = [0]
        dw_seen = set()
        gu_seen = set()

        w("tensor", SLORA, LORA_RELOAD)

        def ffn_rc(rc, post_gu1=None):
            nonlocal psc_tok, x3t_free, x2tmp_tok
            XV = SVE.n
            for aT, dst_row in ((gaT, 0), (uaT, 32)):
                wt("tensor", psc_tok)
                w("tensor", SVE, XV)
                for dtt in range(32):
                    xv = mm(ps_c[0:16, :], aT[:, dtt * RK:(dtt + 1) * RK],
                            x3t[:, dtt * 512:(dtt + 1) * 512],
                            start=(dtt == 0), stop=(dtt == 31),
                            sem=SPV if dtt == 31 else None)
                w("scalar", SPV, xv)
                evv = act(xrpack[dst_row:dst_row + 16, :], ps_c[0:16, :], AF.Copy,
                          sem=SSE)
                psc_tok = (SSE, evv)
            XR_OK = psc_tok[1]

            hid_tok = [None] * NHT
            slot_of_rank = [0] * NCORE
            hidr_done = None
            pending_hidr = [None]

            def gu_half(half):
                nonlocal hidr_done
                for hl_ in range(NHP):
                    ht = half * NHP + hl_
                    hl = ht % NHP
                    hp, hoff = ht // 2, (ht % 2) * 128
                    par = ht % 2
                    if ht % 2 == 0:
                        used = []
                        for r in range(NCORE):
                            sl = gu_i[0] % NGSL
                            wt("sync", gu_slot_tok[sl])
                            wt("gpsimd", gu_slot_tok[sl])
                            guk = "A" if hp < NHPA else "B"
                            if ("sync", guk) not in gu_seen:
                                gu_seen.add(("sync", guk))
                                wt("sync", CC_GATEA if guk == "A" else CC_GATEB)
                            if ("gpsimd", guk) not in gu_seen:
                                gu_seen.add(("gpsimd", guk))
                                wt("gpsimd", CC_UPA if guk == "A" else CC_UPB)
                            gsrc = (gate_g4A[r, hp] if hp < NHPA
                                    else gate_g4B[r, hp - NHPA])
                            usrc = (up_g4A[r, hp] if hp < NHPA
                                    else up_g4B[r, hp - NHPA])
                            gA = guringA.ap().rearrange(
                                "p (s dt f) -> p s dt f", s=NGSL, dt=4)
                            gB = guringB.ap().rearrange(
                                "p (s dt f) -> p s dt f", s=NGSL, dt=4)
                            dma("sync", gA[:, sl, :, :],
                                gsrc.rearrange("dt p f -> p dt f"),
                                SFWs[sl])
                            lvs = dma("gpsimd", gB[:, sl, :, :],
                                      usrc.rearrange("dt p f -> p dt f"),
                                      SFWs[sl])
                            used.append((sl, lvs))
                            slot_of_rank[r] = sl
                            gu_i[0] += 1
                        for sl_, lvs_ in used:
                            w("tensor", SFWs[sl_], lvs_)
                    bg = ps_big[:, (par * 2) * 512:(par * 2 + 1) * 512]
                    bu = ps_big[:, (par * 2 + 1) * 512:(par * 2 + 2) * 512]
                    wt("tensor", banks_tok[par * 2])
                    wt("tensor", banks_tok[par * 2 + 1])
                    if ht == 0:
                        w("tensor", SSE, XR_OK)
                    for dtt in range(32):
                        r, dtl = dtt // 4, dtt % 4
                        off = slot_of_rank[r] * 1024 + dtl * 256 + hoff
                        mm(bg, guringA[:, off:off + 128],
                           x3t[:, dtt * 512:(dtt + 1) * 512],
                           start=(dtt == 0), stop=False)
                        s_l = mm(bu, guringB[:, off:off + 128],
                                 x3t[:, dtt * 512:(dtt + 1) * 512],
                                 start=(dtt == 0), stop=False,
                                 sem=SFT if (dtl == 3 and ht % 2 == 1) else None)
                        if dtl == 3 and ht % 2 == 1:
                            gu_slot_tok[slot_of_rank[r]] = (SFT, s_l)
                    mm(bg, lorapk[0:16, ht * 128:(ht + 1) * 128], xrpack[0:16, :],
                       start=False, stop=True)
                    gl = mm(bu, lorapk[32:48, ht * 128:(ht + 1) * 128],
                            xrpack[32:48, :], start=False, stop=True, sem=SFT)
                    w("scalar", SFT, gl)
                    ts_ = tstage[:, par * 512:(par + 1) * 512]
                    wt("scalar", ts_tok[par])
                    ts_tok[par] = None
                    sv = act(ts_, bg, AF.Silu, sem=SSE)
                    banks_tok[par * 2] = (SSE, sv)
                    w("vector", SSE, sv)
                    hv = vop("tensor_mul", hidt[:, hl * 512:(hl + 1) * 512], bu, ts_)
                    banks_tok[par * 2 + 1] = (SVE, hv)
                    hid_tok[ht] = (SVE, hv)
                    w("tensor", SVE, hv)
                    hidr_done = mm(ps_b[0:16, :], daT[:, ht * RK:(ht + 1) * RK],
                                   hidt[:, hl * 512:(hl + 1) * 512],
                                   start=(ht == 0), stop=(ht == NHT - 1),
                                   sem=SPV if ht == NHT - 1 else None)

            def down_half(half, first):
                for dc in range(8):
                    for rtl in range(4):
                        wt("tensor", banks_tok[rtl])
                    if not first:
                        w("tensor", SSE, HIDR_EV)
                        for rtl in range(4):
                            mm(ps_big[:, rtl * 512:(rtl + 1) * 512],
                               xrpack[64:80, rtl * 128:(rtl + 1) * 128],
                               lorapk[64:80, dc * 512:(dc + 1) * 512],
                               start=True, stop=False)
                    for hl in range(NHP):
                        ht = half * NHP + hl
                        sl = dw_i[0] % NDSL
                        qeng = "sync" if sl % 2 == 0 else "gpsimd"
                        wt(qeng, dw_slot_tok[sl])
                        src = (down_g4A[ht // 11, ht % 11, dc] if dc < 4
                               else down_g4B[ht // 11, ht % 11, dc - 4])
                        key = (qeng, "A" if dc < 4 else "B")
                        if key not in dw_seen:
                            dw_seen.add(key)
                            wt(qeng, CC_DOWNA if dc < 4 else CC_DOWNB)
                        lv = dma(qeng, dwring[:, sl * 512:(sl + 1) * 512],
                                 src, SFDs[sl])
                        dw_i[0] += 1
                        w("tensor", SFDs[sl], lv)
                        wt("tensor", hid_tok[ht])
                        dv = None
                        for rtl in range(4):
                            dv = mm(ps_big[:, rtl * 512:(rtl + 1) * 512],
                                    hidt[:, hl * 512 + rtl * 128:
                                         hl * 512 + (rtl + 1) * 128],
                                    dwring[:, sl * 512:(sl + 1) * 512],
                                    start=(first and hl == 0), stop=(hl == NHP - 1),
                                    sem=SFT if rtl == 3 else None)
                        dw_slot_tok[sl] = (SFT, dv)
                    for rtl in range(4):
                        chunk = out4_dram[rtl * 128:(rtl + 1) * 128,
                                          dc * 512:(dc + 1) * 512]
                        if first:
                            if dc == 0 and rtl == 0:
                                w("sync", SO4R, o4r_prev[0])
                            w("scalar", SFT, SFT.n)
                            wt("scalar", ts_tok[0])
                            evv = act(tstage[:, 0:512],
                                      ps_big[:, rtl * 512:(rtl + 1) * 512],
                                      AF.Copy, sem=SEV)
                            banks_tok[rtl] = (SEV, evv)
                            w("sync", SEV, evv)
                            wv = dma("sync", chunk, tstage[:, 0:512], SO4W)
                            ts_tok[0] = (SO4W, wv)
                        else:
                            wt("sync", ts_tok[0])
                            if dc == 0 and rtl == 0:
                                w("sync", SO4W, SO4W.n)
                            lv2 = dma("sync", tstage[:, 0:512], chunk, SO4L)
                            w("vector", SO4L, lv2)
                            w("vector", SFT, SFT.n)
                            wt("vector", ts_tok[1])
                            evv = vop("tensor_add", tstage[:, 512:1024],
                                      ps_big[:, rtl * 512:(rtl + 1) * 512],
                                      tstage[:, 0:512])
                            banks_tok[rtl] = (SVE, evv)
                            ts_tok[0] = (SVE, evv)
                            w("sync", SVE, evv)
                            wv = dma("sync", chunk, tstage[:, 512:1024], SO4W2)
                            ts_tok[1] = (SO4W2, wv)

            gu_half(0)
            down_half(0, True)
            gu_half(1)
            x3t_free = (SFT, SFT.n)
            w("scalar", SPV, hidr_done)
            HIDR_EV = act(xrpack[64:80, :], ps_b[0:16, :], AF.Copy, sem=SSE)
            down_half(1, False)

            for rtl in range(4):
                rt = rc * 4 + rtl
                sl = rtl % 2
                if rtl == 0:
                    w("sync", SO4W2, SO4W2.n)
                wt("sync", x1row_tok[0])
                l4 = dma("sync", x1row[:], out4_dram[rtl * 128:(rtl + 1) * 128, :],
                         SO4R)
                wt("sync", x2r_tok[sl])
                w("sync", SX2, X2_ALL)
                lx = dma("sync", xrow[:, sl * D:(sl + 1) * D],
                         x2_dram[rt * 128:(rt + 1) * 128, :], SX2Rs[sl])
                w("vector", SO4R, l4)
                w("scalar", SO4R, l4)
                tok_ln, nm4, r4 = ln_stats(x1row[:], D, 0 if sl == 0 else 8,
                                           prewait=zrow_tok2[0])
                wt("vector", tok_ln)
                z4 = vop("tensor_scalar", zrow[:], x1row[:], nm4, r4,
                         ALU.add, ALU.mult)
                x1row_tok[0] = (SVE, z4)
                w("vector", SVE, z4)
                w("vector", SX2Rs[sl], lx)
                for _tk in x2tmp_tok:
                    wt("vector", _tk)
                ov = vop("tensor_add", x2tmp[:], zrow[:], xrow[:, sl * D:(sl + 1) * D])
                zrow_tok2[0] = (SVE, ov)
                x2r_tok[sl] = (SVE, ov)
                w("sync", SVE, ov)
                odma = dma("sync", out_ap[rt * 128:(rt + 1) * 128, :], x2tmp[:], SOUT)
                x2tmp_tok = [(SOUT, odma)]
            o4r_prev[0] = SO4R.n

        ffn_rc(0)

        for rtl in range(4):
            rt = 4 + rtl
            sl = rtl % 2
            wt("sync", x2r_tok[sl])
            w("sync", SX2, X2_ALL)
            lx = dma("sync", xrow[:, sl * D:(sl + 1) * D],
                     x2_dram[rt * 128:(rt + 1) * 128, :], SX2Rs[sl])
            w("vector", SX2Rs[sl], lx)
            wt("vector", x1row_tok[0])
            nx3 = vop("tensor_scalar", x1row[:], xrow[:, sl * D:(sl + 1) * D],
                      stats_st[:, 2 * rt + 1:2 * rt + 2],
                      stats_st[:, 2 * rt:2 * rt + 1], ALU.add, ALU.mult)
            x1row_tok[0] = (SVE, nx3)
            x2r_tok[sl] = (SVE, nx3)
            psT_tok = emit_x3_transposes(rtl, nx3)
        ffn_rc(1)

        return finish()


def _shard_inputs(inputs, consts):
    x = np.asarray(inputs["L0_attn_x"]).reshape(R, D)
    wqkv = np.asarray(inputs["L0_attn_w_qkv"])
    wo = np.asarray(inputs["L0_attn_w_o"])
    gate = np.asarray(inputs["L0_ffn_gate_w"])
    up = np.asarray(inputs["L0_ffn_up_w"])
    down = np.asarray(inputs["L0_ffn_down_w"])
    in_maps = []
    for c in range(NCORE):
        qcols = wqkv[:, c * 512:(c + 1) * 512]
        kcols = wqkv[:, 4096 + c * 128:4096 + (c + 1) * 128]
        vcols = wqkv[:, 5120 + c * 128:5120 + (c + 1) * 128]
        dn = down[:, c * 1408:(c + 1) * 1408]
        if dn.shape[1] < 1408:
            dn = np.concatenate(
                [dn, np.zeros((D, 1408 - dn.shape[1]), np.float32)], axis=1)
        m = {
            "x_rows": np.ascontiguousarray(x[c * OWN:(c + 1) * OWN]),
            "wqkv_sl": np.ascontiguousarray(np.concatenate([qcols, kcols, vcols], axis=1)),
            "wo_sl": np.ascontiguousarray(wo[c * 512:(c + 1) * 512]),
            "gate_sl": np.ascontiguousarray(gate[:, c * 512:(c + 1) * 512]),
            "up_sl": np.ascontiguousarray(up[:, c * 512:(c + 1) * 512]),
            "down_sl": np.ascontiguousarray(dn),
            "gate_a": np.asarray(inputs["L0_ffn_gate_a"]),
            "gate_b": np.asarray(inputs["L0_ffn_gate_b"]),
            "up_a": np.asarray(inputs["L0_ffn_up_a"]),
            "up_b": np.asarray(inputs["L0_ffn_up_b"]),
            "down_a": np.asarray(inputs["L0_ffn_down_a"]),
            "down_b": np.asarray(inputs["L0_ffn_down_b"]),
        }
        m.update(consts)
        in_maps.append(m)
    return in_maps


def kernel(**inputs):
    consts = _host_consts()
    in_maps = _shard_inputs(inputs, consts)
    b = Builder()
    nc = b.build()
    res = run_bass_kernel_spmd(nc, in_maps, core_ids=list(range(NCORE)))
    out = np.concatenate([r["out_rows"] for r in res.results], axis=0)
    return out.reshape(B, S, D)


# version 25
# speedup vs baseline: 1.0661x; 1.0078x over previous
import math
import numpy as np
import ml_dtypes
import concourse.bass as bass
import concourse.mybir as mybir
from concourse.bass_utils import run_bass_kernel_spmd

F32 = mybir.dt.float32
BF16 = mybir.dt.bfloat16
AF = mybir.ActivationFunctionType
ALU = mybir.AluOpType
AX = mybir.AxisListType

NCORE = 8
B = 2
S = 4096
R = B * S
D = 4096
HD = 128
QC = 512
WCOLS = 768
DH = 11008
NHT = DH // 128
NHP = NHT // 2
RK = 16
OWN = R // NCORE
EPS = 1e-5
INV_SQRT_HD = 1.0 / math.sqrt(HD)
NEG_BIG = -30000.0

ENGS = ("sync", "tensor", "vector", "scalar", "gpsimd")


def _host_consts():
    c = {}
    c["ident"] = np.eye(128, dtype=ml_dtypes.bfloat16)
    p = np.zeros((128, 128), dtype=np.float32)
    for i in range(128):
        p[i ^ 1, i] = 1.0
    c["perm"] = p.astype(ml_dtypes.bfloat16)
    inv_freq = 1.0 / (10000.0 ** (np.arange(0, HD, 2, dtype=np.float64) / HD))
    ang = np.arange(S, dtype=np.float64)[None, :] * inv_freq[:, None]
    cos, sin = np.cos(ang), np.sin(ang)
    Ct = np.empty((128, S), np.float32)
    St = np.empty((128, S), np.float32)
    Ct[0::2] = cos
    Ct[1::2] = cos
    St[0::2] = -sin
    St[1::2] = sin
    c["rope_c"] = Ct.astype(ml_dtypes.bfloat16)
    c["rope_s"] = St.astype(ml_dtypes.bfloat16)
    masks = np.zeros((128, 4 * 512), np.float32)
    for k in range(4):
        tloc = 128 * k + np.arange(128)[:, None]
        masks[:, k * 512:(k + 1) * 512] = np.where(
            tloc > np.arange(512)[None, :], NEG_BIG, 0.0)
    c["masks"] = masks.astype(ml_dtypes.bfloat16)
    c["ones_col"] = np.ones((128, 1), dtype=ml_dtypes.bfloat16)
    c["ones_row"] = np.ones((1, 128), dtype=ml_dtypes.bfloat16)
    c["ones_row_f"] = np.ones((1, 128), dtype=np.float32)
    c["fconst"] = np.tile(np.array([[0.0, 1.0, EPS, -0.5, -1.0, INV_SQRT_HD, 2.0, 0.5]],
                           dtype=np.float32), (128, 1))
    return c


class Sem:
    def __init__(self, handle):
        self.h = handle
        self.n = 0

    def inc(self, k=1):
        self.n += k
        return self.n


class Builder:
    def __init__(self, taps=(), stop=99):
        self.nc = bass.Bass("TRN2", target_bir_lowering=False)
        self.taps = set(taps)
        self.stop = stop
        self.plan = {e: [] for e in ENGS}
        self.stack = []
        self.t = {}

    def on(self, eng, fn):
        self.plan[eng].append(fn)

    def w(self, eng, sem, val):
        assert val <= sem.n, f"wait on future value {val} > {sem.n}"
        if val > 0:
            self.on(eng, lambda e, s=sem.h, v=val: e.wait_ge(s, v))

    def wt(self, eng, tok):
        if tok is not None:
            self.w(eng, tok[0], tok[1])

    def dma(self, eng, out, in_, sem):
        def f(e, out=out, in_=in_, s=sem.h):
            e.dma_start(out=out, in_=in_).then_inc(s, 16)
        self.on(eng, f)
        return sem.inc(16)

    def sem(self, name):
        g = self.nc.semaphore(name)
        self.stack.append((name, g))
        return Sem(g.__enter__())

    def sbuf(self, name, shape, dtyp):
        g = self.nc.sbuf_tensor(name, shape, dtyp)
        self.stack.append((name, g))
        t = g.__enter__()
        self.t[name] = t
        return t

    def psum(self, name, shape, dtyp=F32):
        g = self.nc.psum_tensor(name, shape, dtyp)
        self.stack.append((name, g))
        t = g.__enter__()
        self.t[name] = t
        return t

    def free_through(self, name):
        while self.stack:
            n, g = self.stack.pop()
            g.__exit__(None, None, None)
            if n == name:
                return
        raise RuntimeError(f"{name} not on stack")

    def emit(self):
        nc = self.nc
        with nc.Block() as block:
            @block.sync
            def _(e):
                for fn in self.plan["sync"]:
                    fn(e)

            @block.tensor
            def _(e):
                for fn in self.plan["tensor"]:
                    fn(e)

            @block.vector
            def _(e):
                for fn in self.plan["vector"]:
                    fn(e)

            @block.scalar
            def _(e):
                for fn in self.plan["scalar"]:
                    fn(e)

            @block.gpsimd
            def _(e):
                for fn in self.plan["gpsimd"]:
                    fn(e)
        while self.stack:
            _, g = self.stack.pop()
            g.__exit__(None, None, None)
        return nc

    def build(self):
        nc = self.nc
        taps = self.taps
        on, w, wt, dma = self.on, self.w, self.wt, self.dma

        def ein(name, shape, dtyp=F32):
            return nc.dram_tensor(name, shape, dtyp, kind="ExternalInput").ap()

        x_in = ein("x_rows", [OWN, D])
        wqkv_in = ein("wqkv_sl", [D, WCOLS])
        wo_in = ein("wo_sl", [512, D])
        gate_in = ein("gate_sl", [DH, 512])
        up_in = ein("up_sl", [DH, 512])
        down_in = ein("down_sl", [D, 1408])
        ga_in = ein("gate_a", [RK, D])
        gb_in = ein("gate_b", [DH, RK])
        ua_in = ein("up_a", [RK, D])
        ub_in = ein("up_b", [DH, RK])
        da_in = ein("down_a", [RK, DH])
        db_in = ein("down_b", [D, RK])
        ident_in = ein("ident", [128, 128], BF16)
        perm_in = ein("perm", [128, 128], BF16)
        ropec_in = ein("rope_c", [128, S], BF16)
        ropes_in = ein("rope_s", [128, S], BF16)
        masks_in = ein("masks", [128, 4 * 512], BF16)
        onesc_in = ein("ones_col", [128, 1], BF16)
        onesr_in = ein("ones_row", [1, 128], BF16)
        onesrf_in = ein("ones_row_f", [1, 128], F32)
        fconst_in = ein("fconst", [128, 8], F32)

        out_ap = nc.dram_tensor("out_rows", [OWN, D], F32, kind="ExternalOutput").ap()

        def idr(name, shape, dtyp):
            return nc.dram_tensor(name, shape, dtyp, kind="Internal").ap()

        gate_bf = idr("gate_bf", [DH, 512], BF16)
        up_bf = idr("up_bf", [DH, 512], BF16)
        down_bf = idr("down_bf", [D, 1408], BF16)
        x1t_agin_a = idr("x1t_agin_a", [D, 512], BF16)
        x1t_agin_b = idr("x1t_agin_b", [D, 512], BF16)
        x1t_gaA = idr("x1t_gaA", [NCORE * 2048, 512], BF16)
        x1t_gaB = idr("x1t_gaB", [NCORE * 2048, 512], BF16)
        x1t_gbA = idr("x1t_gbA", [NCORE * 2048, 512], BF16)
        x1t_gbB = idr("x1t_gbB", [NCORE * 2048, 512], BF16)
        wo_agin = idr("wo_agin", [4 * 8 * 128, 512], BF16)
        wo_g = idr("wo_g", [NCORE * 4 * 8 * 128, 512], BF16)
        NHPA, NHPB = 22, 21
        gate_agin = idr("gate_agin", [NHP * 4 * 128, 256], BF16)
        gate_gA = idr("gate_gA", [NCORE * NHPA * 4 * 128, 256], BF16)
        gate_gB = idr("gate_gB", [NCORE * NHPB * 4 * 128, 256], BF16)
        up_agin = idr("up_agin", [NHP * 4 * 128, 256], BF16)
        up_gA = idr("up_gA", [NCORE * NHPA * 4 * 128, 256], BF16)
        up_gB = idr("up_gB", [NCORE * NHPB * 4 * 128, 256], BF16)
        down_aginA = idr("down_aginA", [11 * 4 * 128, 512], BF16)
        down_aginB = idr("down_aginB", [11 * 4 * 128, 512], BF16)
        down_gA = idr("down_gA", [NCORE * 11 * 4 * 128, 512], BF16)
        down_gB = idr("down_gB", [NCORE * 11 * 4 * 128, 512], BF16)
        out4_dram = idr("out4_dram", [512, D], BF16)
        ga_bfd = idr("ga_bfd", [RK, D], BF16)
        ua_bfd = idr("ua_bfd", [RK, D], BF16)
        da_bfd = idr("da_bfd", [RK, DH], BF16)
        gb_bfd = idr("gb_bfd", [DH, RK], BF16)
        ub_bfd = idr("ub_bfd", [DH, RK], BF16)
        db_bfd = idr("db_bfd", [D, RK], BF16)
        at_agin_a = idr("at_agin_a", [QC, S], BF16)
        at_agin_b = idr("at_agin_b", [QC, S], BF16)
        atg2 = idr("atg2", [2 * NCORE * QC, S], BF16)
        x2_dram = idr("x2_dram", [OWN, D], F32)
        gaT_d = idr("gaT_d", [128, 32 * RK], BF16)
        uaT_d = idr("uaT_d", [128, 32 * RK], BF16)
        daT_d = idr("daT_d", [128, NHT * RK], BF16)
        stats_agin = idr("stats_agin", [OWN, 2], F32)
        stats_g = idr("stats_g", [NCORE * OWN, 2], F32)

        wo_g4 = wo_g.rearrange("(r dt cc p) f -> r dt cc p f", r=NCORE, dt=4, cc=8, p=128)
        gate_g4A = gate_gA.rearrange("(r hp dt p) f -> r hp dt p f", r=NCORE, hp=NHPA, dt=4, p=128)
        gate_g4B = gate_gB.rearrange("(r hp dt p) f -> r hp dt p f", r=NCORE, hp=NHPB, dt=4, p=128)
        up_g4A = up_gA.rearrange("(r hp dt p) f -> r hp dt p f", r=NCORE, hp=NHPA, dt=4, p=128)
        up_g4B = up_gB.rearrange("(r hp dt p) f -> r hp dt p f", r=NCORE, hp=NHPB, dt=4, p=128)
        down_g4A = down_gA.rearrange("(r ht dc p) f -> r ht dc p f", r=NCORE, ht=11, dc=4, p=128)
        down_g4B = down_gB.rearrange("(r ht dc p) f -> r ht dc p f", r=NCORE, ht=11, dc=4, p=128)
        x1t_g3aA = x1t_gaA.rearrange("(r d) f -> r d f", r=NCORE)
        x1t_g3aB = x1t_gaB.rearrange("(r d) f -> r d f", r=NCORE)
        x1t_g3bA = x1t_gbA.rearrange("(r d) f -> r d f", r=NCORE)
        x1t_g3bB = x1t_gbB.rearrange("(r d) f -> r d f", r=NCORE)
        x1t_ag2a = x1t_agin_a.rearrange("(dt p) f -> dt p f", p=128)
        x1t_ag2b = x1t_agin_b.rearrange("(dt p) f -> dt p f", p=128)
        wo_ag2 = wo_agin.rearrange("(dt cc p) f -> dt cc p f", dt=4, cc=8, p=128)
        gate_ag2 = gate_agin.rearrange("(hp dt p) f -> hp dt p f", hp=NHP, dt=4, p=128)
        up_ag2 = up_agin.rearrange("(hp dt p) f -> hp dt p f", hp=NHP, dt=4, p=128)
        down_ag2A = down_aginA.rearrange("(ht dc p) f -> ht dc p f", ht=11, dc=4, p=128)
        down_ag2B = down_aginB.rearrange("(ht dc p) f -> ht dc p f", ht=11, dc=4, p=128)

        SCONST = self.sem("sconst")
        SCONST2 = self.sem("sconst2")
        SPREG = self.sem("spreg")
        SPREGQ = self.sem("spregq")
        SPREGW = self.sem("spregw")
        SPREGD = self.sem("spregd")
        SLAs = [self.sem("sla0"), self.sem("sla1")]
        SPREGT = self.sem("spregt")
        SPRES = self.sem("spres")
        SPRES1 = self.sem("spres1")
        SXs = [self.sem(f"sx{i}") for i in range(2)]
        SX1T = self.sem("sx1t")
        SX1TB = self.sem("sx1tb")
        SCCs = [self.sem(f"scc{i}") for i in range(8)]
        SQW = self.sem("sqw")
        SQs = [self.sem(f"sq{i}") for i in range(8)]
        SLORA = self.sem("slora")
        STT = self.sem("stt")
        STE = self.sem("ste")
        SVE = self.sem("sve")
        SSE = self.sem("sse")
        SEV = self.sem("sev")
        SPV = self.sem("spv")
        SEXP = self.sem("sexp")
        SMS = self.sem("sms")
        SAGs = [self.sem(f"sag{i}") for i in range(4)]
        SWOs = [self.sem(f"swo{i}") for i in range(8)]
        SWOT = self.sem("swot")
        SAT = self.sem("sat")
        SFWs = [self.sem(f"sfw{i}") for i in range(8)]
        SFDs = [self.sem(f"sfd{i}") for i in range(8)]
        SFT = self.sem("sft")
        SO4W = self.sem("so4w")
        SO4W2 = self.sem("so4w2")
        SO4L = self.sem("so4l")
        SO4R = self.sem("so4r")
        SX2 = self.sem("sx2")
        SX2Rs = [self.sem(f"sx2r{i}") for i in range(2)]
        SOUT = self.sem("sout")
        SSTAs = [self.sem(f"ssta{i}") for i in range(4)]
        SSRs = [self.sem(f"ssr{i}") for i in range(2)]

        pid = nc.partition_id()

        ident = self.sbuf("c_ident", [128, 128], BF16)
        onesc = self.sbuf("c_onesc", [128, 1], BF16)
        onesr = self.sbuf("c_onesr", [1, 128], BF16)
        onesrf = self.sbuf("c_onesrf", [1, 128], F32)
        st_small = self.sbuf("st_small", [128, 32], F32)
        stats_st = self.sbuf("stats_st", [128, 16], F32)

        cpool = self.sbuf("cpool", [128, 8], F32)
        for _i, _v in enumerate([0.0, 1.0, EPS, -0.5, -1.0, INV_SQRT_HD, 2.0, 0.5]):
            nc.const_aps.register(F32, _v, cpool[:, _i:_i + 1])

        ps_big = self.psum("ps_big", [128, 2048])
        ps_a = self.psum("ps_a", [128, 512])
        ps_b = self.psum("ps_b", [128, 512])
        ps_c = self.psum("ps_c", [128, 512])
        psT = self.psum("psT", [128, 1024], BF16)

        def act(out, in_, func, bias=0.0, scale=1.0, accum=None, sem=None):
            sem = sem or SSE
            def f(e, out=out, in_=in_, func=func, bias=bias, scale=scale,
                  accum=accum, s=sem.h):
                e.activation(out, in_, func, bias=bias, scale=scale,
                             accum_out=accum).then_inc(s, 1)
            on("scalar", f)
            return sem.inc()

        def vop(name, *args, sem=None):
            sem = sem or SVE
            def f(e, name=name, args=args, s=sem.h):
                getattr(e, name)(*args).then_inc(s, 1)
            on("vector", f)
            return sem.inc()

        def mm(out, lhsT, rhs, start, stop, sem=None):
            def f(e, out=out, l=lhsT, r=rhs, st=start, sp=stop,
                  s=(sem.h if sem else None)):
                ins = e.matmul(out, l, r, start=st, stop=sp)
                if s is not None:
                    ins.then_inc(s, 1)
            on("tensor", f)
            return sem.inc() if sem else None

        def trans(out, in_, sem=None):
            npt = in_.shape[0]
            idap = ident[:] if npt >= 128 else ident[0:npt, 0:npt]
            def f(e, out=out, in_=in_, idap=idap, s=(sem.h if sem else None)):
                ins = e.transpose(out, in_, idap)
                if s is not None:
                    ins.then_inc(s, 1)
            on("tensor", f)
            return sem.inc() if sem else None

        def ln_stats(src_ap, n_free, base, prewait=None):
            c = lambda k: st_small[:, base + k:base + k + 1]
            nm, rstd, ssum, ssq, var, tt2 = (c(0), c(1), c(2), c(3), c(5), c(6))
            v1 = vop("reduce_sum", ssum, src_ap, AX.X)
            if prewait is not None:
                wt("scalar", prewait)
            s1 = act(zrow[:, 0:n_free], src_ap, AF.Square, accum=ssq)
            w("vector", SVE, v1)
            v2 = vop("tensor_scalar_mul", nm, ssum, -1.0 / n_free)
            w("vector", SVE, v2)
            v3 = vop("tensor_mul", tt2, nm, nm)
            w("vector", SSE, s1)
            v4 = vop("tensor_scalar_mul", var, ssq, 1.0 / n_free)
            w("vector", SVE, v4)
            v5 = vop("tensor_sub", var, var, tt2)
            w("scalar", SVE, v5)
            s2 = act(var, var, AF.Ln, bias=EPS)
            w("scalar", SSE, s2)
            s3 = act(rstd, var, AF.Exp, scale=-0.5)
            return (SSE, s3), nm, rstd

        def _tap(name, src_ap, tok):
            if name not in taps:
                return
            o = nc.dram_tensor("tap_" + name, list(src_ap.shape), src_ap.dtype,
                               kind="ExternalOutput").ap()
            for _t in (tok if isinstance(tok, list) else [tok]):
                wt("sync", _t)
            dma("sync", o, src_ap, SOUT)

        def finish():
            w("sync", SOUT, SOUT.n)
            return self.emit()

        for dst, src in ((ident, ident_in), (onesc, onesc_in), (onesr, onesr_in),
                         (onesrf, onesrf_in), (cpool, fconst_in)):
            dma("sync", dst[:], src, SCONST)
        CONSTS1 = SCONST.n
        w("scalar", SCONST, CONSTS1)
        w("vector", SCONST, CONSTS1)

        qkvw = self.sbuf("qkvw", [128, 32 * WCOLS], BF16)
        for dtt in range(32):
            dma("gpsimd", qkvw[:, dtt * WCOLS:(dtt + 1) * WCOLS],
                wqkv_in[dtt * 128:(dtt + 1) * 128, :], SQW)
        QKVW_L = SQW.n
        CAST_GATE = dma("gpsimd", gate_bf, gate_in, SPREGW)
        for _d, _s in ((ga_bfd, ga_in), (ua_bfd, ua_in), (da_bfd, da_in),
                       (gb_bfd, gb_in), (ub_bfd, ub_in), (db_bfd, db_in)):
            dma("gpsimd", _d, _s, SPREG)
        LORA_CASTS = SPREG.n

        perm = self.sbuf("c_perm", [128, 128], BF16)
        ropec = self.sbuf("c_ropec", [128, S], BF16)
        ropes = self.sbuf("c_ropes", [128, S], BF16)
        maskt = self.sbuf("c_masks", [128, 4 * 512], BF16)
        for dst, src in ((perm, perm_in), (ropec, ropec_in), (ropes, ropes_in),
                         (maskt, masks_in)):
            dma("sync", dst[:], src, SCONST2)
        CONSTS2 = SCONST2.n

        SW = 5632
        stripe = self.sbuf("stripe", [128, 2 * SW], BF16)
        xrow = self.sbuf("xrow", [128, 2 * D], F32)
        tstage = self.sbuf("tstage", [128, 1024], BF16)
        x1row = self.sbuf("x1row", [128, D], BF16)
        zrow = self.sbuf("zrow", [128, D], BF16)
        gaT = self.sbuf("gaT_p", [128, 32 * RK], BF16)
        uaT = self.sbuf("uaT_p", [128, 32 * RK], BF16)
        daT = self.sbuf("daT_p", [128, NHT * RK], BF16)
        lora_b1 = self.sbuf("lora_b1", [16, 2 * 1024], BF16)


        x1t_ag2ap = x1t_agin_a.rearrange("(dt p) f -> p dt f", p=128)
        x1t_ag2bp = x1t_agin_b.rearrange("(dt p) f -> p dt f", p=128)
        tst3 = tstage.ap().rearrange("p (k f) -> p k f", k=8)
        xfree_v = [None, None]
        xfree_s = [None, None]
        sta_tok = [None] * 4
        psT_tok = None
        for rt in range(8):
            sl = rt % 2
            wt("sync", xfree_v[sl])
            wt("sync", xfree_s[sl])
            lx = dma("sync", xrow[:, sl * D:(sl + 1) * D],
                     x_in[rt * 128:(rt + 1) * 128, :], SXs[sl])
            w("vector", SXs[sl], lx)
            w("scalar", SXs[sl], lx)
            nv = vop("tensor_copy", x1row[:], xrow[:, sl * D:(sl + 1) * D])
            if sta_tok[rt % 4] is not None:
                w("vector", SSTAs[rt % 4], sta_tok[rt % 4])
                w("scalar", SSTAs[rt % 4], sta_tok[rt % 4])
            tok_ln, nm1, rstd1 = ln_stats(xrow[:, sl * D:(sl + 1) * D], D,
                                          (rt % 4) * 8)
            xfree_v[sl] = (SVE, SVE.n)
            xfree_s[sl] = (SSE, SSE.n)
            wt("sync", tok_ln)
            dma("sync", stats_agin[rt * 128:(rt + 1) * 128, 0:2],
                st_small[:, (rt % 4) * 8:(rt % 4) * 8 + 2], SSTAs[rt % 4])
            sta_tok[rt % 4] = SSTAs[rt % 4].n
            for g in range(4):
                w("tensor", SVE, nv)
                wt("tensor", psT_tok)
                if rt == 0 and g == 0:
                    w("tensor", SCONST, CONSTS1)
                for k in range(8):
                    dtt = g * 8 + k
                    tk = trans(psT[:, k * 128:(k + 1) * 128],
                               x1row[:, dtt * 128:(dtt + 1) * 128],
                               sem=STE if k == 7 else None)
                w("vector", STE, tk)
                w("vector", SX1T, SX1T.n)
                w("vector", SX1TB, SX1TB.n)
                ev = vop("tensor_copy", tstage[:], psT[:])
                psT_tok = (SVE, ev)
                ag2p = x1t_ag2ap if rt < 4 else x1t_ag2bp
                sem1t = SX1T if rt < 4 else SX1TB
                qeng = "sync" if g % 2 == 0 else "scalar"
                w(qeng, SVE, ev)
                dma(qeng, ag2p[:, g * 8:(g + 1) * 8,
                               (rt % 4) * 128:(rt % 4 + 1) * 128],
                    tst3, sem1t)
            if rt == 3:
                w("gpsimd", SX1T, SX1T.n)
                on("gpsimd", lambda e: e.collective_compute(
                    "AllGather", ALU.bypass, replica_groups=[list(range(NCORE))],
                    ins=[x1t_agin_a[0:2048, :]], outs=[x1t_gaA]).then_inc(SCCs[0].h, 1))
                CC_X1TA1 = (SCCs[0], SCCs[0].inc())
                on("gpsimd", lambda e: e.collective_compute(
                    "AllGather", ALU.bypass, replica_groups=[list(range(NCORE))],
                    ins=[x1t_agin_a[2048:4096, :]], outs=[x1t_gaB]).then_inc(SCCs[0].h, 1))
                CC_X1TA2 = (SCCs[0], SCCs[0].inc())
            if rt == 7:
                for _s in SSTAs:
                    w("gpsimd", _s, _s.n)
                on("gpsimd", lambda e: e.collective_compute(
                    "AllGather", ALU.bypass, replica_groups=[list(range(NCORE))],
                    ins=[stats_agin], outs=[stats_g]).then_inc(SCCs[6].h, 1))
                CC_STATS = (SCCs[6], SCCs[6].inc())
                w("gpsimd", SX1TB, SX1TB.n)
                on("gpsimd", lambda e: e.collective_compute(
                    "AllGather", ALU.bypass, replica_groups=[list(range(NCORE))],
                    ins=[x1t_agin_b[0:2048, :]], outs=[x1t_gbA]).then_inc(SCCs[6].h, 1))
                CC_X1TB1 = (SCCs[6], SCCs[6].inc())
                on("gpsimd", lambda e: e.collective_compute(
                    "AllGather", ALU.bypass, replica_groups=[list(range(NCORE))],
                    ins=[x1t_agin_b[2048:4096, :]], outs=[x1t_gbB]).then_inc(SCCs[6].h, 1))
                CC_X1TB2 = (SCCs[6], SCCs[6].inc())
        CAST_UP = dma("gpsimd", up_bf, up_in, SPREGQ)
        CASTS_ALL = dma("gpsimd", down_bf, down_in, SPREGD)
        for dtl in range(4):
            for cc in range(8):
                dma("gpsimd", wo_ag2[dtl, cc],
                    wo_in[dtl * 128:(dtl + 1) * 128, cc * 512:(cc + 1) * 512],
                    SPREGT)
        WO_TILES = SPREGT.n

        psT_tok2 = psT_tok
        lf_tok = None
        lf2_tok = None

        def lora_a(inp, n_dt, dstT):
            nonlocal psT_tok2, lf_tok
            ngg = (n_dt + 7) // 8
            ch_tok = [None, None]
            if lf_tok is None:
                w("sync", SPREG, LORA_CASTS)
            for gg in range(ngg):
                kk = min(8, n_dt - gg * 8)
                par = gg % 2
                wt("sync", ch_tok[par])
                if gg == 0:
                    wt("sync", lf_tok)
                lv = dma("sync", lora_b1[0:16, par * 1024:par * 1024 + kk * 128],
                         inp[:, gg * 1024:gg * 1024 + kk * 128], SLAs[par])
                w("tensor", SLAs[par], lv)
                wt("tensor", psT_tok2)
                for k in range(kk):
                    tk = trans(psT[:, k * RK:(k + 1) * RK],
                               lora_b1[0:16, par * 1024 + k * 128:
                                       par * 1024 + (k + 1) * 128],
                               sem=STE if k == kk - 1 else None)
                ch_tok[par] = (STE, tk)
                w("vector", STE, tk)
                ev = vop("tensor_copy", dstT[:, gg * 8 * RK:gg * 8 * RK + kk * RK],
                         psT[:, 0:kk * RK])
                psT_tok2 = (SVE, ev)
                lf_tok = (STE, tk)

        lora_a(ga_bfd, 32, gaT)
        lora_a(ua_bfd, 32, uaT)
        lora_a(da_bfd, NHT, daT)
        psT_tok = psT_tok2
        w("sync", SVE, SVE.n)
        for _d, _t in ((gaT_d, gaT), (uaT_d, uaT), (daT_d, daT)):
            dma("sync", _d, _t[:], SLORA)
        LORA_STASH = SLORA.n

        s_par = [0]
        s_tok = [None, None]

        stripe_jobs = []
        stripe_done = [0]
        smark = {}

        def stripe_job(src_bf, width, writes, cast_wait):
            stripe_jobs.append((src_bf, width, writes, cast_wait))

        def emit_stripe_job():
            if not stripe_jobs:
                return
            src_bf, width, writes, cast_wait = stripe_jobs.pop(0)
            p = s_par[0]
            s_par[0] ^= 1
            base = p * SW
            sem_p = SPRES if p == 0 else SPRES1
            if cast_wait is not None:
                w("sync", cast_wait[0], cast_wait[1])
            wt("sync", s_tok[p])
            def tdma(e, src_bf=src_bf, width=width, base=base, sem_p=sem_p):
                e.dma_start(out=stripe[:, base:base + width], in_=src_bf,
                            transpose=True).then_inc(sem_p.h, 16)
            on("sync", tdma)
            w("sync", sem_p, sem_p.inc(16))
            last = None
            for dst, off, wdt in writes:
                last = dma("sync", dst, stripe[:, base + off:base + off + wdt], sem_p)
            s_tok[p] = (sem_p, last)
            stripe_done[0] += 1
            if stripe_done[0] == N_GATE_JOBS:
                smark["gate"] = (SPRES.n, SPRES1.n)
            if stripe_done[0] == N_UP_JOBS:
                smark["up"] = (SPRES.n, SPRES1.n)
            if stripe_done[0] == N_ALL_JOBS:
                smark["all"] = (SPRES.n, SPRES1.n)

        for dtl in range(4):
            for hplo, hphi in ((0, NHPA), (NHPA, NHP)):
                stripe_job(gate_bf[hplo * 256:hphi * 256,
                                   dtl * 128:(dtl + 1) * 128],
                           (hphi - hplo) * 256,
                           [(gate_ag2[hp, dtl], (hp - hplo) * 256, 256)
                            for hp in range(hplo, hphi)],
                           (SPREGW, CAST_GATE))
        N_GATE_JOBS = len(stripe_jobs)
        for dtl in range(4):
            for hplo, hphi in ((0, NHPA), (NHPA, NHP)):
                stripe_job(up_bf[hplo * 256:hphi * 256,
                                 dtl * 128:(dtl + 1) * 128],
                           (hphi - hplo) * 256,
                           [(up_ag2[hp, dtl], (hp - hplo) * 256, 256)
                            for hp in range(hplo, hphi)],
                           (SPREGQ, CAST_UP))
        N_UP_JOBS = len(stripe_jobs)
        for htl in range(11):
            for half in range(2):
                ag2 = down_ag2A if half == 0 else down_ag2B
                stripe_job(down_bf[half * 2048:(half + 1) * 2048,
                                   htl * 128:(htl + 1) * 128], 2048,
                           [(ag2[htl, dcl], dcl * 512, 512) for dcl in range(4)],
                           (SPREGD, CASTS_ALL))
        N_ALL_JOBS = len(stripe_jobs)

        SPLIT_R = NHPA * 4 * 128

        def emit_gather(agin, lo, hi, out, scc):
            on("gpsimd", lambda e, agin=agin, lo=lo, hi=hi, out=out, scc=scc:
                e.collective_compute(
                    "AllGather", ALU.bypass, replica_groups=[list(range(NCORE))],
                    ins=[agin[lo:hi, :]], outs=[out]).then_inc(scc.h, 1))
            return (scc, scc.inc())

        def emit_gate_ags():
            w("gpsimd", SPRES, smark["gate"][0])
            w("gpsimd", SPRES1, smark["gate"][1])
            ra = emit_gather(gate_agin, 0, SPLIT_R, gate_gA, SCCs[2])
            rb = emit_gather(gate_agin, SPLIT_R, NHP * 4 * 128, gate_gB, SCCs[2])
            return ra, rb

        def emit_up_ags():
            w("gpsimd", SPRES, smark["up"][0])
            w("gpsimd", SPRES1, smark["up"][1])
            ra = emit_gather(up_agin, 0, SPLIT_R, up_gA, SCCs[3])
            return ra

        def emit_upb_ag():
            return emit_gather(up_agin, SPLIT_R, NHP * 4 * 128, up_gB, SCCs[3])

        def emit_weight_ags_a():
            w("gpsimd", SPREGT, WO_TILES)
            on("gpsimd", lambda e: e.collective_compute(
                "AllGather", ALU.bypass, replica_groups=[list(range(NCORE))],
                ins=[wo_agin], outs=[wo_g]).then_inc(SCCs[1].h, 1))
            return (SCCs[1], SCCs[1].inc())

        def emit_down_ag():
            w("gpsimd", SPRES, smark["all"][0])
            w("gpsimd", SPRES1, smark["all"][1])
            on("gpsimd", lambda e: e.collective_compute(
                "AllGather", ALU.bypass, replica_groups=[list(range(NCORE))],
                ins=[down_aginA], outs=[down_gA]).then_inc(SCCs[4].h, 1))
            ra = (SCCs[4], SCCs[4].inc())
            on("gpsimd", lambda e: e.collective_compute(
                "AllGather", ALU.bypass, replica_groups=[list(range(NCORE))],
                ins=[down_aginB], outs=[down_gB]).then_inc(SCCs[4].h, 1))
            rb = (SCCs[4], SCCs[4].inc())
            return ra, rb


        if self.stop <= 1:
            return finish()

        on("gpsimd", lambda e: e.wait_ge(STT.h, 256))
        CC_WO = emit_weight_ags_a()

        self.free_through("xrow")
        mxs = self.sbuf("mxs", [128, 33], BF16)
        bias_bh = self.sbuf("bias_bh", [128, 1], F32)
        ptst = self.sbuf("ptst", [128, 2 * 1024], BF16)
        rcln = self.sbuf("rcln", [1, 2 * 512], F32)
        rcb = self.sbuf("rcb", [1, 2 * 512], BF16)
        recipB = self.sbuf("recipB", [128, 2 * 512], BF16)
        aoraw = self.sbuf("aoraw", [128, 2 * 512], BF16)
        sums_sb = self.sbuf("sums_sb", [1, 2 * 512], F32)
        aostage = self.sbuf("aostage", [128, 2 * 512], BF16)
        smax2 = self.sbuf("smax2", [1, 2], F32)
        x1ring = self.sbuf("x1ring", [128, 8 * 512], BF16)
        qts = self.sbuf("qts", [128, 4 * S], BF16)
        kts = self.sbuf("kts", [128, S], BF16)
        vts = self.sbuf("vts", [128, S], BF16)
        ev_q = self.sbuf("ev_q", [128, 7 * 512], BF16)
        rtmp1 = self.sbuf("rtmp1", [128, 512], F32)
        rtmp2 = self.sbuf("rtmp2", [128, 512], F32)
        rsh = self.sbuf("rsh", [128, 512], BF16)
        stat_rows = self.sbuf("stat_rows", [1, 2 * 1024], F32)
        svw = stat_rows.ap().rearrange("p (s t o) -> p s o t", s=2, o=2)
        rows_bf = self.sbuf("rows_bf", [1, 2 * 1536], BF16)
        rstdB = self.sbuf("rstdB", [128, 2 * 512], F32)
        wsum_sb = self.sbuf("wsum_sb", [1, WCOLS], BF16)

        w("sync", SVE, SVE.n)
        w("sync", SSE, SSE.n)
        w("sync", STE, STE.n)
        w("sync", SX1T, SX1T.n)
        w("sync", SX1TB, SX1TB.n)
        w("vector", SX1T, SX1T.n)
        w("vector", SX1TB, SX1TB.n)
        w("vector", STE, STE.n)
        w("vector", SLORA, LORA_STASH)
        w("tensor", SQW, QKVW_L)
        w("tensor", SCONST2, CONSTS2)

        for dtt in range(32):
            mm(ps_c[0:1, 0:512], onesc[:],
               qkvw[:, dtt * WCOLS:dtt * WCOLS + 512],
               start=(dtt == 0), stop=(dtt == 31),
               sem=SPV if dtt == 31 else None)
        wsA = SPV.n
        w("vector", SPV, wsA)
        e1 = vop("tensor_copy", wsum_sb[0:1, 0:512], ps_c[0:1, 0:512])
        w("tensor", SVE, e1)
        for dtt in range(32):
            mm(ps_c[0:1, 0:256], onesc[:],
               qkvw[:, dtt * WCOLS + 512:dtt * WCOLS + 768],
               start=(dtt == 0), stop=(dtt == 31),
               sem=SPV if dtt == 31 else None)
        wsB = SPV.n
        w("vector", SPV, wsB)
        e2 = vop("tensor_copy", wsum_sb[0:1, 512:768], ps_c[0:1, 0:256])

        evac_tok = [None] * 6
        evq_rd_tok = [[] for _ in range(7)]
        pending_v = [None, None]
        rope_vc = [None]
        psc_tok = (SVE, e2)
        srow_free = [None, None]
        SHUF_MASK = [i ^ 1 for i in range(32)]
        aost_tok = [None] * 4
        aoraw_tok = [None, None]
        sums_tok = [None, None]
        psab_tok2 = [None]
        pending_div = [None]
        ptst_tok = [None, None]
        pshalf_tok = [None, None]
        psab_tok = None
        qts_tok = [None]
        x1t_seen = set()
        tile_i = 0

        GRC_ORDER = [0, 2, 4, 6, 1, 3, 5, 7]
        for b in range(B):
            for idx in range(8):
                rcb_i = GRC_ORDER[idx]
                grc = b * 8 + rcb_i
                rank, col = grc // 2, (grc % 2) * 512
                if b == 1 and idx == 0:
                    wt("vector", qts_tok[0])
                for dtt in range(32):
                    sl = tile_i % 8
                    if tile_i >= 8:
                        w("sync", STT, tile_i - 7)
                    if grc % 2 == 0:
                        g3, cctok = ((x1t_g3aA, CC_X1TA1) if dtt < 16
                                     else (x1t_g3aB, CC_X1TA2))
                    else:
                        g3, cctok = ((x1t_g3bA, CC_X1TB1) if dtt < 16
                                     else (x1t_g3bB, CC_X1TB2))
                    if id(cctok) not in x1t_seen:
                        x1t_seen.add(id(cctok))
                        wt("sync", cctok)
                    lv = dma("sync", x1ring[:, sl * 512:(sl + 1) * 512],
                             g3[rank, (dtt % 16) * 128:(dtt % 16 + 1) * 128, :],
                             SQs[sl])
                    w("tensor", SQs[sl], lv)
                    for wti in range(6):
                        if dtt == 0:
                            wt("tensor", evac_tok[wti])
                        dst = (ps_big[:, wti * 512:(wti + 1) * 512] if wti < 4
                               else (ps_a[:] if wti == 4 else ps_b[:]))
                        mm(dst,
                           qkvw[:, dtt * WCOLS + wti * 128:dtt * WCOLS + (wti + 1) * 128],
                           x1ring[:, sl * 512:(sl + 1) * 512],
                           start=(dtt == 0), stop=False,
                           sem=STT if wti == 5 else None)
                    tile_i += 1
                ssl = idx % 2
                sb0 = ssl * 1024
                rb0 = ssl * 1536
                wt("sync", srow_free[ssl])
                if b == 0 and idx == 0:
                    wt("sync", CC_STATS)
                g0 = rank * OWN + (grc % 2) * 512
                l2 = dma("sync", stat_rows[0:1, sb0:sb0 + 1024],
                         stats_g[g0:g0 + 512, 0:2].rearrange(
                             "(a t) o -> a (t o)", a=1), SSRs[ssl])
                w("vector", SSRs[ssl], l2)
                nm_v = svw[0:1, ssl, 0, :]
                rs_v = svw[0:1, ssl, 1, :]
                nmb = vop("tensor_copy", rows_bf[0:1, rb0:rb0 + 512], nm_v)
                rhi = vop("tensor_copy", rows_bf[0:1, rb0 + 512:rb0 + 1024],
                          rs_v)
                w("vector", SVE, rhi)
                rlo = vop("tensor_sub", rows_bf[0:1, rb0 + 1024:rb0 + 1536],
                          rs_v, rows_bf[0:1, rb0 + 512:rb0 + 1024])
                srow_free[ssl] = (SVE, rlo)
                wt("tensor", psc_tok)
                w("tensor", SVE, rlo)
                mm(ps_c[:], onesr[:], rows_bf[0:1, rb0 + 512:rb0 + 1024],
                   start=True, stop=False)
                bmv = mm(ps_c[:], onesr[:], rows_bf[0:1, rb0 + 1024:rb0 + 1536],
                         start=False, stop=True, sem=SPV)
                w("vector", SPV, bmv)
                cb1 = vop("tensor_copy", rstdB[:, 0:512], ps_c[:])
                w("vector", SVE, cb1)
                cb2 = vop("tensor_scalar_mul", rstdB[:, 512:1024],
                          rstdB[:, 0:512], INV_SQRT_HD)
                psc_tok = (SVE, cb1)
                corr = None
                for wti in range(6):
                    dst = (ps_big[:, wti * 512:(wti + 1) * 512] if wti < 4
                           else (ps_a[:] if wti == 4 else ps_b[:]))
                    corr = mm(dst, wsum_sb[0:1, wti * 128:(wti + 1) * 128],
                              rows_bf[0:1, rb0:rb0 + 512],
                              start=False, stop=True,
                              sem=SPV if wti == 5 else None)
                qkv_done = STT.n
                tcol = rcb_i * 512
                vpar = idx % 2
                for wti in range(6):
                    slot = wti if wti < 5 else 5 + vpar
                    w("vector", STT, qkv_done)
                    w("vector", SPV, corr)
                    if wti == 0:
                        w("vector", SVE, cb2)
                    for _tk in evq_rd_tok[slot]:
                        wt("vector", _tk)
                    src = (ps_big[:, wti * 512:(wti + 1) * 512] if wti < 4
                           else (ps_a[:] if wti == 4 else ps_b[:]))
                    evv = vop("tensor_mul",
                              ev_q[:, slot * 512:(slot + 1) * 512], src,
                              rstdB[:, 512:1024] if wti < 4 else rstdB[:, 0:512])
                    evac_tok[wti] = (SVE, evv)
                vc = rope_vc[0]
                for j in range(5):
                    evq = ev_q[:, j * 512:(j + 1) * 512]
                    wt("vector", evac_tok[j])
                    if vc is not None:
                        w("vector", SVE, vc)
                    sh = vop("stream_shuffle", rsh[:], evq, SHUF_MASK)
                    va = vop("tensor_mul", rtmp1[:], evq, ropec[:, tcol:tcol + 512])
                    w("vector", SVE, sh)
                    vb = vop("tensor_mul", rtmp2[:], rsh[:], ropes[:, tcol:tcol + 512])
                    w("vector", SVE, vb)
                    dst = (qts[:, j * S + tcol:j * S + tcol + 512] if j < 4
                           else kts[:, tcol:tcol + 512])
                    vc = vop("tensor_add", dst, rtmp1[:], rtmp2[:])
                    rope_vc[0] = vc
                    evq_rd_tok[j] = [(SVE, va)]
                def emit_vtrans():
                    nonlocal psT_tok
                    vsl, tc = pending_v[0]
                    w("tensor", SVE, pending_v[1])
                    wt("tensor", psT_tok)
                    for k in range(4):
                        tk = trans(psT[:, k * 128:(k + 1) * 128],
                                   ev_q[:, vsl * 512 + k * 128:
                                        vsl * 512 + (k + 1) * 128],
                                   sem=STE if k == 3 else None)
                    w("vector", STE, tk)
                    ev2 = vop("tensor_copy", vts[:, tc:tc + 512], psT[:, 0:512])
                    psT_tok = (SVE, ev2)
                    evq_rd_tok[vsl] = [(STE, tk)]
                pending_v_new = ((5 + vpar, tcol), evac_tok[5][1])
                if pending_v[0] is not None:
                    emit_vtrans()
                pending_v[0] = pending_v_new[0]
                pending_v[1] = pending_v_new[1]

            if pending_v[0] is not None:
                vslf, tcf = pending_v[0]
                w("tensor", SVE, pending_v[1])
                wt("tensor", psT_tok)
                for k in range(4):
                    tkf = trans(psT[:, k * 128:(k + 1) * 128],
                                ev_q[:, vslf * 512 + k * 128:
                                     vslf * 512 + (k + 1) * 128],
                                sem=STE if k == 3 else None)
                w("vector", STE, tkf)
                ev2f = vop("tensor_copy", vts[:, tcf:tcf + 512], psT[:, 0:512])
                psT_tok = (SVE, ev2f)
                evq_rd_tok[vslf] = [(STE, tkf)]
                pending_v[0] = None

            if b == 1:
                w("gpsimd", STT, 512)
                for _s in SAGs[0:2]:
                    w("gpsimd", _s, _s.n)
                on("gpsimd", lambda e: e.collective_compute(
                    "AllGather", ALU.bypass, replica_groups=[list(range(NCORE))],
                    ins=[at_agin_a], outs=[atg2[0:NCORE * QC, :]]).then_inc(SCCs[5].h, 1))
                CC_ATA = (SCCs[5], SCCs[5].inc())
                CC_UPA = emit_up_ags()
                CC_UPB = emit_upb_ag()
            for h in range(4):
                bh = b * 4 + h
                p1_tok = [pshalf_tok[0], pshalf_tok[1]]
                for qt in range(32):
                    par = qt % 2
                    wt("tensor", p1_tok[par])
                    step = 16 if qt < 16 else 32
                    cnt = (qt + 1) * 128 // step
                    pv = mm(ps_big[:, par * 1024:par * 1024 + cnt],
                            qts[:, h * S + qt * 128:h * S + (qt + 1) * 128],
                            kts[:, 0:(qt + 1) * 128:step],
                            start=True, stop=True, sem=SPV)
                    w("vector", SPV, pv)
                    mv = vop("reduce_max", mxs[:, qt:qt + 1],
                             ps_big[:, par * 1024:par * 1024 + cnt], AX.X)
                    p1_tok[par] = (SVE, mv)
                pshalf_tok = p1_tok
                w("vector", SVE, SVE.n)
                mv1 = vop("reduce_max", mxs[:, 32:33], mxs[:, 0:32], AX.X)
                wt("tensor", psT_tok)
                w("tensor", SVE, mv1)
                tv1 = trans(psT[0:1, 0:128], mxs[:, 32:33], sem=STE)
                w("vector", STE, tv1)
                mv2 = vop("reduce_max", smax2[0:1, 0:1], psT[0:1, 0:128], AX.X)
                psT_tok = (SVE, mv2)
                w("vector", SVE, mv2)
                mv3 = vop("tensor_scalar_mul", smax2[0:1, 1:2], smax2[0:1, 0:1], -1.0)
                wt("tensor", psc_tok)
                w("tensor", SVE, mv3)
                bb = mm(ps_c[:, 0:1], onesrf[:], smax2[0:1, 1:2],
                        start=True, stop=True, sem=SPV)
                w("scalar", SPV, bb)
                bs = act(bias_bh[:], ps_c[:, 0:1], AF.Copy)
                psc_tok = (SSE, bs)
                BIAS_OK = bs

                for qc in range(8):
                    ntt = 4 * (qc + 1)
                    qrhs = qts[:, h * S + qc * 512:h * S + (qc + 1) * 512]
                    ng = ntt // 2

                    def emit_S(g):
                        par = g % 2
                        half = lambda j, par=par: ps_big[:, par * 1024 + j * 512:
                                                         par * 1024 + (j + 1) * 512]
                        wt("tensor", pshalf_tok[par])
                        sv = None
                        for j in range(2):
                            tt = 2 * g + j
                            sv = mm(half(j), kts[:, tt * 128:(tt + 1) * 128], qrhs,
                                    start=True, stop=True,
                                    sem=SPV if j == 1 else None)
                        if (2 * g + 1) >= 4 * qc:
                            w("vector", SPV, sv)
                            msv = None
                            for j in range(2):
                                tt = 2 * g + j
                                if tt >= 4 * qc:
                                    msv = vop("tensor_add", half(j), half(j),
                                              maskt[:, (tt - 4 * qc) * 512:
                                                    (tt - 4 * qc + 1) * 512], sem=SMS)
                            w("scalar", SMS, msv)
                        else:
                            w("scalar", SPV, sv)
                        w("scalar", SSE, BIAS_OK)
                        wt("scalar", ptst_tok[par])
                        ex = act(ptst[:, par * 1024:(par + 1) * 1024],
                                 ps_big[:, par * 1024:(par + 1) * 1024],
                                 AF.Exp, bias=bias_bh[:], sem=SEXP)
                        pshalf_tok[par] = (SEXP, ex)

                    def emit_PV(g):
                        par = g % 2
                        w("tensor", SEXP, pshalf_tok[par][1])
                        if g == 0:
                            wt("tensor", psab_tok)
                            wt("tensor", psab_tok2[0])
                        last = None
                        for j in range(2):
                            tt = 2 * g + j
                            mm(ps_a[:], vts[:, tt * 128:(tt + 1) * 128],
                               ptst[:, par * 1024 + j * 512:par * 1024 + (j + 1) * 512],
                               start=(tt == 0), stop=(tt == ntt - 1))
                            last = mm(ps_b[0:1, :], onesc[:],
                                      ptst[:, par * 1024 + j * 512:
                                           par * 1024 + (j + 1) * 512],
                                      start=(tt == 0), stop=(tt == ntt - 1),
                                      sem=SPV if tt == ntt - 1 else None)
                        ptst_tok[par] = (SPV, SPV.n)
                        return last

                    emit_S(0)
                    last_pv = None
                    for g in range(1, ng):
                        emit_S(g)
                        last_pv = emit_PV(g - 1)
                    last_pv = emit_PV(ng - 1)

                    dsl = (bh * 8 + qc) % 2
                    wt("vector", aoraw_tok[dsl])
                    w("vector", SPV, last_pv)
                    eva = vop("tensor_copy", aoraw[:, dsl * 512:(dsl + 1) * 512],
                              ps_a[:])
                    w("scalar", SPV, last_pv)
                    wt("scalar", sums_tok[dsl])
                    evs = act(sums_sb[0:1, dsl * 512:(dsl + 1) * 512], ps_b[0:1, :],
                              AF.Copy, sem=SSE)
                    psab_tok = (SSE, evs)
                    psab_tok2[0] = (SVE, eva)
                    w("scalar", SSE, evs)
                    l1 = act(rcln[0:1, dsl * 512:(dsl + 1) * 512],
                             sums_sb[0:1, dsl * 512:(dsl + 1) * 512],
                             AF.Ln, sem=SSE)
                    w("scalar", SSE, l1)
                    l2 = act(rcb[0:1, dsl * 512:(dsl + 1) * 512],
                             rcln[0:1, dsl * 512:(dsl + 1) * 512],
                             AF.Exp, scale=-1.0, sem=SSE)
                    sums_tok[dsl] = (SSE, l1)

                    def emit_divide(dsl=dsl, bh=bh, qc=qc, b=b, h=h,
                                    eva=eva, l2=l2):
                        nonlocal psc_tok
                        sl2 = dsl
                        wt("tensor", psc_tok)
                        w("tensor", SSE, l2)
                        bc = mm(ps_c[:], onesr[:], rcb[0:1, sl2 * 512:(sl2 + 1) * 512],
                                start=True, stop=True, sem=SPV)
                        w("scalar", SPV, bc)
                        cb = act(recipB[:, sl2 * 512:(sl2 + 1) * 512], ps_c[:],
                                 AF.Copy, sem=SSE)
                        psc_tok = (SSE, cb)
                        asl = (bh * 8 + qc) % 2
                        wt("vector", aost_tok[asl])
                        w("vector", SSE, cb)
                        w("vector", SVE, eva)
                        dv = vop("tensor_mul", aostage[:, asl * 512:(asl + 1) * 512],
                                 aoraw[:, sl2 * 512:(sl2 + 1) * 512],
                                 recipB[:, sl2 * 512:(sl2 + 1) * 512])
                        aoraw_tok[sl2] = (SVE, dv)
                        w("sync", SVE, dv)
                        agin = at_agin_a if b == 0 else at_agin_b
                        av = dma("sync",
                                 agin[h * 128:(h + 1) * 128,
                                      qc * 512:(qc + 1) * 512],
                                 aostage[:, asl * 512:(asl + 1) * 512], SAGs[asl])
                        aost_tok[asl] = (SAGs[asl], av)
                        emit_stripe_job()
                    if pending_div[0] is not None:
                        pending_div[0]()
                    pending_div[0] = emit_divide
            if pending_div[0] is not None:
                pending_div[0]()
                pending_div[0] = None
            if b == 0:
                CC_GATEA, CC_GATEB = emit_gate_ags()
            if b == 1:
                for _s in SAGs[0:2]:
                    w("gpsimd", _s, _s.n)
                on("gpsimd", lambda e: e.collective_compute(
                    "AllGather", ALU.bypass, replica_groups=[list(range(NCORE))],
                    ins=[at_agin_b],
                    outs=[atg2[NCORE * QC:2 * NCORE * QC, :]]).then_inc(SCCs[7].h, 1))
                CC_ATB = (SCCs[7], SCCs[7].inc())
                while stripe_jobs:
                    emit_stripe_job()
                CC_DOWNA, CC_DOWNB = emit_down_ag()
            qts_tok[0] = (SVE, SVE.n)

        if pending_div[0] is not None:
            pending_div[0]()
            pending_div[0] = None
        if self.stop <= 2:
            return finish()

        self.free_through("qkvw")
        w("vector", SPRES, smark["all"][0])
        w("vector", SPRES1, smark["all"][1])
        xrow = self.sbuf("xrow_b", [128, 2 * D], F32)
        xfree = [None, None]
        tstage = self.sbuf("tstage_b", [128, 1024], BF16)
        x1row = self.sbuf("x1row_b", [128, D], BF16)
        zrow = self.sbuf("zrow_b", [128, D], BF16)
        gaT = self.sbuf("gaT", [128, 32 * RK], BF16)
        uaT = self.sbuf("uaT", [128, 32 * RK], BF16)
        daT = self.sbuf("daT", [128, NHT * RK], BF16)
        lorapk = self.sbuf("lorapk", [128, DH], BF16)
        xrpack = self.sbuf("xrpack", [128, 512], BF16)
        x2tmp = self.sbuf("x2tmp", [128, D], F32)
        x3t = self.sbuf("x3t", [128, 32 * 512], BF16)

        lora_b2 = self.sbuf("lora_b2", [128, NHT * RK], BF16)
        w("sync", SLORA, LORA_STASH)
        for _t, _d in ((gaT, gaT_d), (uaT, uaT_d), (daT, daT_d)):
            dma("sync", _t[:], _d, SLORA)
        LORA_RELOAD = SLORA.n

        psT_tok3 = psT_tok
        lfb_tok = None

        def lora_b(inp, n_ht, dst_row):
            nonlocal psT_tok3, lfb_tok
            wt("sync", lfb_tok)
            lv = dma("sync",
                     lora_b2.ap().rearrange("p (ht j) -> p ht j", j=RK)[:, 0:n_ht, :],
                     inp.rearrange("(ht p) j -> p ht j", p=128), SLORA)
            w("tensor", SLORA, lv)
            for gg in range((n_ht + 7) // 8):
                kk = min(8, n_ht - gg * 8)
                wt("tensor", psT_tok3)
                for k in range(kk):
                    ht = gg * 8 + k
                    tk = trans(psT[0:16, k * 128:(k + 1) * 128],
                               lora_b2[:, ht * RK:(ht + 1) * RK],
                               sem=STE if k == kk - 1 else None)
                w("vector", STE, tk)
                ev = vop("tensor_copy",
                         lorapk[dst_row:dst_row + 16,
                                gg * 1024:gg * 1024 + kk * 128],
                         psT[0:16, 0:kk * 128])
                psT_tok3 = (SVE, ev)
                lfb_tok = (STE, tk)

        lora_b(gb_bfd, NHT, 0)
        lora_b(ub_bfd, NHT, 32)
        lora_b(db_bfd, 32, 64)
        psT_tok = psT_tok3

        out1 = self.sbuf("out1", [128, 4 * D], BF16)
        ats_q = self.sbuf("ats_q", [128, 32 * 512], BF16)
        woring = self.sbuf("woring", [128, 8 * 512], BF16)

        wo_bank_tok = [None] * 4
        out1_free = None
        x2tmp_tok = []
        zrow_tok = None
        x3t_free = None

        x3t_v = x3t.ap().rearrange("p (dt rtl f) -> p dt rtl f", dt=32, rtl=4)

        def emit_x3_transposes(rtl, nx3):
            nonlocal psT_tok
            for g in range(4):
                w("tensor", SVE, nx3)
                wt("tensor", psT_tok)
                for k in range(8):
                    dtt = g * 8 + k
                    tk = trans(psT[:, k * 128:(k + 1) * 128],
                               x1row[:, dtt * 128:(dtt + 1) * 128],
                               sem=STE if k == 7 else None)
                w("vector", STE, tk)
                wt("vector", x3t_free)
                ev = vop("tensor_copy",
                         x3t_v[:, g * 8:(g + 1) * 8, rtl, :],
                         psT.ap().rearrange("p (k f) -> p k f", k=8))
                psT_tok = (SVE, ev)
            return psT_tok
        wotile_i = 0
        ats_tok = None
        for rtq in range(2):
            wt("sync", CC_ATA)
            wt("sync", CC_ATB)
            wt("sync", ats_tok)
            atv = None
            at_g3 = atg2.rearrange("(x p) r -> p x r", p=128)
            ats_q3 = ats_q.ap().rearrange("p (dt r) -> p dt r", dt=32)
            xoff = (pid // 4) * 32
            coff = (pid % 4) * 1024 + rtq * 512
            for dg in range(4):
                atv = dma("sync", ats_q3[:, dg * 8:(dg + 1) * 8, :],
                          at_g3[:, bass.ds(xoff + dg * 8, 8),
                                bass.ds(coff, 512)], SAT)
            for cc in range(8):
                for dtt in range(32):
                    sl = wotile_i % 8
                    qeng = "sync" if sl % 2 == 0 else "gpsimd"
                    if wotile_i >= 8:
                        w(qeng, SWOT, wotile_i - 7)
                    if wotile_i < 2:
                        wt(qeng, CC_WO)
                    lv = dma(qeng, woring[:, sl * 512:(sl + 1) * 512],
                             wo_g4[dtt // 4, dtt % 4, cc], SWOs[sl])
                    w("tensor", SWOs[sl], lv)
                    if dtt == 0:
                        w("tensor", SAT, atv)
                        for rtl in range(4):
                            wt("tensor", wo_bank_tok[rtl])
                    for rtl in range(4):
                        mm(ps_big[:, rtl * 512:(rtl + 1) * 512],
                           ats_q[:, dtt * 512 + rtl * 128:dtt * 512 + (rtl + 1) * 128],
                           woring[:, sl * 512:(sl + 1) * 512],
                           start=(dtt == 0), stop=(dtt == 31),
                           sem=SWOT if rtl == 3 else None)
                    wotile_i += 1
                for rtl in range(4):
                    w("scalar", SWOT, SWOT.n)
                    wt("scalar", out1_free)
                    evv = act(out1[:, rtl * D + cc * 512:rtl * D + (cc + 1) * 512],
                              ps_big[:, rtl * 512:(rtl + 1) * 512], AF.Copy, sem=SEV)
                    wo_bank_tok[rtl] = (SEV, evv)
            WO_EVAC = SEV.n
            ats_tok = (SWOT, SWOT.n)
            for rtl in range(4):
                rt = rtq * 4 + rtl
                sl = rt % 2
                wt("sync", xfree[sl])
                lx = dma("sync", xrow[:, sl * D:(sl + 1) * D],
                         x_in[rt * 128:(rt + 1) * 128, :], SXs[sl])
                w("vector", SEV, WO_EVAC)
                w("scalar", SEV, WO_EVAC)
                o1 = out1[:, rtl * D:(rtl + 1) * D]
                tok_ln, nm2, r2 = ln_stats(o1, D, 0 if sl == 0 else 8,
                                           prewait=zrow_tok)
                wt("vector", tok_ln)
                z2 = vop("tensor_scalar", zrow[:], o1, nm2, r2, ALU.add, ALU.mult)
                w("vector", SVE, z2)
                w("vector", SXs[sl], lx)
                for _tk in x2tmp_tok:
                    wt("vector", _tk)
                xv = vop("tensor_add", x2tmp[:], zrow[:], xrow[:, sl * D:(sl + 1) * D])
                zrow_tok = (SVE, xv)
                xfree[sl] = (SVE, xv)
                w("sync", SVE, xv)
                x2dma = dma("sync", x2_dram[rt * 128:(rt + 1) * 128, :], x2tmp[:], SX2)
                tok3, nm3, r3 = ln_stats(x2tmp[:], D, 16 if sl == 0 else 24,
                                         prewait=zrow_tok)
                w("vector", SSE, tok3[1])
                c1 = vop("tensor_copy", stats_st[:, 2 * rt:2 * rt + 1], r3)
                c2 = vop("tensor_copy", stats_st[:, 2 * rt + 1:2 * rt + 2], nm3)
                if rtq == 0:
                    nx3 = vop("tensor_scalar", x1row[:], x2tmp[:], nm3, r3,
                              ALU.add, ALU.mult)
                    x2tmp_tok = [(SVE, nx3), (SX2, x2dma)]
                    psT_tok = emit_x3_transposes(rtl, nx3)
                else:
                    x2tmp_tok = [(SVE, c2), (SX2, x2dma)]
            out1_free = (SSE, SSE.n)
        X2_ALL = SX2.n

        self.free_through("lora_b2")
        NGSL = 8
        NDSL = 6
        hidt = self.sbuf("hidt", [128, NHP * 512], BF16)
        guringA = self.sbuf("guringA", [128, NGSL * 1024], BF16)
        guringB = self.sbuf("guringB", [128, NGSL * 1024], BF16)
        dwring = self.sbuf("dwring", [128, NDSL * 512], BF16)

        x2r_tok = [None, None]
        zrow_tok2 = [zrow_tok]
        banks_tok = [None] * 4
        gu_slot_tok = [None] * NGSL
        dw_slot_tok = [None] * NDSL
        ts_tok = [None, None]
        x1row_tok = [None]
        o4r_prev = [0]
        gu_i = [0]
        dw_i = [0]
        dw_seen = set()
        gu_seen = set()

        w("tensor", SLORA, LORA_RELOAD)

        def ffn_rc(rc, post_gu1=None):
            nonlocal psc_tok, x3t_free, x2tmp_tok
            XV = SVE.n
            for aT, dst_row in ((gaT, 0), (uaT, 32)):
                wt("tensor", psc_tok)
                w("tensor", SVE, XV)
                for dtt in range(32):
                    xv = mm(ps_c[0:16, :], aT[:, dtt * RK:(dtt + 1) * RK],
                            x3t[:, dtt * 512:(dtt + 1) * 512],
                            start=(dtt == 0), stop=(dtt == 31),
                            sem=SPV if dtt == 31 else None)
                w("scalar", SPV, xv)
                evv = act(xrpack[dst_row:dst_row + 16, :], ps_c[0:16, :], AF.Copy,
                          sem=SSE)
                psc_tok = (SSE, evv)
            XR_OK = psc_tok[1]

            hid_tok = [None] * NHT
            slot_of_rank = [0] * NCORE
            hidr_done = None
            pending_hidr = [None]

            def gu_half(half):
                nonlocal hidr_done
                for hl_ in range(NHP):
                    ht = half * NHP + hl_
                    hl = ht % NHP
                    hp, hoff = ht // 2, (ht % 2) * 128
                    par = ht % 2
                    if ht % 2 == 0:
                        used = []
                        for r in range(NCORE):
                            sl = gu_i[0] % NGSL
                            wt("sync", gu_slot_tok[sl])
                            wt("gpsimd", gu_slot_tok[sl])
                            guk = "A" if hp < NHPA else "B"
                            if ("sync", guk) not in gu_seen:
                                gu_seen.add(("sync", guk))
                                wt("sync", CC_GATEA if guk == "A" else CC_GATEB)
                            if ("gpsimd", guk) not in gu_seen:
                                gu_seen.add(("gpsimd", guk))
                                wt("gpsimd", CC_UPA if guk == "A" else CC_UPB)
                            gsrc = (gate_g4A[r, hp] if hp < NHPA
                                    else gate_g4B[r, hp - NHPA])
                            usrc = (up_g4A[r, hp] if hp < NHPA
                                    else up_g4B[r, hp - NHPA])
                            gA = guringA.ap().rearrange(
                                "p (s dt f) -> p s dt f", s=NGSL, dt=4)
                            gB = guringB.ap().rearrange(
                                "p (s dt f) -> p s dt f", s=NGSL, dt=4)
                            dma("sync", gA[:, sl, :, :],
                                gsrc.rearrange("dt p f -> p dt f"),
                                SFWs[sl])
                            lvs = dma("gpsimd", gB[:, sl, :, :],
                                      usrc.rearrange("dt p f -> p dt f"),
                                      SFWs[sl])
                            used.append((sl, lvs))
                            slot_of_rank[r] = sl
                            gu_i[0] += 1
                        for sl_, lvs_ in used:
                            w("tensor", SFWs[sl_], lvs_)
                    bg = ps_big[:, (par * 2) * 512:(par * 2 + 1) * 512]
                    bu = ps_big[:, (par * 2 + 1) * 512:(par * 2 + 2) * 512]
                    wt("tensor", banks_tok[par * 2])
                    wt("tensor", banks_tok[par * 2 + 1])
                    if ht == 0:
                        w("tensor", SSE, XR_OK)
                    for dtt in range(32):
                        r, dtl = dtt // 4, dtt % 4
                        off = slot_of_rank[r] * 1024 + dtl * 256 + hoff
                        mm(bg, guringA[:, off:off + 128],
                           x3t[:, dtt * 512:(dtt + 1) * 512],
                           start=(dtt == 0), stop=False)
                        s_l = mm(bu, guringB[:, off:off + 128],
                                 x3t[:, dtt * 512:(dtt + 1) * 512],
                                 start=(dtt == 0), stop=False,
                                 sem=SFT if (dtl == 3 and ht % 2 == 1) else None)
                        if dtl == 3 and ht % 2 == 1:
                            gu_slot_tok[slot_of_rank[r]] = (SFT, s_l)
                    mm(bg, lorapk[0:16, ht * 128:(ht + 1) * 128], xrpack[0:16, :],
                       start=False, stop=True)
                    gl = mm(bu, lorapk[32:48, ht * 128:(ht + 1) * 128],
                            xrpack[32:48, :], start=False, stop=True, sem=SFT)
                    w("scalar", SFT, gl)
                    ts_ = tstage[:, par * 512:(par + 1) * 512]
                    wt("scalar", ts_tok[par])
                    ts_tok[par] = None
                    sv = act(ts_, bg, AF.Silu, sem=SSE)
                    banks_tok[par * 2] = (SSE, sv)
                    w("vector", SSE, sv)
                    hv = vop("tensor_mul", hidt[:, hl * 512:(hl + 1) * 512], bu, ts_)
                    banks_tok[par * 2 + 1] = (SVE, hv)
                    hid_tok[ht] = (SVE, hv)
                    w("tensor", SVE, hv)
                    hidr_done = mm(ps_b[0:16, :], daT[:, ht * RK:(ht + 1) * RK],
                                   hidt[:, hl * 512:(hl + 1) * 512],
                                   start=(ht == 0), stop=(ht == NHT - 1),
                                   sem=SPV if ht == NHT - 1 else None)

            def down_half(half, first):
                for dc in range(8):
                    for rtl in range(4):
                        wt("tensor", banks_tok[rtl])
                    if not first:
                        w("tensor", SSE, HIDR_EV)
                        for rtl in range(4):
                            mm(ps_big[:, rtl * 512:(rtl + 1) * 512],
                               xrpack[64:80, rtl * 128:(rtl + 1) * 128],
                               lorapk[64:80, dc * 512:(dc + 1) * 512],
                               start=True, stop=False)
                    for hl in range(NHP):
                        ht = half * NHP + hl
                        sl = dw_i[0] % NDSL
                        qeng = "sync" if sl % 2 == 0 else "gpsimd"
                        wt(qeng, dw_slot_tok[sl])
                        src = (down_g4A[ht // 11, ht % 11, dc] if dc < 4
                               else down_g4B[ht // 11, ht % 11, dc - 4])
                        key = (qeng, "A" if dc < 4 else "B")
                        if key not in dw_seen:
                            dw_seen.add(key)
                            wt(qeng, CC_DOWNA if dc < 4 else CC_DOWNB)
                        lv = dma(qeng, dwring[:, sl * 512:(sl + 1) * 512],
                                 src, SFDs[sl])
                        dw_i[0] += 1
                        w("tensor", SFDs[sl], lv)
                        wt("tensor", hid_tok[ht])
                        dv = None
                        for rtl in range(4):
                            dv = mm(ps_big[:, rtl * 512:(rtl + 1) * 512],
                                    hidt[:, hl * 512 + rtl * 128:
                                         hl * 512 + (rtl + 1) * 128],
                                    dwring[:, sl * 512:(sl + 1) * 512],
                                    start=(first and hl == 0), stop=(hl == NHP - 1),
                                    sem=SFT if rtl == 3 else None)
                        dw_slot_tok[sl] = (SFT, dv)
                    for rtl in range(4):
                        chunk = out4_dram[rtl * 128:(rtl + 1) * 128,
                                          dc * 512:(dc + 1) * 512]
                        if first:
                            if dc == 0 and rtl == 0:
                                w("sync", SO4R, o4r_prev[0])
                            w("scalar", SFT, SFT.n)
                            wt("scalar", ts_tok[0])
                            evv = act(tstage[:, 0:512],
                                      ps_big[:, rtl * 512:(rtl + 1) * 512],
                                      AF.Copy, sem=SEV)
                            banks_tok[rtl] = (SEV, evv)
                            w("sync", SEV, evv)
                            wv = dma("sync", chunk, tstage[:, 0:512], SO4W)
                            ts_tok[0] = (SO4W, wv)
                        else:
                            wt("sync", ts_tok[0])
                            if dc == 0 and rtl == 0:
                                w("sync", SO4W, SO4W.n)
                            lv2 = dma("sync", tstage[:, 0:512], chunk, SO4L)
                            w("vector", SO4L, lv2)
                            w("vector", SFT, SFT.n)
                            wt("vector", ts_tok[1])
                            evv = vop("tensor_add", tstage[:, 512:1024],
                                      ps_big[:, rtl * 512:(rtl + 1) * 512],
                                      tstage[:, 0:512])
                            banks_tok[rtl] = (SVE, evv)
                            ts_tok[0] = (SVE, evv)
                            w("sync", SVE, evv)
                            wv = dma("sync", chunk, tstage[:, 512:1024], SO4W2)
                            ts_tok[1] = (SO4W2, wv)

            gu_half(0)
            down_half(0, True)
            gu_half(1)
            x3t_free = (SFT, SFT.n)
            w("scalar", SPV, hidr_done)
            HIDR_EV = act(xrpack[64:80, :], ps_b[0:16, :], AF.Copy, sem=SSE)
            down_half(1, False)

            for rtl in range(4):
                rt = rc * 4 + rtl
                sl = rtl % 2
                if rtl == 0:
                    w("sync", SO4W2, SO4W2.n)
                wt("sync", x1row_tok[0])
                l4 = dma("sync", x1row[:], out4_dram[rtl * 128:(rtl + 1) * 128, :],
                         SO4R)
                wt("sync", x2r_tok[sl])
                w("sync", SX2, X2_ALL)
                lx = dma("sync", xrow[:, sl * D:(sl + 1) * D],
                         x2_dram[rt * 128:(rt + 1) * 128, :], SX2Rs[sl])
                w("vector", SO4R, l4)
                w("scalar", SO4R, l4)
                tok_ln, nm4, r4 = ln_stats(x1row[:], D, 0 if sl == 0 else 8,
                                           prewait=zrow_tok2[0])
                wt("vector", tok_ln)
                z4 = vop("tensor_scalar", zrow[:], x1row[:], nm4, r4,
                         ALU.add, ALU.mult)
                x1row_tok[0] = (SVE, z4)
                w("vector", SVE, z4)
                w("vector", SX2Rs[sl], lx)
                for _tk in x2tmp_tok:
                    wt("vector", _tk)
                ov = vop("tensor_add", x2tmp[:], zrow[:], xrow[:, sl * D:(sl + 1) * D])
                zrow_tok2[0] = (SVE, ov)
                x2r_tok[sl] = (SVE, ov)
                w("sync", SVE, ov)
                odma = dma("sync", out_ap[rt * 128:(rt + 1) * 128, :], x2tmp[:], SOUT)
                x2tmp_tok = [(SOUT, odma)]
            o4r_prev[0] = SO4R.n

        ffn_rc(0)

        for rtl in range(4):
            rt = 4 + rtl
            sl = rtl % 2
            wt("sync", x2r_tok[sl])
            w("sync", SX2, X2_ALL)
            lx = dma("sync", xrow[:, sl * D:(sl + 1) * D],
                     x2_dram[rt * 128:(rt + 1) * 128, :], SX2Rs[sl])
            w("vector", SX2Rs[sl], lx)
            wt("vector", x1row_tok[0])
            nx3 = vop("tensor_scalar", x1row[:], xrow[:, sl * D:(sl + 1) * D],
                      stats_st[:, 2 * rt + 1:2 * rt + 2],
                      stats_st[:, 2 * rt:2 * rt + 1], ALU.add, ALU.mult)
            x1row_tok[0] = (SVE, nx3)
            x2r_tok[sl] = (SVE, nx3)
            psT_tok = emit_x3_transposes(rtl, nx3)
        ffn_rc(1)

        return finish()


def _shard_inputs(inputs, consts):
    x = np.asarray(inputs["L0_attn_x"]).reshape(R, D)
    wqkv = np.asarray(inputs["L0_attn_w_qkv"])
    wo = np.asarray(inputs["L0_attn_w_o"])
    gate = np.asarray(inputs["L0_ffn_gate_w"])
    up = np.asarray(inputs["L0_ffn_up_w"])
    down = np.asarray(inputs["L0_ffn_down_w"])
    in_maps = []
    for c in range(NCORE):
        qcols = wqkv[:, c * 512:(c + 1) * 512]
        kcols = wqkv[:, 4096 + c * 128:4096 + (c + 1) * 128]
        vcols = wqkv[:, 5120 + c * 128:5120 + (c + 1) * 128]
        dn = down[:, c * 1408:(c + 1) * 1408]
        if dn.shape[1] < 1408:
            dn = np.concatenate(
                [dn, np.zeros((D, 1408 - dn.shape[1]), np.float32)], axis=1)
        m = {
            "x_rows": np.ascontiguousarray(x[c * OWN:(c + 1) * OWN]),
            "wqkv_sl": np.ascontiguousarray(np.concatenate([qcols, kcols, vcols], axis=1)),
            "wo_sl": np.ascontiguousarray(wo[c * 512:(c + 1) * 512]),
            "gate_sl": np.ascontiguousarray(gate[:, c * 512:(c + 1) * 512]),
            "up_sl": np.ascontiguousarray(up[:, c * 512:(c + 1) * 512]),
            "down_sl": np.ascontiguousarray(dn),
            "gate_a": np.asarray(inputs["L0_ffn_gate_a"]),
            "gate_b": np.asarray(inputs["L0_ffn_gate_b"]),
            "up_a": np.asarray(inputs["L0_ffn_up_a"]),
            "up_b": np.asarray(inputs["L0_ffn_up_b"]),
            "down_a": np.asarray(inputs["L0_ffn_down_a"]),
            "down_b": np.asarray(inputs["L0_ffn_down_b"]),
        }
        m.update(consts)
        in_maps.append(m)
    return in_maps


def kernel(**inputs):
    consts = _host_consts()
    in_maps = _shard_inputs(inputs, consts)
    b = Builder()
    nc = b.build()
    res = run_bass_kernel_spmd(nc, in_maps, core_ids=list(range(NCORE)))
    out = np.concatenate([r["out_rows"] for r in res.results], axis=0)
    return out.reshape(B, S, D)
